# version 27
# baseline (speedup 1.0000x reference)
import numpy as np
import ml_dtypes
from contextlib import ExitStack

import concourse.bass as bass
import concourse.mybir as mybir
from concourse.bass_utils import run_bass_kernel_spmd

F32 = mybir.dt.float32
BF16 = mybir.dt.bfloat16
I16 = mybir.dt.int16
ALU = mybir.AluOpType
AF = mybir.ActivationFunctionType

NB, S, D = 16, 2048, 2048
NCORES = 8
SPC = NB // NCORES
TCORE = SPC * S
TT = 512
NTILES = TCORE // TT
NCH = D // 128
H = 16
MLP = 16384
NSUPER = 2048
NSUP = MLP // NSUPER
NCPS = NSUPER // 128
GRP = 256
NGRP = NSUPER // GRP
JPG = GRP // 128
A_LEVEL = 4
M_LEVEL = 5
M_SQ = True
M_CP = True


def perm2048():
    p = np.empty(2048, dtype=np.int64)
    j = 0
    for c in range(8):
        for par in range(2):
            for q in range(128):
                p[j] = 2 * (128 * c + q) + par
                j += 1
    return p


def perm16384():
    p = np.empty(16384, dtype=np.int64)
    j = 0
    for c in range(64):
        for par in range(2):
            for q in range(128):
                p[j] = 2 * (128 * c + q) + par
                j += 1
    return p


class Em:
    def __init__(self):
        self.ops = {k: [] for k in ("sync", "gpsimd", "vector", "scalar", "tensor")}
        self.cnt = {}

    def val(self, name):
        return self.cnt.get(name, 0)

    def bump(self, name, k):
        self.cnt[name] = self.cnt.get(name, 0) + k
        return self.cnt[name]


class Pool:
    def __init__(self, tensor, n_i16):
        self.t = tensor
        self.n = n_i16
        self.off = 0

    def take(self, shape, dt):
        assert shape[0] == 128
        elems = int(np.prod(shape[1:]))
        nbytes = elems * mybir.dt.size(dt)
        assert nbytes % 2 == 0
        n16 = nbytes // 2
        off = (self.off + 31) & ~31
        assert off + n16 <= self.n, f"pool overflow: {off}+{n16}>{self.n}"
        self.off = off + n16
        ap = self.t[:, off:off + n16].bitcast(dt)
        if len(shape) > 2:
            ap = ap.rearrange("p (a b) -> p a b", b=shape[-1])
        return ap


def build_nc(phases=("dec", "A", "B"), dbg=(), detect_races=True):
    nc = bass.Bass("TRN2", target_bir_lowering=False,
                   detect_race_conditions=detect_races)

    x_in = nc.dram_tensor("x_in", [D, TCORE], F32, kind="ExternalInput").ap()
    wq_t = nc.dram_tensor("wq_t", [16, 128, 16, 128], F32, kind="ExternalInput").ap()
    wkv_t = nc.dram_tensor("wkv_t", [16, 128, 16, 128], F32, kind="ExternalInput").ap()
    wo_t = nc.dram_tensor("wo_t", [16, 128, 16, 128], F32, kind="ExternalInput").ap()
    g16 = nc.dram_tensor("g16", [1024, MLP], I16, kind="ExternalInput").ap()
    u16 = nc.dram_tensor("u16", [1024, MLP], I16, kind="ExternalInput").ap()
    d16 = nc.dram_tensor("d16", [8192, 2048], I16, kind="ExternalInput").ap()
    gse = nc.dram_tensor("gse", [1024, MLP], F32, kind="ExternalInput").ap()
    use_ = nc.dram_tensor("use", [1024, MLP], F32, kind="ExternalInput").ap()
    dse = nc.dram_tensor("dse", [8192, 2048], F32, kind="ExternalInput").ap()
    dftc = nc.dram_tensor("dftc", [128, 128], BF16, kind="ExternalInput").ap()
    dfts = nc.dram_tensor("dfts", [128, 128], BF16, kind="ExternalInput").ap()
    dftci = nc.dram_tensor("dftci", [128, 128], BF16, kind="ExternalInput").ap()
    dftsi = nc.dram_tensor("dftsi", [128, 128], BF16, kind="ExternalInput").ap()

    out = nc.dram_tensor("out", [D, TCORE], F32, kind="ExternalOutput").ap()

    NGRPS = MLP // GRP
    gdec = nc.dram_tensor("gdec", [NGRPS, 128, NCH, GRP], BF16, kind="Internal").ap()
    udec = nc.dram_tensor("udec", [NGRPS, 128, NCH, GRP], BF16, kind="Internal").ap()
    ddec = nc.dram_tensor("ddec", [NSUP, NCH, 128, NCPS, 128], BF16, kind="Internal").ap()
    x2d = nc.dram_tensor("x2d", [D, TCORE], F32, kind="Internal").ap()

    dbg_outs = {}
    if "dec" in dbg:
        dbg_outs["o_gdec"] = nc.dram_tensor("o_gdec", [MLP // GRP, 128, NCH, GRP], BF16,
                                            kind="ExternalOutput").ap()
        dbg_outs["o_ddec"] = nc.dram_tensor("o_ddec", [NSUP, NCH, 128, NCPS, 128], BF16,
                                            kind="ExternalOutput").ap()
    if "x2" in dbg:
        dbg_outs["o_x2"] = nc.dram_tensor("o_x2", [D, TCORE], F32, kind="ExternalOutput").ap()

    em = Em()

    with ExitStack() as ctx:
        E = ctx.enter_context
        dS = E(nc.semaphore("dS"))
        dG = E(nc.semaphore("dG"))
        dA = E(nc.semaphore("dA"))
        sP = E(nc.semaphore("sP"))
        sV = E(nc.semaphore("sV"))
        sA = E(nc.semaphore("sA"))
        sG = E(nc.semaphore("sG"))
        SEM = {"dS": dS, "dG": dG, "dA": dA, "sP": sP, "sV": sV, "sA": sA, "sG": sG}

        def W(eng, name, v):
            em.ops[eng].append(lambda e, s=SEM[name], vv=v: e.wait_ge(s, vv))

        def O(eng, fn, incname=None, k=1):
            if incname is None:
                em.ops[eng].append(fn)
                return None
            v = em.bump(incname, k)
            em.ops[eng].append(lambda e, f=fn, s=SEM[incname], kk=k: f(e).then_inc(s, kk))
            return v

        ones_col = E(nc.sbuf_tensor("ones_col", [128, 1], BF16))
        ones_row = E(nc.sbuf_tensor("ones_row", [1, 128], BF16))
        ones_all = E(nc.sbuf_tensor("ones_all", [128, 128], BF16))
        zeros_t = E(nc.sbuf_tensor("zeros_t", [128, TT], F32))
        c_neg = E(nc.sbuf_tensor("c_neg", [1, 1], F32))
        c_inv = E(nc.sbuf_tensor("c_inv", [1, 1], F32))
        c_eps = E(nc.sbuf_tensor("c_eps", [1, 1], F32))
        c_nh = E(nc.sbuf_tensor("c_nh", [1, 1], F32))
        c_nh128 = E(nc.sbuf_tensor("c_nh128", [128, 1], F32))
        c_lnb = E(nc.sbuf_tensor("c_lnb", [128, 1], F32))
        t_dftc = E(nc.sbuf_tensor("t_dftc", [128, 128], BF16))
        t_dfts = E(nc.sbuf_tensor("t_dfts", [128, 128], BF16))
        t_dftci = E(nc.sbuf_tensor("t_dftci", [128, 128], BF16))
        t_dftsi = E(nc.sbuf_tensor("t_dftsi", [128, 128], BF16))
        pad1 = E(nc.sbuf_tensor("pad1", [1, 8], F32))
        pad2 = E(nc.sbuf_tensor("pad2", [1, 8], F32))
        rowS = E(nc.sbuf_tensor("rowS", [1, 8 * TT], F32))
        negm_bf = E(nc.sbuf_tensor("negm_bf", [1, TT], BF16))
        wbar_q = E(nc.sbuf_tensor("wbar_q", [1, 2048], BF16))
        wbar_kv = E(nc.sbuf_tensor("wbar_kv", [1, 2048], BF16))

        POOL_N = 89 * 1024
        pool_t = E(nc.sbuf_tensor("pool", [128, POOL_N], I16))

        ps6 = [E(nc.psum_tensor(f"ps{i}", [128, TT], F32)) for i in range(8)]
        psR = [ps6[6][0:1, :], ps6[7][0:1, :]]

        O("vector", lambda v: v.memset(ones_col[:], 1.0))
        O("vector", lambda v: v.memset(ones_row[:], 1.0))
        O("vector", lambda v: v.memset(ones_all[:], 1.0))
        O("vector", lambda v: v.memset(zeros_t[:], 0.0))
        O("vector", lambda v: v.memset(c_neg[:], -1.0 / 2048.0))
        O("vector", lambda v: v.memset(c_inv[:], 1.0 / 2048.0))
        O("vector", lambda v: v.memset(c_eps[:], 1e-5))
        O("vector", lambda v: v.memset(c_nh[:], -0.5))
        O("vector", lambda v: v.memset(c_nh128[:], -0.5))
        O("vector", lambda v: v.memset(c_lnb[:], -2.4260151319598084))
        O("vector", lambda v: v.memset(pad1[:], 0.0))
        O("gpsimd", lambda g: g.memset(pad2[:], 0.0))
        O("vector", lambda v: v.tensor_copy(out=pad1[:], in_=pad1[:]))
        v_init = O("vector", lambda v: v.tensor_copy(out=pad1[:], in_=pad1[:]), "sV")
        O("sync", lambda e: e.dma_start(out=t_dftc[:], in_=dftc), "dS", 16)
        O("sync", lambda e: e.dma_start(out=t_dfts[:], in_=dfts), "dS", 16)
        O("sync", lambda e: e.dma_start(out=t_dftci[:], in_=dftci), "dS", 16)
        d_init = O("sync", lambda e: e.dma_start(out=t_dftsi[:], in_=dftsi), "dS", 16)

        def dve_pad(n=1):
            for _ in range(n):
                O("vector", lambda v: v.tensor_copy(out=pad1[:], in_=pad1[:]))

        def act_pad(n=1):
            for _ in range(n):
                O("scalar", lambda s: s.activation(out=pad1[:], in_=pad1[:], func=AF.Copy))

        def gp_pad(n=1):
            for _ in range(n):
                O("gpsimd", lambda g: g.tensor_copy(out=pad2[:], in_=pad2[:]))

        if "dec" in phases:
            dp = Pool(pool_t, POOL_N)
            FD = 4096
            pk_b = [dp.take([128, FD], I16) for _ in range(2)]
            sc_b = [dp.take([128, FD], F32) for _ in range(2)]
            uL = dp.take([128, FD], I16); uH = dp.take([128, FD], I16)
            eL = dp.take([128, FD], I16); eH = dp.take([128, FD], I16)
            mL = dp.take([128, FD], I16); mH = dp.take([128, FD], I16)
            bL = dp.take([128, FD], I16); bH = dp.take([128, FD], I16)
            plL = [dp.take([128, FD], BF16) for _ in range(2)]
            plH = [dp.take([128, FD], BF16) for _ in range(2)]

            tiles = []
            for mat, scl, dst in ((g16, gse, gdec), (u16, use_, udec)):
                for c in range(8):
                    for t4 in range(4):
                        n0 = t4 * 4096
                        tiles.append((
                            mat[c * 128:(c + 1) * 128, n0:n0 + 4096],
                            scl[c * 128:(c + 1) * 128, n0:n0 + 4096],
                            (dst, c, n0),
                            False))
            for c2 in range(32):
                r0 = c2 * 256
                dsts = []
                for half in range(2):
                    C = 2 * c2 + half
                    for par in range(2):
                        nch_ = 2 * C + par
                        nsx, r = nch_ // NCPS, nch_ % NCPS
                        dsts.append(ddec[nsx, :, :, r, :].transpose([1, 0, 2]))
                tiles.append((
                    d16[r0:r0 + 256, :].rearrange("(h p) f -> p h f", p=128),
                    dse[r0:r0 + 256, :].rearrange("(h p) f -> p h f", p=128),
                    dsts, True))

            dec_done = []
            st_cnt = []
            for i, (src, ssrc, dsts, is_down) in enumerate(tiles):
                pg = i % 2
                if i >= 2:
                    W("sync", "sV", dec_done[i - 2])
                pk_v, sc_v = pk_b[pg], sc_b[pg]
                if is_down:
                    pk_v = pk_v.rearrange("p (h f) -> p h f", h=2)
                    sc_v = sc_v.rearrange("p (h f) -> p h f", h=2)
                O("sync", lambda e, o=pk_v, s=src: e.dma_start(out=o, in_=s), "dS", 16)
                nl1 = O("sync", lambda e, o=sc_v, s=ssrc: e.dma_start(out=o, in_=s), "dS", 16)
                W("vector", "dS", nl1)
                if i >= 2:
                    W("vector", "dA", st_cnt[i - 2])
                pk, sc = pk_b[pg], sc_b[pg]
                oL, oH = plL[pg], plH[pg]

                def dec_ops(v, pk=pk, sc=sc, oL=oL, oH=oH):
                    v.tensor_scalar(out=uL, in0=pk, scalar1=7, scalar2=None, op0=ALU.bitwise_and)
                    v.tensor_scalar(out=uH, in0=pk, scalar1=4, scalar2=7,
                                    op0=ALU.logical_shift_right, op1=ALU.bitwise_and)
                    v.tensor_scalar(out=mL, in0=uL, scalar1=1.0, scalar2=-1.0,
                                    op0=ALU.is_ge, op1=ALU.mult)
                    v.tensor_scalar(out=mH, in0=uH, scalar1=1.0, scalar2=-1.0,
                                    op0=ALU.is_ge, op1=ALU.mult)
                    v.tensor_scalar(out=eL, in0=uL, scalar1=1.0, scalar2=None, op0=ALU.is_equal)
                    v.tensor_scalar(out=eH, in0=uH, scalar1=1.0, scalar2=None, op0=ALU.is_equal)
                    v.tensor_tensor(out=uL, in0=uL, in1=eL, op=ALU.bitwise_xor)
                    v.tensor_tensor(out=uH, in0=uH, in1=eH, op=ALU.bitwise_xor)
                    v.tensor_scalar(out=bL, in0=uL, scalar1=64.0, scalar2=16128.0,
                                    op0=ALU.mult, op1=ALU.add)
                    v.tensor_scalar(out=bH, in0=uH, scalar1=64.0, scalar2=16128.0,
                                    op0=ALU.mult, op1=ALU.add)
                    v.tensor_tensor(out=bL, in0=bL, in1=mL, op=ALU.bitwise_and)
                    v.tensor_tensor(out=bH, in0=bH, in1=mH, op=ALU.bitwise_and)
                    v.tensor_scalar(out=eL, in0=pk, scalar1=8, scalar2=12,
                                    op0=ALU.bitwise_and, op1=ALU.logical_shift_left)
                    v.tensor_scalar(out=eH, in0=pk, scalar1=128, scalar2=8,
                                    op0=ALU.bitwise_and, op1=ALU.logical_shift_left)
                    v.tensor_tensor(out=bL, in0=bL, in1=eL, op=ALU.bitwise_or)
                    v.tensor_tensor(out=bH, in0=bH, in1=eH, op=ALU.bitwise_or)
                    v.tensor_tensor(out=oL, in0=bL.bitcast(BF16), in1=sc, op=ALU.mult)
                    return v.tensor_tensor(out=oH, in0=bH.bitcast(BF16), in1=sc, op=ALU.mult)
                O("vector", dec_ops)
                dve_pad(1)
                dv = O("vector", lambda v: v.tensor_copy(out=pad1[:], in_=pad1[:]), "sV")
                dec_done.append(dv)
                W("scalar", "sV", dv)
                if not is_down:
                    dst3, cc, n0_ = dsts
                    for par, opl in ((0, oL), (1, oH)):
                        kc = 2 * cc + par
                        for s in range(4096 // GRP):
                            sc2 = O("scalar",
                                    lambda e, o=dst3[n0_ // GRP + s][:, kc, :],
                                    s_=opl[:, s * GRP:(s + 1) * GRP]:
                                    e.dma_start(out=o, in_=s_), "dA", 16)
                else:
                    for hh in range(2):
                        for par, opl in ((0, oL), (1, oH)):
                            srcv = opl.rearrange("p (h f) -> p h f", h=2)[:, hh, :]
                            srcv = srcv.rearrange("p (fo f) -> p fo f", fo=16)
                            sc2 = O("scalar",
                                    lambda e, o=dsts[hh * 2 + par], s=srcv: e.dma_start(out=o, in_=s),
                                    "dA", 16)
                st_cnt.append(sc2)
        dec_dS = em.val("dS")
        dec_dA = em.val("dA")
        dec_sV = em.val("sV")

        if "A" in phases:
            ap_ = Pool(pool_t, POOL_N)
            x_f32 = ap_.take([128, NCH * TT], F32)
            x_bf = ap_.take([128, NCH * TT], BF16)
            q_bf = ap_.take([128, NCH * TT], BF16)
            kv_bf = ap_.take([128, NCH * TT], BF16)
            vh_bf = ap_.take([128, NCH * TT], BF16)
            wbuf = [ap_.take([128, 2048], BF16) for _ in range(2)]
            qfre = [ap_.take([128, TT], BF16) for _ in range(3)]
            qfim = [ap_.take([128, TT], BF16) for _ in range(3)]
            kvre = [ap_.take([128, TT], BF16) for _ in range(3)]
            kvim = [ap_.take([128, TT], BF16) for _ in range(3)]
            sqst = [ap_.take([128, TT], BF16) for _ in range(2)]
            kvnre = ap_.take([128, TT], BF16); kvnim = ap_.take([128, TT], BF16)
            kvcre = ap_.take([128, TT], F32); kvcim = ap_.take([128, TT], F32)
            t1 = ap_.take([128, TT], F32); t2 = ap_.take([128, TT], F32)
            t3 = ap_.take([128, TT], F32); t4 = ap_.take([128, TT], F32)
            qvre = ap_.take([128, TT], BF16); qvim = ap_.take([128, TT], BF16)
            lq_t = ap_.take([128, TT], F32); lkv_t = ap_.take([128, TT], F32)
            rq_t = [ap_.take([128, TT], F32) for _ in range(3)]
            rkv_t = [ap_.take([128, TT], F32) for _ in range(3)]
            carry_re = ap_.take([128, H], F32); carry_im = ap_.take([128, H], F32)
            x2st = [ap_.take([128, TT], F32) for _ in range(2)]
            n2st = ap_.take([128, 16 * TT], BF16)

            def n2row(r):
                return n2st[64 * (r % 2):64 * (r % 2) + 1,
                            (r // 2) * TT:(r // 2 + 1) * TT]

            W("vector", "dS", d_init)
            W("tensor", "sV", v_init)
            W("tensor", "dS", d_init)
            W("sync", "sV", dec_sV)
            W("sync", "dA", dec_dA)
            W("gpsimd", "sV", dec_sV)
            W("gpsimd", "dA", dec_dA)

            xin3 = x_in.rearrange("(c p) t -> p c t", p=128)
            x_f32_3 = x_f32.rearrange("p (c t) -> p c t", t=TT)
            pm_slot = {}
            ev_slot = {}
            o_ev = {}
            o_st = {}
            for ti in range(NTILES):
                t0 = ti * TT
                for cg in range(4):
                    O("sync", lambda e, cg=cg, t0=t0:
                      e.dma_start(out=x_f32_3[:, cg * 4:(cg + 1) * 4, :],
                                  in_=xin3[:, cg * 4:(cg + 1) * 4, t0:t0 + TT]), "dS", 16)
                ld_x = em.val("dS")
                W("vector", "dS", ld_x)
                O("vector", lambda v: v.tensor_copy(out=x_bf[:], in_=x_f32[:]))
                dve_pad(1)
                c_cast = O("vector", lambda v: v.tensor_copy(out=pad1[:], in_=pad1[:]), "sV")
                W("tensor", "sV", c_cast)

                def mean_mms(t):
                    for kc in range(NCH):
                        r = t.matmul(psR[0][:], lhsT=ones_col[:],
                                     rhs=x_bf[:, kc * TT:(kc + 1) * TT],
                                     start=(kc == 0), stop=(kc == NCH - 1))
                    return r
                p_mean = O("tensor", mean_mms, "sP")
                W("scalar", "sP", p_mean)
                O("scalar", lambda s: s.activation(out=negm_bf[:], in_=psR[0][:],
                                                   func=AF.Copy, scale=c_neg[:]))
                act_pad(1)
                a_negm = O("scalar", lambda s: s.activation(out=pad1[:], in_=pad1[:],
                                                            func=AF.Copy), "sA")

                a_sqp = {}; p_n2 = {}; v_n2 = {}

                def emit_n2(sl):
                    W("tensor", "sA", a_sqp[sl])
                    pb = psR[0] if ti == 0 else psR[sl % 2]
                    gidx = sl - 1 if ti == 0 else sl - 2
                    if gidx >= 0 and gidx in v_n2:
                        W("tensor", "sV", v_n2[gidx])
                    pmn = O("tensor", lambda t, pb=pb, sl=sl: t.matmul(
                        pb[:], lhsT=ones_col[:], rhs=sqst[sl % 2][:],
                        start=True, stop=True), "sP")
                    p_n2[sl] = pmn
                    W("vector", "sP", pmn)
                    O("vector", lambda v, pb=pb, sl=sl: v.tensor_copy(
                        out=n2row(sl), in_=pb[:]))
                    dve_pad(1)
                    v_n2[sl] = O("vector", lambda v: v.tensor_copy(out=pad1[:], in_=pad1[:]),
                                 "sV")

                for pri, (wt, dst, wbar) in (enumerate(((wq_t, q_bf, wbar_q),
                                                       (wkv_t, kv_bf, wbar_kv))) if A_LEVEL >= 2 else []):
                    for oc in range(NCH):
                        slot = pri * NCH + oc
                        pg = slot % 2
                        if slot >= 2:
                            emit_n2(slot - 2)
                        prev = (ti, slot - 2) if slot >= 2 else (ti - 1, 46 + slot)
                        if prev in pm_slot:
                            W("gpsimd", "sP", pm_slot[prev])
                        ldw = O("gpsimd", lambda e, wt=wt, oc=oc, b=wbuf[pg]:
                                e.dma_start(out=b, in_=wt[oc]), "dG", 16)
                        W("tensor", "dG", ldw)
                        if prev in ev_slot:
                            W("tensor", "sA", ev_slot[prev])
                        ps = ps6[pg]

                        def proj_mms(t, b=wbuf[pg], ps=ps):
                            for kc in range(NCH):
                                t.matmul(ps[:], lhsT=b[:, kc * 128:(kc + 1) * 128],
                                         rhs=x_bf[:, kc * TT:(kc + 1) * TT],
                                         start=(kc == 0), stop=False)
                            return t
                        O("tensor", proj_mms)
                        if ti == 0:
                            def wbar_mms(t, b=wbuf[pg]):
                                for kc in range(NCH):
                                    r = t.matmul(psR[1][:, 0:128],
                                                 lhsT=ones_col[:],
                                                 rhs=b[:, kc * 128:(kc + 1) * 128],
                                                 start=(kc == 0), stop=(kc == NCH - 1))
                                return r
                            pw = O("tensor", wbar_mms, "sP")
                            W("scalar", "sP", pw)
                            O("scalar", lambda s, w=wbar, oc=oc:
                              s.activation(out=w[:, oc * 128:(oc + 1) * 128],
                                           in_=psR[1][:, 0:128], func=AF.Copy))
                            aw = O("scalar", lambda s: s.activation(out=pad1[:], in_=pad1[:],
                                                                    func=AF.Copy), "sA")
                            W("tensor", "sA", aw)
                        if slot == 0:
                            W("tensor", "sA", a_negm)
                        pm = O("tensor", lambda t, w=wbar, oc=oc, ps=ps:
                               t.matmul(ps[:], lhsT=w[:, oc * 128:(oc + 1) * 128],
                                        rhs=negm_bf[:], start=False, stop=True), "sP")
                        pm_slot[(ti, slot)] = pm
                        W("scalar", "sP", pm)
                        O("scalar", lambda s, ps=ps, dst=dst, oc=oc:
                          s.activation(out=dst[:, oc * TT:(oc + 1) * TT], in_=ps[:],
                                       func=AF.Copy))
                        ae = O("scalar", lambda s: s.activation(out=pad1[:], in_=pad1[:],
                                                                func=AF.Copy), "sA")
                        ev_slot[(ti, slot)] = ae
                        if slot >= 2 and (slot - 2) in p_n2:
                            W("scalar", "sP", p_n2[slot - 2])
                        O("scalar", lambda s, dst=dst, oc=oc, sl=slot: s.activation(
                            out=sqst[sl % 2][:], in_=dst[:, oc * TT:(oc + 1) * TT],
                            func=AF.Square))
                        a_sqp[slot] = O("scalar", lambda s: s.activation(
                            out=pad1[:], in_=pad1[:], func=AF.Copy), "sA")
                if A_LEVEL >= 2:
                    emit_n2(30)
                    emit_n2(31)
                proj_all = em.val("sA")

                first_of_seq = (ti % (S // TT) == 0)
                f_pfft = {}; f_vcp = {}; f_acp = {}; f_pbc = {}; f_arsq = {}
                f_vscan = {}; f_vmix = {}; f_gmix = {}; f_gkvn = {}
                f_pifft = {}; f_vvh = {}
                for i in (range(H + 5) if A_LEVEL >= 3 else []):
                    if i < H:
                        hs = slice(i * TT, (i + 1) * TT)
                        if i == 0:
                            W("tensor", "sA", proj_all)
                        if i >= 1:
                            W("tensor", "sV", f_vcp[i - 1])
                            W("tensor", "sA", f_acp[i - 1])

                        def fft_mms(t, hs=hs):
                            t.matmul(ps6[0][:], lhsT=t_dftc[:], rhs=q_bf[:, hs], start=True, stop=True)
                            t.matmul(ps6[1][:], lhsT=t_dfts[:], rhs=q_bf[:, hs], start=True, stop=True)
                            t.matmul(ps6[2][:], lhsT=t_dftc[:], rhs=kv_bf[:, hs], start=True, stop=True)
                            return t.matmul(ps6[3][:], lhsT=t_dfts[:], rhs=kv_bf[:, hs],
                                            start=True, stop=True)
                        f_pfft[i] = O("tensor", fft_mms, "sP")
                    k = i - 2
                    if 0 <= k < H:
                        W("tensor", "sV", v_n2[16 + k])
                        if k >= 1:
                            W("tensor", "sA", f_arsq[k - 1])

                        def bcn_mms(t, k=k):
                            bkv = 64 * ((16 + k) % 2)
                            bq = 64 * (k % 2)
                            t.matmul(ps6[4][:], lhsT=ones_all[bkv:bkv + 1, :],
                                     rhs=n2row(16 + k), start=True, stop=True)
                            return t.matmul(ps6[5][:], lhsT=ones_all[bq:bq + 1, :],
                                            rhs=n2row(k), start=True, stop=True)
                        f_pbc[k] = O("tensor", bcn_mms, "sP")
                    k = i - 4
                    if 0 <= k < H:
                        W("tensor", "sV", f_vmix[k])
                        W("tensor", "sG", f_gmix[k])
                        if k >= 2:
                            W("tensor", "sV", f_vvh[k - 2])
                        elif k < 2:
                            W("tensor", "sV", v_n2[30 + k])

                        def ifft_mms(t, k=k):
                            pb = ps6[6 + k % 2]
                            t.matmul(pb[:], lhsT=t_dftci[:], rhs=qvre[:], start=True, stop=False)
                            return t.matmul(pb[:], lhsT=t_dftsi[:], rhs=qvim[:],
                                            start=False, stop=True)
                        f_pifft[k] = O("tensor", ifft_mms, "sP")

                    k = i - 5
                    if 0 <= k < H:
                        W("vector", "sP", f_pifft[k])
                        O("vector", lambda v, k=k: v.tensor_tensor(
                            out=vh_bf[:, k * TT:(k + 1) * TT], in0=ps6[6 + k % 2][:],
                            in1=rq_t[k % 3][:], op=ALU.mult))
                        dve_pad(1)
                        f_vvh[k] = O("vector", lambda v: v.tensor_copy(out=pad1[:], in_=pad1[:]),
                                     "sV")
                    k = i - 2
                    if 0 <= k < H:
                        W("scalar", "sP", f_pbc[k])
                        if k >= 3:
                            W("scalar", "sV", f_vvh[k - 3])

                        def rsq_ops(s, k=k):
                            s.activation(out=lkv_t[:], in_=ps6[4][:], func=AF.Ln)
                            s.activation(out=lq_t[:], in_=ps6[5][:], func=AF.Ln)
                            s.activation(out=rkv_t[k % 3][:], in_=lkv_t[:], func=AF.Exp,
                                         scale=c_nh128[:], bias=c_lnb[:])
                            return s.activation(out=rq_t[k % 3][:], in_=lq_t[:], func=AF.Exp,
                                                scale=c_nh128[:], bias=c_lnb[:])
                        O("scalar", rsq_ops)
                        act_pad(1)
                        f_arsq[k] = O("scalar", lambda s: s.activation(out=pad1[:], in_=pad1[:],
                                                                       func=AF.Copy), "sA")

                    k = i - 3
                    if 0 <= k < H:
                        W("vector", "sG", f_gkvn[k])
                        if k >= 1:
                            W("vector", "sG", f_gmix[k - 1])

                        def scan_ops(v, k=k, first=first_of_seq):
                            ire = 0.0 if first else carry_re[:, k:k + 1]
                            iim = 0.0 if first else carry_im[:, k:k + 1]
                            v.tensor_tensor_scan(out=kvcre[:], data0=kvnre[:], data1=zeros_t[:],
                                                 initial=ire, op0=ALU.add, op1=ALU.add)
                            return v.tensor_tensor_scan(out=kvcim[:], data0=kvnim[:], data1=zeros_t[:],
                                                        initial=iim, op0=ALU.add, op1=ALU.add)
                        O("vector", scan_ops)
                        dve_pad(1)
                        f_vscan[k] = O("vector", lambda v: v.tensor_copy(out=pad1[:], in_=pad1[:]),
                                       "sV")
                        if k >= 1:
                            W("vector", "sP", f_pifft[k - 1])

                        def cm_ops(v, k=k):
                            v.tensor_copy(out=carry_re[:, k:k + 1], in_=kvcre[:, TT - 1:TT])
                            v.tensor_copy(out=carry_im[:, k:k + 1], in_=kvcim[:, TT - 1:TT])
                            v.tensor_tensor(out=t1[:], in0=qfre[k % 3][:], in1=kvcre[:], op=ALU.mult)
                            return v.tensor_tensor(out=t2[:], in0=qfim[k % 3][:], in1=kvcim[:],
                                                   op=ALU.mult)
                        O("vector", cm_ops)
                        dve_pad(1)
                        O("vector", lambda v: v.tensor_tensor(out=qvre[:], in0=t1[:], in1=t2[:],
                                                              op=ALU.subtract))
                        dve_pad(1)
                        f_vmix[k] = O("vector", lambda v: v.tensor_copy(out=pad1[:], in_=pad1[:]),
                                      "sV")
                    k = i - 2
                    if 0 <= k < H:
                        W("gpsimd", "sA", f_arsq[k])
                        if k >= 1:
                            W("gpsimd", "sV", f_vscan[k - 1])

                        def gkvn_ops(g, k=k):
                            g.tensor_tensor(out=kvnre[:], in0=kvre[k % 3][:], in1=rkv_t[k % 3][:],
                                            op=ALU.mult)
                            g.tensor_copy(out=pad2[:], in_=pad2[:])
                            return g.tensor_tensor(out=kvnim[:], in0=kvim[k % 3][:],
                                                   in1=rkv_t[k % 3][:], op=ALU.mult)
                        O("gpsimd", gkvn_ops)
                        gp_pad(1)
                        f_gkvn[k] = O("gpsimd", lambda g: g.tensor_copy(out=pad2[:], in_=pad2[:]),
                                      "sG")
                    k = i - 3
                    if 0 <= k < H:
                        W("gpsimd", "sV", f_vscan[k])
                        if k >= 1:
                            W("gpsimd", "sP", f_pifft[k - 1])

                        def gmix_ops(g, k=k):
                            g.tensor_tensor(out=t3[:], in0=qfre[k % 3][:], in1=kvcim[:], op=ALU.mult)
                            g.tensor_tensor(out=t4[:], in0=qfim[k % 3][:], in1=kvcre[:], op=ALU.mult)
                            g.tensor_copy(out=pad2[:], in_=pad2[:])
                            return g.tensor_tensor(out=qvim[:], in0=t3[:], in1=t4[:], op=ALU.add)
                        O("gpsimd", gmix_ops)
                        gp_pad(1)
                        f_gmix[k] = O("gpsimd", lambda g: g.tensor_copy(out=pad2[:], in_=pad2[:]),
                                      "sG")
                    if i < H:
                        W("vector", "sP", f_pfft[i])
                        if i >= 3:
                            W("vector", "sG", f_gmix[i - 3])

                        def cp_ops(v, i=i):
                            v.tensor_copy(out=qfre[i % 3][:], in_=ps6[0][:])
                            return v.tensor_copy(out=qfim[i % 3][:], in_=ps6[1][:])
                        O("vector", cp_ops)
                        dve_pad(1)
                        f_vcp[i] = O("vector", lambda v: v.tensor_copy(out=pad1[:], in_=pad1[:]),
                                     "sV")
                    if i < H:
                        W("scalar", "sP", f_pfft[i])
                        if i >= 3:
                            W("scalar", "sV", f_vscan[i - 3])

                        def cpkv_ops(s, i=i):
                            s.activation(out=kvre[i % 3][:], in_=ps6[2][:], func=AF.Copy)
                            return s.activation(out=kvim[i % 3][:], in_=ps6[3][:], func=AF.Copy)
                        O("scalar", cpkv_ops)
                        act_pad(1)
                        f_acp[i] = O("scalar", lambda s: s.activation(out=pad1[:], in_=pad1[:],
                                                                      func=AF.Copy), "sA")
                mix_all = em.val("sV")

                for oc in (range(NCH) if A_LEVEL >= 4 else []):
                    slot = 32 + oc
                    pg = slot % 2
                    prev = (ti, slot - 2)
                    if prev in pm_slot:
                        W("gpsimd", "sP", pm_slot[prev])
                    ldw = O("gpsimd", lambda e, oc=oc, b=wbuf[pg]:
                            e.dma_start(out=b, in_=wo_t[oc]), "dG", 16)
                    W("tensor", "dG", ldw)
                    if oc == 0:
                        W("tensor", "sV", mix_all)
                    if (ti, oc - 2) in o_ev:
                        W("tensor", "sV", o_ev[(ti, oc - 2)])
                    ps = ps6[pg]

                    def omms(t, b=wbuf[pg], ps=ps):
                        for kc in range(NCH):
                            r = t.matmul(ps[:], lhsT=b[:, kc * 128:(kc + 1) * 128],
                                         rhs=vh_bf[:, kc * TT:(kc + 1) * TT],
                                         start=(kc == 0), stop=(kc == NCH - 1))
                        return r
                    pm = O("tensor", omms, "sP")
                    pm_slot[(ti, slot)] = pm
                    W("vector", "sP", pm)
                    if (ti, oc - 2) in o_st:
                        W("vector", "dS", o_st[(ti, oc - 2)])
                    O("vector", lambda v, ps=ps, oc=oc, pg=pg:
                      v.tensor_tensor(out=x2st[pg][:], in0=ps[:],
                                      in1=x_f32[:, oc * TT:(oc + 1) * TT], op=ALU.add))
                    dve_pad(1)
                    ev = O("vector", lambda v: v.tensor_copy(out=pad1[:], in_=pad1[:]), "sV")
                    o_ev[(ti, oc)] = ev
                    W("sync", "sV", ev)
                    st = O("sync", lambda e, oc=oc, pg=pg, t0=t0:
                           e.dma_start(out=x2d[oc * 128:(oc + 1) * 128, t0:t0 + TT],
                                       in_=x2st[pg][:]), "dS", 16)
                    o_st[(ti, oc)] = st
                W("sync", "sV", em.val("sV"))
        A_dS = em.val("dS")
        A_sV = em.val("sV")

        if "B" in phases:
            bp = Pool(pool_t, POOL_N)
            x2_sb = bp.take([128, NCH * TT], F32)
            xln = bp.take([128, NCH * TT], BF16)
            inter2 = [bp.take([128, NCPS * TT], BF16) for _ in range(2)]
            gbuf = [bp.take([128, 16 * GRP], BF16) for _ in range(2)]
            ubuf = [bp.take([128, 16 * GRP], BF16) for _ in range(2)]
            dbuf = [bp.take([128, NCPS * 128], BF16) for _ in range(4)]
            sq_c = [bp.take([128, TT], BF16) for _ in range(3)]
            R_t = bp.take([128, TT], F32)
            M_t = bp.take([128, TT], BF16)
            gcp = [bp.take([128, TT], F32) for _ in range(2)]
            g2t = [bp.take([128, TT], F32) for _ in range(2)]
            tnh = [bp.take([128, TT], F32) for _ in range(2)]
            gel = [bp.take([128, TT], BF16) for _ in range(2)]
            upc = [bp.take([128, TT], BF16) for _ in range(2)]
            mean_r = rowS[:, 0:TT]; msq_r = rowS[:, TT:2 * TT]
            tv_r = rowS[:, 2 * TT:3 * TT]; var_r = rowS[:, 3 * TT:4 * TT]
            lnv_r = rowS[:, 4 * TT:5 * TT]; rstd_r = rowS[:, 5 * TT:6 * TT]
            rstd_bfv = rowS[:, 6 * TT:6 * TT + TT // 2].bitcast(BF16)[:, 0:TT]
            mean_bfv = rowS[:, 6 * TT + TT // 2:7 * TT].bitcast(BF16)[:, 0:TT]

            W("tensor", "dA", dec_dA)
            W("tensor", "dS", A_dS)
            W("tensor", "sV", A_sV)
            W("sync", "dA", dec_dA)
            W("sync", "sV", A_sV)
            W("scalar", "sV", A_sV)
            W("scalar", "dS", A_dS)
            W("gpsimd", "dA", dec_dA)
            W("gpsimd", "sV", A_sV)
            W("gpsimd", "dS", A_dS)

            x23 = x2d.rearrange("(c p) t -> p c t", p=128)
            x2_sb3 = x2_sb.rearrange("p (c t) -> p c t", t=TT)

            GA, GB = 0.035677408136300795, 0.7978845608028654

            gu_pm = {}
            rel_v = {}
            rel_a = {}
            vv_g = {}
            inter_done = {}
            dn_pm = {}
            dn_ev = {}
            last_rd = {}
            glob = 0
            gfo = 0
            out_st = None

            def emit_dn(ti_, ns_src, fo, par_src, bc_guard=None):
                nonlocal gfo
                fp = gfo % 4
                if gfo >= 4:
                    W("gpsimd", "sP", dn_pm[gfo - 4])
                ldd = O("gpsimd", lambda e, ns=ns_src, fo=fo, b=dbuf[fp]:
                        e.dma_start(out=b.rearrange("p (r f) -> p r f", f=128),
                                    in_=ddec[ns, fo]), "dG", 16)
                W("tensor", "dG", ldd)
                if fo == 0:
                    W("tensor", "sV", inter_done[(ti_, ns_src)])
                if gfo >= 2:
                    W("tensor", "sV", dn_ev[gfo - 2])
                if bc_guard is not None:
                    W("tensor", "sA", bc_guard)
                Dp = ps6[4 + gfo % 2]
                itr = inter2[par_src]

                def dn_mms(t, b=dbuf[fp], Dp=Dp, itr=itr):
                    for r in range(NCPS):
                        rr = t.matmul(Dp[:], lhsT=b[:, r * 128:(r + 1) * 128],
                                      rhs=itr[:, r * TT:(r + 1) * TT],
                                      start=(r == 0), stop=(r == NCPS - 1))
                    return rr
                pm = O("tensor", dn_mms, "sP")
                dn_pm[gfo] = pm
                last_rd[par_src] = pm
                W("vector", "sP", pm)
                O("vector", lambda v, Dp=Dp, fo=fo: v.tensor_tensor(
                    out=x2_sb[:, fo * TT:(fo + 1) * TT], in0=Dp[:],
                    in1=x2_sb[:, fo * TT:(fo + 1) * TT], op=ALU.add))
                dve_pad(1)
                dn_ev[gfo] = O("vector", lambda v: v.tensor_copy(out=pad1[:], in_=pad1[:]), "sV")
                gfo += 1

            for ti in range(NTILES):
                t0 = ti * TT
                if out_st is not None:
                    W("sync", "dS", out_st)
                for cg in range(4):
                    O("sync", lambda e, cg=cg, t0=t0:
                      e.dma_start(out=x2_sb3[:, cg * 4:(cg + 1) * 4, :],
                                  in_=x23[:, cg * 4:(cg + 1) * 4, t0:t0 + TT]), "dS", 16)
                ld_x2 = em.val("dS")
                W("vector", "dS", ld_x2)
                O("vector", lambda v: v.tensor_copy(out=xln[:], in_=x2_sb[:]))
                dve_pad(1)
                v_c = O("vector", lambda v: v.tensor_copy(out=pad1[:], in_=pad1[:]), "sV")
                W("scalar", "sV", v_c)
                W("tensor", "sV", v_c)

                def mean_mms2(t):
                    for kc in range(NCH):
                        r = t.matmul(psR[0][:], lhsT=ones_col[:], rhs=xln[:, kc * TT:(kc + 1) * TT],
                                     start=(kc == 0), stop=(kc == NCH - 1))
                    return r
                O("tensor", mean_mms2)
                sq_used = {}
                for kc in range(NCH):
                    if kc >= 3:
                        W("scalar", "sP", sq_used[kc - 3])
                    ak = O("scalar", lambda s, kc=kc: s.activation(
                        out=sq_c[kc % 3][:], in_=xln[:, kc * TT:(kc + 1) * TT],
                        func=AF.Square), "sA")
                    W("tensor", "sA", ak)
                    sq_used[kc] = O("tensor", lambda t, kc=kc: t.matmul(
                        psR[1][:], lhsT=ones_col[:], rhs=sq_c[kc % 3][:],
                        start=(kc == 0), stop=(kc == NCH - 1)), "sP")
                p_ln = em.val("sP")
                W("scalar", "sP", p_ln)
                O("scalar", lambda s: s.activation(out=mean_r, in_=psR[0][:], func=AF.Copy,
                                                   scale=c_inv[:]))
                O("scalar", lambda s: s.activation(out=msq_r, in_=psR[1][:], func=AF.Copy,
                                                   scale=c_inv[:]))
                a_mr = O("scalar", lambda s: s.activation(out=pad1[:], in_=pad1[:],
                                                          func=AF.Copy), "sA")
                W("vector", "sA", a_mr)
                O("vector", lambda v: v.tensor_tensor(out=tv_r, in0=mean_r, in1=mean_r, op=ALU.mult))
                dve_pad(1)
                O("vector", lambda v: v.tensor_tensor(out=var_r, in0=msq_r, in1=tv_r, op=ALU.subtract))
                dve_pad(1)
                v_var = O("vector", lambda v: v.tensor_copy(out=pad1[:], in_=pad1[:]), "sV")
                W("scalar", "sV", v_var)
                O("scalar", lambda s: s.activation(out=lnv_r, in_=var_r, func=AF.Ln, bias=c_eps[:]))
                act_pad(1)
                O("scalar", lambda s: s.activation(out=rstd_r, in_=lnv_r, func=AF.Exp,
                                                   scale=c_nh[:]))
                a_std = O("scalar", lambda s: s.activation(out=pad1[:], in_=pad1[:],
                                                           func=AF.Copy), "sA")
                W("vector", "sA", a_std)
                O("vector", lambda v: v.tensor_copy(out=rstd_bfv, in_=rstd_r))
                O("vector", lambda v: v.tensor_copy(out=mean_bfv, in_=mean_r))
                dve_pad(1)
                v_rw = O("vector", lambda v: v.tensor_copy(out=pad1[:], in_=pad1[:]), "sV")
                W("tensor", "sV", v_rw)
                if gfo >= 1:
                    W("tensor", "sV", dn_ev[gfo - 1])

                def bc2(t):
                    t.matmul(ps6[4][:], lhsT=ones_row[:], rhs=rstd_bfv, start=True, stop=True)
                    return t.matmul(ps6[5][:], lhsT=ones_row[:], rhs=mean_bfv, start=True, stop=True)
                p_bc2 = O("tensor", bc2, "sP")
                W("scalar", "sP", p_bc2)
                O("scalar", lambda s: s.activation(out=R_t[:], in_=ps6[4][:], func=AF.Copy))
                O("scalar", lambda s: s.activation(out=M_t[:], in_=ps6[5][:], func=AF.Copy))
                a_bc = O("scalar", lambda s: s.activation(out=pad1[:], in_=pad1[:],
                                                          func=AF.Copy), "sA")
                W("vector", "sA", a_bc)

                def center_ops(v):
                    for oc in range(NCH):
                        v.tensor_tensor(out=xln[:, oc * TT:(oc + 1) * TT],
                                        in0=xln[:, oc * TT:(oc + 1) * TT],
                                        in1=M_t[:], op=ALU.subtract)
                    return v.tensor_copy(out=pad1[:], in_=pad1[:])
                O("vector", center_ops)
                v_xln = O("vector", lambda v: v.tensor_copy(out=pad1[:], in_=pad1[:]), "sV")
                W("tensor", "sV", v_xln)

                for ns in range(NSUP):
                    par = ns % 2
                    first_write_guard = last_rd.get(par)
                    for g in range(NGRP):
                        gp = (glob // JPG) % 2
                        grp_abs = ns * NGRP + g
                        if glob >= 2 * JPG:
                            W("sync", "sP", gu_pm[glob - JPG - 1])
                        O("sync", lambda e, gi=grp_abs, b=gbuf[gp]:
                          e.dma_start(out=b, in_=gdec[gi].rearrange("p kc n -> p (kc n)")),
                          "dS", 16)
                        O("sync", lambda e, gi=grp_abs, b=ubuf[gp]:
                          e.dma_start(out=b, in_=udec[gi].rearrange("p kc n -> p (kc n)")),
                          "dS", 16)
                        ldg = em.val("dS")
                        for j in range(JPG):
                            jj = g * JPG + j
                            p = glob % 2
                            W("tensor", "dS", ldg)
                            G = ps6[p]
                            U = ps6[2 + p]
                            if glob >= 2:
                                W("tensor", "sV", rel_v[glob - 2])
                                W("tensor", "sA", rel_a[glob - 2])

                            def gu_mms(t, b=gbuf[gp], bu=ubuf[gp], j=j, G=G, U=U):
                                for kc in range(NCH):
                                    t.matmul(G[:], lhsT=b[:, kc * GRP + j * 128:kc * GRP + (j + 1) * 128],
                                             rhs=xln[:, kc * TT:(kc + 1) * TT],
                                             start=(kc == 0), stop=(kc == NCH - 1))
                                for kc in range(NCH):
                                    r = t.matmul(U[:], lhsT=bu[:, kc * GRP + j * 128:kc * GRP + (j + 1) * 128],
                                                 rhs=xln[:, kc * TT:(kc + 1) * TT],
                                                 start=(kc == 0), stop=(kc == NCH - 1))
                                return r
                            pm = O("tensor", gu_mms, "sP")
                            gu_pm[glob] = pm

                            W("vector", "sP", pm)
                            O("vector", lambda v, G=G, p=p: v.tensor_tensor(
                                out=gcp[p][:], in0=G[:], in1=R_t[:], op=ALU.mult))
                            O("vector", lambda v, U=U, p=p: v.tensor_tensor(
                                out=upc[p][:], in0=U[:], in1=R_t[:], op=ALU.mult))
                            dve_pad(1)
                            va = O("vector", lambda v: v.tensor_copy(out=pad1[:], in_=pad1[:]), "sV")
                            rel_v[glob] = va

                            W("scalar", "sV", va)
                            O("scalar", lambda s, p=p: s.activation(
                                out=g2t[p][:], in_=gcp[p][:], func=AF.Square))
                            aa = O("scalar", lambda s: s.activation(out=pad1[:], in_=pad1[:],
                                                                    func=AF.Copy), "sA")
                            rel_a[glob] = aa

                            W("vector", "sA", aa)
                            O("vector", lambda v, p=p: v.tensor_scalar(
                                out=g2t[p][:], in0=g2t[p][:], scalar1=GA, scalar2=GB,
                                op0=ALU.mult, op1=ALU.add))
                            dve_pad(1)
                            O("vector", lambda v, p=p: v.tensor_tensor(
                                out=g2t[p][:], in0=g2t[p][:], in1=gcp[p][:], op=ALU.mult))
                            dve_pad(1)
                            vb = O("vector", lambda v: v.tensor_copy(out=pad1[:], in_=pad1[:]), "sV")

                            W("scalar", "sV", vb)
                            O("scalar", lambda s, p=p: s.activation(
                                out=tnh[p][:], in_=g2t[p][:], func=AF.Tanh))
                            at = O("scalar", lambda s: s.activation(out=pad1[:], in_=pad1[:],
                                                                    func=AF.Copy), "sA")

                            W("vector", "sA", at)
                            if jj == 0 and first_write_guard is not None:
                                W("vector", "sP", first_write_guard)
                            O("vector", lambda v, p=p: v.tensor_scalar(
                                out=tnh[p][:], in0=tnh[p][:], scalar1=0.5, scalar2=0.5,
                                op0=ALU.mult, op1=ALU.add))
                            dve_pad(1)
                            O("vector", lambda v, p=p: v.tensor_tensor(
                                out=gel[p][:], in0=tnh[p][:], in1=gcp[p][:], op=ALU.mult))
                            dve_pad(1)
                            O("vector", lambda v, p=p, jj=jj, par=par: v.tensor_tensor(
                                out=inter2[par][:, jj * TT:(jj + 1) * TT], in0=gel[p][:],
                                in1=upc[p][:], op=ALU.mult))
                            dve_pad(1)
                            vv = O("vector", lambda v: v.tensor_copy(out=pad1[:], in_=pad1[:]), "sV")
                            vv_g[glob] = vv
                            glob += 1

                            if ns >= 1:
                                emit_dn(ti, ns - 1, jj, 1 - par,
                                        bc_guard=(a_bc if (ns == 1 and jj < 2) else None))
                    inter_done[(ti, ns)] = vv_g[glob - 1]
                for fo in range(NCH):
                    emit_dn(ti, NSUP - 1, fo, (NSUP - 1) % 2)
                acc_done = em.val("sV")
                W("sync", "sV", acc_done)
                for oc in range(NCH):
                    O("sync", lambda e, oc=oc, t0=t0:
                      e.dma_start(out=out[oc * 128:(oc + 1) * 128, t0:t0 + TT],
                                  in_=x2_sb[:, oc * TT:(oc + 1) * TT]), "dS", 16)
                out_st = em.val("dS")

        if dbg_outs:
            W("gpsimd", "dA", em.val("dA"))
            W("gpsimd", "dS", em.val("dS"))
            W("gpsimd", "sV", em.val("sV"))
            if "o_gdec" in dbg_outs:
                O("gpsimd", lambda e: e.dma_start(
                    out=dbg_outs["o_gdec"].rearrange("a b c d -> (a b) (c d)"),
                    in_=gdec.rearrange("a b c d -> (a b) (c d)")), "dG", 16)
                O("gpsimd", lambda e: e.dma_start(
                    out=dbg_outs["o_ddec"].rearrange("a b c d e -> (a b) (c d e)"),
                    in_=ddec.rearrange("a b c d e -> (a b) (c d e)")), "dG", 16)
            if "o_x2" in dbg_outs:
                O("gpsimd", lambda e: e.dma_start(out=dbg_outs["o_x2"], in_=x2d), "dG", 16)

        W("sync", "dS", em.val("dS"))
        W("sync", "dA", em.val("dA"))
        W("sync", "dG", em.val("dG"))
        W("gpsimd", "dG", em.val("dG"))
        W("scalar", "dA", em.val("dA"))

        with nc.Block() as block:
            @block.sync
            def _(e):
                for fn in em.ops["sync"]:
                    fn(e)

            @block.gpsimd
            def _(e):
                for fn in em.ops["gpsimd"]:
                    fn(e)

            @block.vector
            def _(e):
                for fn in em.ops["vector"]:
                    fn(e)

            @block.scalar
            def _(e):
                for fn in em.ops["scalar"]:
                    fn(e)

            @block.tensor
            def _(e):
                for fn in em.ops["tensor"]:
                    fn(e)

    return nc


def _stage_inputs(L0_attn_x, L0_attn_W_q, L0_attn_W_kv, L0_attn_W_o,
                  L0_ffn_gate_W_packed, L0_ffn_gate_scales,
                  L0_ffn_up_W_packed, L0_ffn_up_scales,
                  L0_ffn_down_W_packed, L0_ffn_down_scales):
    pi = perm2048()
    PI = perm16384()

    def tile_wT(wT):
        return np.ascontiguousarray(wT.reshape(16, 128, 16, 128).transpose(2, 1, 0, 3))

    x = np.asarray(L0_attn_x, dtype=np.float32)
    wq = np.asarray(L0_attn_W_q, dtype=np.float32)
    wkv = np.asarray(L0_attn_W_kv, dtype=np.float32)
    wo = np.asarray(L0_attn_W_o, dtype=np.float32)

    wq_t = tile_wT(np.ascontiguousarray(wq[:, pi].T))
    wkv_t = tile_wT(np.ascontiguousarray(wkv[:, pi].T))
    wo_t = tile_wT(np.ascontiguousarray(wo[pi, :].T))

    def pack16(Wp):
        return np.ascontiguousarray(np.asarray(Wp).view(np.int16)[:, 0::2])

    g16 = np.ascontiguousarray(pack16(L0_ffn_gate_W_packed)[PI, :].T)
    u16 = np.ascontiguousarray(pack16(L0_ffn_up_W_packed)[PI, :].T)
    d16 = np.ascontiguousarray(pack16(L0_ffn_down_W_packed)[pi, :].T)

    gs = np.asarray(L0_ffn_gate_scales, dtype=np.float32)
    us = np.asarray(L0_ffn_up_scales, dtype=np.float32)
    ds = np.asarray(L0_ffn_down_scales, dtype=np.float32)
    gse = np.ascontiguousarray(np.repeat(gs[PI, :].T, 16, axis=0))
    use2 = np.ascontiguousarray(np.repeat(us[PI, :].T, 16, axis=0))
    dse = np.ascontiguousarray(np.repeat(ds[pi, :].T, 16, axis=0))

    ar = np.arange(128, dtype=np.float64)
    ph = 2.0 * np.pi * np.outer(ar, ar) / 128.0
    dftc = np.cos(ph).astype(ml_dtypes.bfloat16)
    dfts = (-np.sin(ph)).astype(ml_dtypes.bfloat16)
    dftci = (np.cos(ph) / 128.0).astype(ml_dtypes.bfloat16)
    dftsi = (-np.sin(ph) / 128.0).astype(ml_dtypes.bfloat16)

    in_maps = []
    for c in range(NCORES):
        xs = [np.ascontiguousarray(x[c * SPC + b].T[pi]) for b in range(SPC)]
        x_core = np.ascontiguousarray(np.concatenate(xs, axis=1))
        in_maps.append(dict(
            x_in=x_core, wq_t=wq_t, wkv_t=wkv_t, wo_t=wo_t,
            g16=g16, u16=u16, d16=d16, gse=gse, use=use2, dse=dse,
            dftc=dftc, dfts=dfts, dftci=dftci, dftsi=dftsi,
        ))
    return in_maps, pi


def kernel(**inputs):
    in_maps, pi = _stage_inputs(**inputs)
    nc = build_nc()
    res = run_bass_kernel_spmd(nc, in_maps, core_ids=list(range(NCORES)))
    outp = np.empty((NB, S, D), dtype=np.float32)
    for c in range(NCORES):
        r = np.asarray(res.results[c]["out"])
        for b in range(SPC):
            seq = r[:, b * S:(b + 1) * S]
            o = np.empty((S, D), dtype=np.float32)
            o[:, pi] = seq.T
            outp[c * SPC + b] = o
    return outp


# version 28
# speedup vs baseline: 1.0552x; 1.0552x over previous
import numpy as np
import ml_dtypes
from contextlib import ExitStack

import concourse.bass as bass
import concourse.mybir as mybir
from concourse.bass_utils import run_bass_kernel_spmd

F32 = mybir.dt.float32
BF16 = mybir.dt.bfloat16
I16 = mybir.dt.int16
ALU = mybir.AluOpType
AF = mybir.ActivationFunctionType

NB, S, D = 16, 2048, 2048
NCORES = 8
SPC = NB // NCORES
TCORE = SPC * S
TT = 512
NTILES = TCORE // TT
NCH = D // 128
H = 16
MLP = 16384
NSUPER = 2048
NSUP = MLP // NSUPER
NCPS = NSUPER // 128
GRP = 256
NGRP = NSUPER // GRP
JPG = GRP // 128
A_LEVEL = 4
M_LEVEL = 5
M_SQ = True
M_CP = True


def perm2048():
    p = np.empty(2048, dtype=np.int64)
    j = 0
    for c in range(8):
        for par in range(2):
            for q in range(128):
                p[j] = 2 * (128 * c + q) + par
                j += 1
    return p


def perm16384():
    p = np.empty(16384, dtype=np.int64)
    j = 0
    for c in range(64):
        for par in range(2):
            for q in range(128):
                p[j] = 2 * (128 * c + q) + par
                j += 1
    return p


class Em:
    def __init__(self):
        self.ops = {k: [] for k in ("sync", "gpsimd", "vector", "scalar", "tensor")}
        self.cnt = {}

    def val(self, name):
        return self.cnt.get(name, 0)

    def bump(self, name, k):
        self.cnt[name] = self.cnt.get(name, 0) + k
        return self.cnt[name]


class Pool:
    def __init__(self, tensor, n_i16):
        self.t = tensor
        self.n = n_i16
        self.off = 0

    def take(self, shape, dt):
        assert shape[0] == 128
        elems = int(np.prod(shape[1:]))
        nbytes = elems * mybir.dt.size(dt)
        assert nbytes % 2 == 0
        n16 = nbytes // 2
        off = (self.off + 31) & ~31
        assert off + n16 <= self.n, f"pool overflow: {off}+{n16}>{self.n}"
        self.off = off + n16
        ap = self.t[:, off:off + n16].bitcast(dt)
        if len(shape) > 2:
            ap = ap.rearrange("p (a b) -> p a b", b=shape[-1])
        return ap


def build_nc(phases=("dec", "A", "B"), dbg=(), detect_races=True):
    nc = bass.Bass("TRN2", target_bir_lowering=False,
                   detect_race_conditions=detect_races)

    x_in = nc.dram_tensor("x_in", [D, TCORE], F32, kind="ExternalInput").ap()
    wq_t = nc.dram_tensor("wq_t", [16, 128, 16, 128], F32, kind="ExternalInput").ap()
    wkv_t = nc.dram_tensor("wkv_t", [16, 128, 16, 128], F32, kind="ExternalInput").ap()
    wo_t = nc.dram_tensor("wo_t", [16, 128, 16, 128], F32, kind="ExternalInput").ap()
    g16 = nc.dram_tensor("g16", [1024, MLP], I16, kind="ExternalInput").ap()
    u16 = nc.dram_tensor("u16", [1024, MLP], I16, kind="ExternalInput").ap()
    d16 = nc.dram_tensor("d16", [8192, 2048], I16, kind="ExternalInput").ap()
    gse = nc.dram_tensor("gse", [1024, MLP], F32, kind="ExternalInput").ap()
    use_ = nc.dram_tensor("use", [1024, MLP], F32, kind="ExternalInput").ap()
    dse = nc.dram_tensor("dse", [8192, 2048], F32, kind="ExternalInput").ap()
    dftc = nc.dram_tensor("dftc", [128, 128], BF16, kind="ExternalInput").ap()
    dfts = nc.dram_tensor("dfts", [128, 128], BF16, kind="ExternalInput").ap()
    dftci = nc.dram_tensor("dftci", [128, 128], BF16, kind="ExternalInput").ap()
    dftsi = nc.dram_tensor("dftsi", [128, 128], BF16, kind="ExternalInput").ap()

    out = nc.dram_tensor("out", [D, TCORE], F32, kind="ExternalOutput").ap()

    NGRPS = MLP // GRP
    gdec = nc.dram_tensor("gdec", [NGRPS, 128, NCH, GRP], BF16, kind="Internal").ap()
    udec = nc.dram_tensor("udec", [NGRPS, 128, NCH, GRP], BF16, kind="Internal").ap()
    ddec = nc.dram_tensor("ddec", [NSUP, NCH, 128, NCPS, 128], BF16, kind="Internal").ap()
    x2d = nc.dram_tensor("x2d", [D, TCORE], F32, kind="Internal").ap()

    dbg_outs = {}
    if "dec" in dbg:
        dbg_outs["o_gdec"] = nc.dram_tensor("o_gdec", [MLP // GRP, 128, NCH, GRP], BF16,
                                            kind="ExternalOutput").ap()
        dbg_outs["o_ddec"] = nc.dram_tensor("o_ddec", [NSUP, NCH, 128, NCPS, 128], BF16,
                                            kind="ExternalOutput").ap()
    if "x2" in dbg:
        dbg_outs["o_x2"] = nc.dram_tensor("o_x2", [D, TCORE], F32, kind="ExternalOutput").ap()

    em = Em()

    with ExitStack() as ctx:
        E = ctx.enter_context
        dS = E(nc.semaphore("dS"))
        dG = E(nc.semaphore("dG"))
        dA = E(nc.semaphore("dA"))
        sP = E(nc.semaphore("sP"))
        sV = E(nc.semaphore("sV"))
        sA = E(nc.semaphore("sA"))
        sG = E(nc.semaphore("sG"))
        SEM = {"dS": dS, "dG": dG, "dA": dA, "sP": sP, "sV": sV, "sA": sA, "sG": sG}

        def W(eng, name, v):
            em.ops[eng].append(lambda e, s=SEM[name], vv=v: e.wait_ge(s, vv))

        def O(eng, fn, incname=None, k=1):
            if incname is None:
                em.ops[eng].append(fn)
                return None
            v = em.bump(incname, k)
            em.ops[eng].append(lambda e, f=fn, s=SEM[incname], kk=k: f(e).then_inc(s, kk))
            return v

        ones_col = E(nc.sbuf_tensor("ones_col", [128, 1], BF16))
        ones_row = E(nc.sbuf_tensor("ones_row", [1, 128], BF16))
        ones_all = E(nc.sbuf_tensor("ones_all", [128, 128], BF16))
        zeros_t = E(nc.sbuf_tensor("zeros_t", [128, TT], F32))
        c_neg = E(nc.sbuf_tensor("c_neg", [1, 1], F32))
        c_inv = E(nc.sbuf_tensor("c_inv", [1, 1], F32))
        c_eps = E(nc.sbuf_tensor("c_eps", [1, 1], F32))
        c_nh = E(nc.sbuf_tensor("c_nh", [1, 1], F32))
        c_nh128 = E(nc.sbuf_tensor("c_nh128", [128, 1], F32))
        c_lnb = E(nc.sbuf_tensor("c_lnb", [128, 1], F32))
        t_dftc = E(nc.sbuf_tensor("t_dftc", [128, 128], BF16))
        t_dfts = E(nc.sbuf_tensor("t_dfts", [128, 128], BF16))
        t_dftci = E(nc.sbuf_tensor("t_dftci", [128, 128], BF16))
        t_dftsi = E(nc.sbuf_tensor("t_dftsi", [128, 128], BF16))
        pad1 = E(nc.sbuf_tensor("pad1", [1, 8], F32))
        pad2 = E(nc.sbuf_tensor("pad2", [1, 8], F32))
        rowS = E(nc.sbuf_tensor("rowS", [1, 8 * TT], F32))
        negm_bf = E(nc.sbuf_tensor("negm_bf", [1, TT], BF16))
        wbar_q = E(nc.sbuf_tensor("wbar_q", [1, 2048], BF16))
        wbar_kv = E(nc.sbuf_tensor("wbar_kv", [1, 2048], BF16))

        POOL_N = 89 * 1024
        pool_t = E(nc.sbuf_tensor("pool", [128, POOL_N], I16))

        ps6 = [E(nc.psum_tensor(f"ps{i}", [128, TT], F32)) for i in range(8)]
        psR = [ps6[6][0:1, :], ps6[7][0:1, :]]

        O("vector", lambda v: v.memset(ones_col[:], 1.0))
        O("vector", lambda v: v.memset(ones_row[:], 1.0))
        O("vector", lambda v: v.memset(ones_all[:], 1.0))
        O("vector", lambda v: v.memset(zeros_t[:], 0.0))
        O("vector", lambda v: v.memset(c_neg[:], -1.0 / 2048.0))
        O("vector", lambda v: v.memset(c_inv[:], 1.0 / 2048.0))
        O("vector", lambda v: v.memset(c_eps[:], 1e-5))
        O("vector", lambda v: v.memset(c_nh[:], -0.5))
        O("vector", lambda v: v.memset(c_nh128[:], -0.5))
        O("vector", lambda v: v.memset(c_lnb[:], -2.4260151319598084))
        O("vector", lambda v: v.memset(pad1[:], 0.0))
        O("gpsimd", lambda g: g.memset(pad2[:], 0.0))
        O("vector", lambda v: v.tensor_copy(out=pad1[:], in_=pad1[:]))
        v_init = O("vector", lambda v: v.tensor_copy(out=pad1[:], in_=pad1[:]), "sV")
        O("sync", lambda e: e.dma_start(out=t_dftc[:], in_=dftc), "dS", 16)
        O("sync", lambda e: e.dma_start(out=t_dfts[:], in_=dfts), "dS", 16)
        O("sync", lambda e: e.dma_start(out=t_dftci[:], in_=dftci), "dS", 16)
        d_init = O("sync", lambda e: e.dma_start(out=t_dftsi[:], in_=dftsi), "dS", 16)

        def dve_pad(n=1):
            for _ in range(n):
                O("vector", lambda v: v.tensor_copy(out=pad1[:], in_=pad1[:]))

        def act_pad(n=1):
            for _ in range(n):
                O("scalar", lambda s: s.activation(out=pad1[:], in_=pad1[:], func=AF.Copy))

        def gp_pad(n=1):
            for _ in range(n):
                O("gpsimd", lambda g: g.tensor_copy(out=pad2[:], in_=pad2[:]))

        if "dec" in phases:
            dp = Pool(pool_t, POOL_N)
            FD = 4096
            pk_b = [dp.take([128, FD], I16) for _ in range(2)]
            sc_b = [dp.take([128, FD], F32) for _ in range(2)]
            uL = dp.take([128, FD], I16); uH = dp.take([128, FD], I16)
            eL = dp.take([128, FD], I16); eH = dp.take([128, FD], I16)
            mL = dp.take([128, FD], I16); mH = dp.take([128, FD], I16)
            bL = dp.take([128, FD], I16); bH = dp.take([128, FD], I16)
            plL = [dp.take([128, FD], BF16) for _ in range(2)]
            plH = [dp.take([128, FD], BF16) for _ in range(2)]

            tiles = []
            for mat, scl, dst in ((g16, gse, gdec), (u16, use_, udec)):
                for c in range(8):
                    for t4 in range(4):
                        n0 = t4 * 4096
                        tiles.append((
                            mat[c * 128:(c + 1) * 128, n0:n0 + 4096],
                            scl[c * 128:(c + 1) * 128, n0:n0 + 4096],
                            (dst, c, n0),
                            False))
            for c2 in range(32):
                r0 = c2 * 256
                dsts = []
                for half in range(2):
                    C = 2 * c2 + half
                    for par in range(2):
                        nch_ = 2 * C + par
                        nsx, r = nch_ // NCPS, nch_ % NCPS
                        dsts.append(ddec[nsx, :, :, r, :].transpose([1, 0, 2]))
                tiles.append((
                    d16[r0:r0 + 256, :].rearrange("(h p) f -> p h f", p=128),
                    dse[r0:r0 + 256, :].rearrange("(h p) f -> p h f", p=128),
                    dsts, True))

            dec_done = []
            st_cnt = []
            for i, (src, ssrc, dsts, is_down) in enumerate(tiles):
                pg = i % 2
                if i >= 2:
                    W("sync", "sV", dec_done[i - 2])
                pk_v, sc_v = pk_b[pg], sc_b[pg]
                if is_down:
                    pk_v = pk_v.rearrange("p (h f) -> p h f", h=2)
                    sc_v = sc_v.rearrange("p (h f) -> p h f", h=2)
                O("sync", lambda e, o=pk_v, s=src: e.dma_start(out=o, in_=s), "dS", 16)
                nl1 = O("sync", lambda e, o=sc_v, s=ssrc: e.dma_start(out=o, in_=s), "dS", 16)
                W("vector", "dS", nl1)
                if i >= 2:
                    W("vector", "dA", st_cnt[i - 2])
                pk, sc = pk_b[pg], sc_b[pg]
                oL, oH = plL[pg], plH[pg]

                def dec_ops(v, pk=pk, sc=sc, oL=oL, oH=oH):
                    v.tensor_scalar(out=uL, in0=pk, scalar1=7, scalar2=None, op0=ALU.bitwise_and)
                    v.tensor_scalar(out=uH, in0=pk, scalar1=4, scalar2=7,
                                    op0=ALU.logical_shift_right, op1=ALU.bitwise_and)
                    v.tensor_scalar(out=mL, in0=uL, scalar1=1.0, scalar2=-1.0,
                                    op0=ALU.is_ge, op1=ALU.mult)
                    v.tensor_scalar(out=mH, in0=uH, scalar1=1.0, scalar2=-1.0,
                                    op0=ALU.is_ge, op1=ALU.mult)
                    v.tensor_scalar(out=eL, in0=uL, scalar1=1.0, scalar2=None, op0=ALU.is_equal)
                    v.tensor_scalar(out=eH, in0=uH, scalar1=1.0, scalar2=None, op0=ALU.is_equal)
                    v.tensor_tensor(out=uL, in0=uL, in1=eL, op=ALU.bitwise_xor)
                    v.tensor_tensor(out=uH, in0=uH, in1=eH, op=ALU.bitwise_xor)
                    v.tensor_scalar(out=bL, in0=uL, scalar1=64.0, scalar2=16128.0,
                                    op0=ALU.mult, op1=ALU.add)
                    v.tensor_scalar(out=bH, in0=uH, scalar1=64.0, scalar2=16128.0,
                                    op0=ALU.mult, op1=ALU.add)
                    v.tensor_tensor(out=bL, in0=bL, in1=mL, op=ALU.bitwise_and)
                    v.tensor_tensor(out=bH, in0=bH, in1=mH, op=ALU.bitwise_and)
                    v.tensor_scalar(out=eL, in0=pk, scalar1=8, scalar2=12,
                                    op0=ALU.bitwise_and, op1=ALU.logical_shift_left)
                    v.tensor_scalar(out=eH, in0=pk, scalar1=128, scalar2=8,
                                    op0=ALU.bitwise_and, op1=ALU.logical_shift_left)
                    v.tensor_tensor(out=bL, in0=bL, in1=eL, op=ALU.bitwise_or)
                    v.tensor_tensor(out=bH, in0=bH, in1=eH, op=ALU.bitwise_or)
                    v.tensor_tensor(out=oL, in0=bL.bitcast(BF16), in1=sc, op=ALU.mult)
                    return v.tensor_tensor(out=oH, in0=bH.bitcast(BF16), in1=sc, op=ALU.mult)
                O("vector", dec_ops)
                dve_pad(1)
                dv = O("vector", lambda v: v.tensor_copy(out=pad1[:], in_=pad1[:]), "sV")
                dec_done.append(dv)
                W("scalar", "sV", dv)
                if not is_down:
                    dst3, cc, n0_ = dsts
                    for par, opl in ((0, oL), (1, oH)):
                        kc = 2 * cc + par
                        for s in range(4096 // GRP):
                            sc2 = O("scalar",
                                    lambda e, o=dst3[n0_ // GRP + s][:, kc, :],
                                    s_=opl[:, s * GRP:(s + 1) * GRP]:
                                    e.dma_start(out=o, in_=s_), "dA", 16)
                else:
                    for hh in range(2):
                        for par, opl in ((0, oL), (1, oH)):
                            srcv = opl.rearrange("p (h f) -> p h f", h=2)[:, hh, :]
                            srcv = srcv.rearrange("p (fo f) -> p fo f", fo=16)
                            sc2 = O("scalar",
                                    lambda e, o=dsts[hh * 2 + par], s=srcv: e.dma_start(out=o, in_=s),
                                    "dA", 16)
                st_cnt.append(sc2)
        dec_dS = em.val("dS")
        dec_dA = em.val("dA")
        dec_sV = em.val("sV")

        if "A" in phases:
            ap_ = Pool(pool_t, POOL_N)
            x_f32 = ap_.take([128, NCH * TT], F32)
            x_bf = ap_.take([128, NCH * TT], BF16)
            q_bf = ap_.take([128, NCH * TT], BF16)
            kv_bf = ap_.take([128, NCH * TT], BF16)
            vh_bf = ap_.take([128, NCH * TT], BF16)
            wbuf = [ap_.take([128, 2048], BF16) for _ in range(2)]
            qfre = [ap_.take([128, TT], BF16) for _ in range(3)]
            qfim = [ap_.take([128, TT], BF16) for _ in range(3)]
            kvre = [ap_.take([128, TT], BF16) for _ in range(3)]
            kvim = [ap_.take([128, TT], BF16) for _ in range(3)]
            sqst = [ap_.take([128, TT], BF16) for _ in range(2)]
            kvnre = [ap_.take([128, TT], BF16) for _ in range(2)]
            kvnim = [ap_.take([128, TT], BF16) for _ in range(2)]
            kvcre = ap_.take([128, TT], F32); kvcim = ap_.take([128, TT], F32)
            t1 = ap_.take([128, TT], F32); t2 = ap_.take([128, TT], F32)
            t3 = ap_.take([128, TT], F32); t4 = ap_.take([128, TT], F32)
            qvre = ap_.take([128, TT], BF16); qvim = ap_.take([128, TT], BF16)
            lq_t = ap_.take([128, TT], F32); lkv_t = ap_.take([128, TT], F32)
            rq_t = [ap_.take([128, TT], F32) for _ in range(3)]
            rkv_t = [ap_.take([128, TT], F32) for _ in range(3)]
            carry_re = ap_.take([128, H], F32); carry_im = ap_.take([128, H], F32)
            x2st = [ap_.take([128, TT], F32) for _ in range(2)]
            n2st = ap_.take([128, 16 * TT], BF16)

            def n2row(r):
                return n2st[64 * (r % 2):64 * (r % 2) + 1,
                            (r // 2) * TT:(r // 2 + 1) * TT]

            W("vector", "dS", d_init)
            W("tensor", "sV", v_init)
            W("tensor", "dS", d_init)
            W("sync", "sV", dec_sV)
            W("sync", "dA", dec_dA)
            W("gpsimd", "sV", dec_sV)
            W("gpsimd", "dA", dec_dA)

            xin3 = x_in.rearrange("(c p) t -> p c t", p=128)
            x_f32_3 = x_f32.rearrange("p (c t) -> p c t", t=TT)
            pm_slot = {}
            ev_slot = {}
            o_ev = {}
            o_st = {}
            for ti in range(NTILES):
                t0 = ti * TT
                for cg in range(4):
                    O("sync", lambda e, cg=cg, t0=t0:
                      e.dma_start(out=x_f32_3[:, cg * 4:(cg + 1) * 4, :],
                                  in_=xin3[:, cg * 4:(cg + 1) * 4, t0:t0 + TT]), "dS", 16)
                ld_x = em.val("dS")
                W("vector", "dS", ld_x)
                O("vector", lambda v: v.tensor_copy(out=x_bf[:], in_=x_f32[:]))
                dve_pad(1)
                c_cast = O("vector", lambda v: v.tensor_copy(out=pad1[:], in_=pad1[:]), "sV")
                W("tensor", "sV", c_cast)

                def mean_mms(t):
                    for kc in range(NCH):
                        r = t.matmul(psR[0][:], lhsT=ones_col[:],
                                     rhs=x_bf[:, kc * TT:(kc + 1) * TT],
                                     start=(kc == 0), stop=(kc == NCH - 1))
                    return r
                p_mean = O("tensor", mean_mms, "sP")
                W("scalar", "sP", p_mean)
                O("scalar", lambda s: s.activation(out=negm_bf[:], in_=psR[0][:],
                                                   func=AF.Copy, scale=c_neg[:]))
                act_pad(1)
                a_negm = O("scalar", lambda s: s.activation(out=pad1[:], in_=pad1[:],
                                                            func=AF.Copy), "sA")

                a_sqp = {}; p_n2 = {}; v_n2 = {}

                def emit_n2(sl):
                    W("tensor", "sA", a_sqp[sl])
                    pb = psR[0] if ti == 0 else psR[sl % 2]
                    gidx = sl - 1 if ti == 0 else sl - 2
                    if gidx >= 0 and gidx in v_n2:
                        W("tensor", "sV", v_n2[gidx])
                    pmn = O("tensor", lambda t, pb=pb, sl=sl: t.matmul(
                        pb[:], lhsT=ones_col[:], rhs=sqst[sl % 2][:],
                        start=True, stop=True), "sP")
                    p_n2[sl] = pmn
                    W("vector", "sP", pmn)
                    O("vector", lambda v, pb=pb, sl=sl: v.tensor_copy(
                        out=n2row(sl), in_=pb[:]))
                    dve_pad(1)
                    v_n2[sl] = O("vector", lambda v: v.tensor_copy(out=pad1[:], in_=pad1[:]),
                                 "sV")

                for pri, (wt, dst, wbar) in (enumerate(((wq_t, q_bf, wbar_q),
                                                       (wkv_t, kv_bf, wbar_kv))) if A_LEVEL >= 2 else []):
                    for oc in range(NCH):
                        slot = pri * NCH + oc
                        pg = slot % 2
                        if slot >= 2:
                            emit_n2(slot - 2)
                        prev = (ti, slot - 2) if slot >= 2 else (ti - 1, 46 + slot)
                        if prev in pm_slot:
                            W("gpsimd", "sP", pm_slot[prev])
                        ldw = O("gpsimd", lambda e, wt=wt, oc=oc, b=wbuf[pg]:
                                e.dma_start(out=b, in_=wt[oc]), "dG", 16)
                        W("tensor", "dG", ldw)
                        if prev in ev_slot:
                            W("tensor", "sA", ev_slot[prev])
                        ps = ps6[pg]

                        def proj_mms(t, b=wbuf[pg], ps=ps):
                            for kc in range(NCH):
                                t.matmul(ps[:], lhsT=b[:, kc * 128:(kc + 1) * 128],
                                         rhs=x_bf[:, kc * TT:(kc + 1) * TT],
                                         start=(kc == 0), stop=False)
                            return t
                        O("tensor", proj_mms)
                        if ti == 0:
                            def wbar_mms(t, b=wbuf[pg]):
                                for kc in range(NCH):
                                    r = t.matmul(psR[1][:, 0:128],
                                                 lhsT=ones_col[:],
                                                 rhs=b[:, kc * 128:(kc + 1) * 128],
                                                 start=(kc == 0), stop=(kc == NCH - 1))
                                return r
                            pw = O("tensor", wbar_mms, "sP")
                            W("scalar", "sP", pw)
                            O("scalar", lambda s, w=wbar, oc=oc:
                              s.activation(out=w[:, oc * 128:(oc + 1) * 128],
                                           in_=psR[1][:, 0:128], func=AF.Copy))
                            aw = O("scalar", lambda s: s.activation(out=pad1[:], in_=pad1[:],
                                                                    func=AF.Copy), "sA")
                            W("tensor", "sA", aw)
                        if slot == 0:
                            W("tensor", "sA", a_negm)
                        pm = O("tensor", lambda t, w=wbar, oc=oc, ps=ps:
                               t.matmul(ps[:], lhsT=w[:, oc * 128:(oc + 1) * 128],
                                        rhs=negm_bf[:], start=False, stop=True), "sP")
                        pm_slot[(ti, slot)] = pm
                        W("scalar", "sP", pm)
                        O("scalar", lambda s, ps=ps, dst=dst, oc=oc:
                          s.activation(out=dst[:, oc * TT:(oc + 1) * TT], in_=ps[:],
                                       func=AF.Copy))
                        ae = O("scalar", lambda s: s.activation(out=pad1[:], in_=pad1[:],
                                                                func=AF.Copy), "sA")
                        ev_slot[(ti, slot)] = ae
                        if slot >= 2 and (slot - 2) in p_n2:
                            W("scalar", "sP", p_n2[slot - 2])
                        O("scalar", lambda s, dst=dst, oc=oc, sl=slot: s.activation(
                            out=sqst[sl % 2][:], in_=dst[:, oc * TT:(oc + 1) * TT],
                            func=AF.Square))
                        a_sqp[slot] = O("scalar", lambda s: s.activation(
                            out=pad1[:], in_=pad1[:], func=AF.Copy), "sA")
                if A_LEVEL >= 2:
                    emit_n2(30)
                    emit_n2(31)
                proj_all = em.val("sA")

                first_of_seq = (ti % (S // TT) == 0)
                f_pfft = {}; f_vcp = {}; f_acp = {}; f_pbc = {}; f_arsq = {}
                f_vscan = {}; f_vmix = {}; f_gmix = {}; f_gkvn = {}
                f_pifft = {}; f_vvh = {}
                for i in (range(H + 5) if A_LEVEL >= 3 else []):
                    if i < H:
                        hs = slice(i * TT, (i + 1) * TT)
                        if i == 0:
                            W("tensor", "sA", proj_all)
                        if i >= 1:
                            W("tensor", "sV", f_vcp[i - 1])
                            W("tensor", "sA", f_acp[i - 1])

                        def fft_mms(t, hs=hs):
                            t.matmul(ps6[0][:], lhsT=t_dftc[:], rhs=q_bf[:, hs], start=True, stop=True)
                            t.matmul(ps6[1][:], lhsT=t_dfts[:], rhs=q_bf[:, hs], start=True, stop=True)
                            t.matmul(ps6[2][:], lhsT=t_dftc[:], rhs=kv_bf[:, hs], start=True, stop=True)
                            return t.matmul(ps6[3][:], lhsT=t_dfts[:], rhs=kv_bf[:, hs],
                                            start=True, stop=True)
                        f_pfft[i] = O("tensor", fft_mms, "sP")
                    k = i - 2
                    if 0 <= k < H:
                        W("tensor", "sV", v_n2[16 + k])
                        if k >= 1:
                            W("tensor", "sA", f_arsq[k - 1])

                        def bcn_mms(t, k=k):
                            bkv = 64 * ((16 + k) % 2)
                            bq = 64 * (k % 2)
                            t.matmul(ps6[4][:], lhsT=ones_all[bkv:bkv + 1, :],
                                     rhs=n2row(16 + k), start=True, stop=True)
                            return t.matmul(ps6[5][:], lhsT=ones_all[bq:bq + 1, :],
                                            rhs=n2row(k), start=True, stop=True)
                        f_pbc[k] = O("tensor", bcn_mms, "sP")
                    k = i - 4
                    if 0 <= k < H:
                        W("tensor", "sV", f_vmix[k])
                        W("tensor", "sG", f_gmix[k])
                        if k >= 2:
                            W("tensor", "sV", f_vvh[k - 2])
                        elif k < 2:
                            W("tensor", "sV", v_n2[30 + k])

                        def ifft_mms(t, k=k):
                            pb = ps6[6 + k % 2]
                            t.matmul(pb[:], lhsT=t_dftci[:], rhs=qvre[:], start=True, stop=False)
                            return t.matmul(pb[:], lhsT=t_dftsi[:], rhs=qvim[:],
                                            start=False, stop=True)
                        f_pifft[k] = O("tensor", ifft_mms, "sP")

                    k = i - 5
                    if 0 <= k < H:
                        W("vector", "sP", f_pifft[k])
                        O("vector", lambda v, k=k: v.tensor_tensor(
                            out=vh_bf[:, k * TT:(k + 1) * TT], in0=ps6[6 + k % 2][:],
                            in1=rq_t[k % 3][:], op=ALU.mult))
                        dve_pad(1)
                        f_vvh[k] = O("vector", lambda v: v.tensor_copy(out=pad1[:], in_=pad1[:]),
                                     "sV")
                    k = i - 2
                    if 0 <= k < H:
                        W("scalar", "sP", f_pbc[k])
                        if k >= 3:
                            W("scalar", "sV", f_vvh[k - 3])

                        def rsq_ops(s, k=k):
                            s.activation(out=lkv_t[:], in_=ps6[4][:], func=AF.Ln)
                            s.activation(out=lq_t[:], in_=ps6[5][:], func=AF.Ln)
                            s.activation(out=rkv_t[k % 3][:], in_=lkv_t[:], func=AF.Exp,
                                         scale=c_nh128[:], bias=c_lnb[:])
                            return s.activation(out=rq_t[k % 3][:], in_=lq_t[:], func=AF.Exp,
                                                scale=c_nh128[:], bias=c_lnb[:])
                        O("scalar", rsq_ops)
                        act_pad(1)
                        f_arsq[k] = O("scalar", lambda s: s.activation(out=pad1[:], in_=pad1[:],
                                                                       func=AF.Copy), "sA")

                    k = i - 3
                    if 0 <= k < H:
                        W("vector", "sG", f_gkvn[k])
                        if k >= 1:
                            W("vector", "sG", f_gmix[k - 1])

                        def scan_ops(v, k=k, first=first_of_seq):
                            ire = 0.0 if first else carry_re[:, k:k + 1]
                            iim = 0.0 if first else carry_im[:, k:k + 1]
                            v.tensor_tensor_scan(out=kvcre[:], data0=kvnre[k % 2][:], data1=zeros_t[:],
                                                 initial=ire, op0=ALU.add, op1=ALU.add)
                            return v.tensor_tensor_scan(out=kvcim[:], data0=kvnim[k % 2][:], data1=zeros_t[:],
                                                        initial=iim, op0=ALU.add, op1=ALU.add)
                        O("vector", scan_ops)
                        dve_pad(1)
                        f_vscan[k] = O("vector", lambda v: v.tensor_copy(out=pad1[:], in_=pad1[:]),
                                       "sV")
                        if k >= 1:
                            W("vector", "sP", f_pifft[k - 1])

                        def cm_ops(v, k=k):
                            v.tensor_copy(out=carry_re[:, k:k + 1], in_=kvcre[:, TT - 1:TT])
                            v.tensor_copy(out=carry_im[:, k:k + 1], in_=kvcim[:, TT - 1:TT])
                            v.tensor_tensor(out=t1[:], in0=qfre[k % 3][:], in1=kvcre[:], op=ALU.mult)
                            return v.tensor_tensor(out=t2[:], in0=qfim[k % 3][:], in1=kvcim[:],
                                                   op=ALU.mult)
                        O("vector", cm_ops)
                        dve_pad(1)
                        O("vector", lambda v: v.tensor_tensor(out=qvre[:], in0=t1[:], in1=t2[:],
                                                              op=ALU.subtract))
                        dve_pad(1)
                        f_vmix[k] = O("vector", lambda v: v.tensor_copy(out=pad1[:], in_=pad1[:]),
                                      "sV")
                    k = i - 2
                    if 0 <= k < H:
                        W("gpsimd", "sA", f_arsq[k])
                        if k >= 2:
                            W("gpsimd", "sV", f_vscan[k - 2])

                        def gkvn_ops(g, k=k):
                            g.tensor_tensor(out=kvnre[k % 2][:], in0=kvre[k % 3][:], in1=rkv_t[k % 3][:],
                                            op=ALU.mult)
                            g.tensor_copy(out=pad2[:], in_=pad2[:])
                            return g.tensor_tensor(out=kvnim[k % 2][:], in0=kvim[k % 3][:],
                                                   in1=rkv_t[k % 3][:], op=ALU.mult)
                        O("gpsimd", gkvn_ops)
                        gp_pad(1)
                        f_gkvn[k] = O("gpsimd", lambda g: g.tensor_copy(out=pad2[:], in_=pad2[:]),
                                      "sG")
                    k = i - 3
                    if 0 <= k < H:
                        W("gpsimd", "sV", f_vscan[k])
                        if k >= 1:
                            W("gpsimd", "sP", f_pifft[k - 1])

                        def gmix_ops(g, k=k):
                            g.tensor_tensor(out=t3[:], in0=qfre[k % 3][:], in1=kvcim[:], op=ALU.mult)
                            g.tensor_tensor(out=t4[:], in0=qfim[k % 3][:], in1=kvcre[:], op=ALU.mult)
                            g.tensor_copy(out=pad2[:], in_=pad2[:])
                            return g.tensor_tensor(out=qvim[:], in0=t3[:], in1=t4[:], op=ALU.add)
                        O("gpsimd", gmix_ops)
                        gp_pad(1)
                        f_gmix[k] = O("gpsimd", lambda g: g.tensor_copy(out=pad2[:], in_=pad2[:]),
                                      "sG")
                    if i < H:
                        W("vector", "sP", f_pfft[i])
                        if i >= 3:
                            W("vector", "sG", f_gmix[i - 3])

                        def cp_ops(v, i=i):
                            v.tensor_copy(out=qfre[i % 3][:], in_=ps6[0][:])
                            return v.tensor_copy(out=qfim[i % 3][:], in_=ps6[1][:])
                        O("vector", cp_ops)
                        dve_pad(1)
                        f_vcp[i] = O("vector", lambda v: v.tensor_copy(out=pad1[:], in_=pad1[:]),
                                     "sV")
                    if i < H:
                        W("scalar", "sP", f_pfft[i])
                        if i >= 3:
                            W("scalar", "sV", f_vscan[i - 3])

                        def cpkv_ops(s, i=i):
                            s.activation(out=kvre[i % 3][:], in_=ps6[2][:], func=AF.Copy)
                            return s.activation(out=kvim[i % 3][:], in_=ps6[3][:], func=AF.Copy)
                        O("scalar", cpkv_ops)
                        act_pad(1)
                        f_acp[i] = O("scalar", lambda s: s.activation(out=pad1[:], in_=pad1[:],
                                                                      func=AF.Copy), "sA")
                mix_all = em.val("sV")

                for oc in (range(NCH) if A_LEVEL >= 4 else []):
                    slot = 32 + oc
                    pg = slot % 2
                    prev = (ti, slot - 2)
                    if prev in pm_slot:
                        W("gpsimd", "sP", pm_slot[prev])
                    ldw = O("gpsimd", lambda e, oc=oc, b=wbuf[pg]:
                            e.dma_start(out=b, in_=wo_t[oc]), "dG", 16)
                    W("tensor", "dG", ldw)
                    if oc == 0:
                        W("tensor", "sV", mix_all)
                    if (ti, oc - 2) in o_ev:
                        W("tensor", "sV", o_ev[(ti, oc - 2)])
                    ps = ps6[pg]

                    def omms(t, b=wbuf[pg], ps=ps):
                        for kc in range(NCH):
                            r = t.matmul(ps[:], lhsT=b[:, kc * 128:(kc + 1) * 128],
                                         rhs=vh_bf[:, kc * TT:(kc + 1) * TT],
                                         start=(kc == 0), stop=(kc == NCH - 1))
                        return r
                    pm = O("tensor", omms, "sP")
                    pm_slot[(ti, slot)] = pm
                    W("vector", "sP", pm)
                    if (ti, oc - 2) in o_st:
                        W("vector", "dS", o_st[(ti, oc - 2)])
                    O("vector", lambda v, ps=ps, oc=oc, pg=pg:
                      v.tensor_tensor(out=x2st[pg][:], in0=ps[:],
                                      in1=x_f32[:, oc * TT:(oc + 1) * TT], op=ALU.add))
                    dve_pad(1)
                    ev = O("vector", lambda v: v.tensor_copy(out=pad1[:], in_=pad1[:]), "sV")
                    o_ev[(ti, oc)] = ev
                    W("sync", "sV", ev)
                    st = O("sync", lambda e, oc=oc, pg=pg, t0=t0:
                           e.dma_start(out=x2d[oc * 128:(oc + 1) * 128, t0:t0 + TT],
                                       in_=x2st[pg][:]), "dS", 16)
                    o_st[(ti, oc)] = st
                W("sync", "sV", em.val("sV"))
        A_dS = em.val("dS")
        A_sV = em.val("sV")

        if "B" in phases:
            bp = Pool(pool_t, POOL_N)
            x2_sb = bp.take([128, NCH * TT], F32)
            xln = bp.take([128, NCH * TT], BF16)
            inter2 = [bp.take([128, NCPS * TT], BF16) for _ in range(2)]
            gbuf = [bp.take([128, 16 * GRP], BF16) for _ in range(2)]
            ubuf = [bp.take([128, 16 * GRP], BF16) for _ in range(2)]
            dbuf = [bp.take([128, NCPS * 128], BF16) for _ in range(4)]
            sq_c = [bp.take([128, TT], BF16) for _ in range(3)]
            R_t = bp.take([128, TT], F32)
            M_t = bp.take([128, TT], BF16)
            gcp = [bp.take([128, TT], F32) for _ in range(2)]
            g2t = [bp.take([128, TT], F32) for _ in range(2)]
            tnh = [bp.take([128, TT], F32) for _ in range(2)]
            gel = [bp.take([128, TT], BF16) for _ in range(2)]
            upc = [bp.take([128, TT], BF16) for _ in range(2)]
            mean_r = rowS[:, 0:TT]; msq_r = rowS[:, TT:2 * TT]
            tv_r = rowS[:, 2 * TT:3 * TT]; var_r = rowS[:, 3 * TT:4 * TT]
            lnv_r = rowS[:, 4 * TT:5 * TT]; rstd_r = rowS[:, 5 * TT:6 * TT]
            rstd_bfv = rowS[:, 6 * TT:6 * TT + TT // 2].bitcast(BF16)[:, 0:TT]
            mean_bfv = rowS[:, 6 * TT + TT // 2:7 * TT].bitcast(BF16)[:, 0:TT]

            W("tensor", "dA", dec_dA)
            W("tensor", "dS", A_dS)
            W("tensor", "sV", A_sV)
            W("sync", "dA", dec_dA)
            W("sync", "sV", A_sV)
            W("scalar", "sV", A_sV)
            W("scalar", "dS", A_dS)
            W("gpsimd", "dA", dec_dA)
            W("gpsimd", "sV", A_sV)
            W("gpsimd", "dS", A_dS)

            x23 = x2d.rearrange("(c p) t -> p c t", p=128)
            x2_sb3 = x2_sb.rearrange("p (c t) -> p c t", t=TT)

            GA, GB = 0.035677408136300795, 0.7978845608028654

            gu_pm = {}
            rel_v = {}
            rel_a = {}
            vv_g = {}
            inter_done = {}
            dn_pm = {}
            dn_ev = {}
            last_rd = {}
            glob = 0
            gfo = 0
            out_st = None

            def emit_dn(ti_, ns_src, fo, par_src, bc_guard=None):
                nonlocal gfo
                fp = gfo % 4
                if gfo >= 4:
                    W("gpsimd", "sP", dn_pm[gfo - 4])
                ldd = O("gpsimd", lambda e, ns=ns_src, fo=fo, b=dbuf[fp]:
                        e.dma_start(out=b.rearrange("p (r f) -> p r f", f=128),
                                    in_=ddec[ns, fo]), "dG", 16)
                W("tensor", "dG", ldd)
                if fo == 0:
                    W("tensor", "sV", inter_done[(ti_, ns_src)])
                if gfo >= 2:
                    W("tensor", "sV", dn_ev[gfo - 2])
                if bc_guard is not None:
                    W("tensor", "sA", bc_guard)
                Dp = ps6[4 + gfo % 2]
                itr = inter2[par_src]

                def dn_mms(t, b=dbuf[fp], Dp=Dp, itr=itr):
                    for r in range(NCPS):
                        rr = t.matmul(Dp[:], lhsT=b[:, r * 128:(r + 1) * 128],
                                      rhs=itr[:, r * TT:(r + 1) * TT],
                                      start=(r == 0), stop=(r == NCPS - 1))
                    return rr
                pm = O("tensor", dn_mms, "sP")
                dn_pm[gfo] = pm
                last_rd[par_src] = pm
                W("vector", "sP", pm)
                O("vector", lambda v, Dp=Dp, fo=fo: v.tensor_tensor(
                    out=x2_sb[:, fo * TT:(fo + 1) * TT], in0=Dp[:],
                    in1=x2_sb[:, fo * TT:(fo + 1) * TT], op=ALU.add))
                dve_pad(1)
                dn_ev[gfo] = O("vector", lambda v: v.tensor_copy(out=pad1[:], in_=pad1[:]), "sV")
                gfo += 1

            for ti in range(NTILES):
                t0 = ti * TT
                if out_st is not None:
                    W("sync", "dS", out_st)
                for cg in range(4):
                    O("sync", lambda e, cg=cg, t0=t0:
                      e.dma_start(out=x2_sb3[:, cg * 4:(cg + 1) * 4, :],
                                  in_=x23[:, cg * 4:(cg + 1) * 4, t0:t0 + TT]), "dS", 16)
                ld_x2 = em.val("dS")
                W("vector", "dS", ld_x2)
                O("vector", lambda v: v.tensor_copy(out=xln[:], in_=x2_sb[:]))
                dve_pad(1)
                v_c = O("vector", lambda v: v.tensor_copy(out=pad1[:], in_=pad1[:]), "sV")
                W("scalar", "sV", v_c)
                W("tensor", "sV", v_c)

                def mean_mms2(t):
                    for kc in range(NCH):
                        r = t.matmul(psR[0][:], lhsT=ones_col[:], rhs=xln[:, kc * TT:(kc + 1) * TT],
                                     start=(kc == 0), stop=(kc == NCH - 1))
                    return r
                O("tensor", mean_mms2)
                sq_used = {}
                for kc in range(NCH):
                    if kc >= 3:
                        W("scalar", "sP", sq_used[kc - 3])
                    ak = O("scalar", lambda s, kc=kc: s.activation(
                        out=sq_c[kc % 3][:], in_=xln[:, kc * TT:(kc + 1) * TT],
                        func=AF.Square), "sA")
                    W("tensor", "sA", ak)
                    sq_used[kc] = O("tensor", lambda t, kc=kc: t.matmul(
                        psR[1][:], lhsT=ones_col[:], rhs=sq_c[kc % 3][:],
                        start=(kc == 0), stop=(kc == NCH - 1)), "sP")
                p_ln = em.val("sP")
                W("scalar", "sP", p_ln)
                O("scalar", lambda s: s.activation(out=mean_r, in_=psR[0][:], func=AF.Copy,
                                                   scale=c_inv[:]))
                O("scalar", lambda s: s.activation(out=msq_r, in_=psR[1][:], func=AF.Copy,
                                                   scale=c_inv[:]))
                a_mr = O("scalar", lambda s: s.activation(out=pad1[:], in_=pad1[:],
                                                          func=AF.Copy), "sA")
                W("vector", "sA", a_mr)
                O("vector", lambda v: v.tensor_tensor(out=tv_r, in0=mean_r, in1=mean_r, op=ALU.mult))
                dve_pad(1)
                O("vector", lambda v: v.tensor_tensor(out=var_r, in0=msq_r, in1=tv_r, op=ALU.subtract))
                dve_pad(1)
                v_var = O("vector", lambda v: v.tensor_copy(out=pad1[:], in_=pad1[:]), "sV")
                W("scalar", "sV", v_var)
                O("scalar", lambda s: s.activation(out=lnv_r, in_=var_r, func=AF.Ln, bias=c_eps[:]))
                act_pad(1)
                O("scalar", lambda s: s.activation(out=rstd_r, in_=lnv_r, func=AF.Exp,
                                                   scale=c_nh[:]))
                a_std = O("scalar", lambda s: s.activation(out=pad1[:], in_=pad1[:],
                                                           func=AF.Copy), "sA")
                W("vector", "sA", a_std)
                O("vector", lambda v: v.tensor_copy(out=rstd_bfv, in_=rstd_r))
                O("vector", lambda v: v.tensor_copy(out=mean_bfv, in_=mean_r))
                dve_pad(1)
                v_rw = O("vector", lambda v: v.tensor_copy(out=pad1[:], in_=pad1[:]), "sV")
                W("tensor", "sV", v_rw)
                if gfo >= 1:
                    W("tensor", "sV", dn_ev[gfo - 1])

                def bc2(t):
                    t.matmul(ps6[4][:], lhsT=ones_row[:], rhs=rstd_bfv, start=True, stop=True)
                    return t.matmul(ps6[5][:], lhsT=ones_row[:], rhs=mean_bfv, start=True, stop=True)
                p_bc2 = O("tensor", bc2, "sP")
                W("scalar", "sP", p_bc2)
                O("scalar", lambda s: s.activation(out=R_t[:], in_=ps6[4][:], func=AF.Copy))
                O("scalar", lambda s: s.activation(out=M_t[:], in_=ps6[5][:], func=AF.Copy))
                a_bc = O("scalar", lambda s: s.activation(out=pad1[:], in_=pad1[:],
                                                          func=AF.Copy), "sA")
                W("vector", "sA", a_bc)

                def center_ops(v):
                    for oc in range(NCH):
                        v.tensor_tensor(out=xln[:, oc * TT:(oc + 1) * TT],
                                        in0=xln[:, oc * TT:(oc + 1) * TT],
                                        in1=M_t[:], op=ALU.subtract)
                    return v.tensor_copy(out=pad1[:], in_=pad1[:])
                O("vector", center_ops)
                v_xln = O("vector", lambda v: v.tensor_copy(out=pad1[:], in_=pad1[:]), "sV")
                W("tensor", "sV", v_xln)

                for ns in range(NSUP):
                    par = ns % 2
                    first_write_guard = last_rd.get(par)
                    for g in range(NGRP):
                        gp = (glob // JPG) % 2
                        grp_abs = ns * NGRP + g
                        if glob >= 2 * JPG:
                            W("sync", "sP", gu_pm[glob - JPG - 1])
                        O("sync", lambda e, gi=grp_abs, b=gbuf[gp]:
                          e.dma_start(out=b, in_=gdec[gi].rearrange("p kc n -> p (kc n)")),
                          "dS", 16)
                        O("sync", lambda e, gi=grp_abs, b=ubuf[gp]:
                          e.dma_start(out=b, in_=udec[gi].rearrange("p kc n -> p (kc n)")),
                          "dS", 16)
                        ldg = em.val("dS")
                        for j in range(JPG):
                            jj = g * JPG + j
                            p = glob % 2
                            W("tensor", "dS", ldg)
                            G = ps6[p]
                            U = ps6[2 + p]
                            if glob >= 2:
                                W("tensor", "sV", rel_v[glob - 2])
                                W("tensor", "sA", rel_a[glob - 2])

                            def gu_mms(t, b=gbuf[gp], bu=ubuf[gp], j=j, G=G, U=U):
                                for kc in range(NCH):
                                    t.matmul(G[:], lhsT=b[:, kc * GRP + j * 128:kc * GRP + (j + 1) * 128],
                                             rhs=xln[:, kc * TT:(kc + 1) * TT],
                                             start=(kc == 0), stop=(kc == NCH - 1))
                                for kc in range(NCH):
                                    r = t.matmul(U[:], lhsT=bu[:, kc * GRP + j * 128:kc * GRP + (j + 1) * 128],
                                                 rhs=xln[:, kc * TT:(kc + 1) * TT],
                                                 start=(kc == 0), stop=(kc == NCH - 1))
                                return r
                            pm = O("tensor", gu_mms, "sP")
                            gu_pm[glob] = pm

                            W("vector", "sP", pm)
                            O("vector", lambda v, G=G, p=p: v.tensor_tensor(
                                out=gcp[p][:], in0=G[:], in1=R_t[:], op=ALU.mult))
                            O("vector", lambda v, U=U, p=p: v.tensor_tensor(
                                out=upc[p][:], in0=U[:], in1=R_t[:], op=ALU.mult))
                            dve_pad(1)
                            va = O("vector", lambda v: v.tensor_copy(out=pad1[:], in_=pad1[:]), "sV")
                            rel_v[glob] = va

                            W("scalar", "sV", va)
                            O("scalar", lambda s, p=p: s.activation(
                                out=g2t[p][:], in_=gcp[p][:], func=AF.Square))
                            aa = O("scalar", lambda s: s.activation(out=pad1[:], in_=pad1[:],
                                                                    func=AF.Copy), "sA")
                            rel_a[glob] = aa

                            W("vector", "sA", aa)
                            O("vector", lambda v, p=p: v.tensor_scalar(
                                out=g2t[p][:], in0=g2t[p][:], scalar1=GA, scalar2=GB,
                                op0=ALU.mult, op1=ALU.add))
                            dve_pad(1)
                            O("vector", lambda v, p=p: v.tensor_tensor(
                                out=g2t[p][:], in0=g2t[p][:], in1=gcp[p][:], op=ALU.mult))
                            dve_pad(1)
                            vb = O("vector", lambda v: v.tensor_copy(out=pad1[:], in_=pad1[:]), "sV")

                            W("scalar", "sV", vb)
                            O("scalar", lambda s, p=p: s.activation(
                                out=tnh[p][:], in_=g2t[p][:], func=AF.Tanh))
                            at = O("scalar", lambda s: s.activation(out=pad1[:], in_=pad1[:],
                                                                    func=AF.Copy), "sA")

                            W("vector", "sA", at)
                            if jj == 0 and first_write_guard is not None:
                                W("vector", "sP", first_write_guard)
                            O("vector", lambda v, p=p: v.tensor_scalar(
                                out=tnh[p][:], in0=tnh[p][:], scalar1=0.5, scalar2=0.5,
                                op0=ALU.mult, op1=ALU.add))
                            dve_pad(1)
                            O("vector", lambda v, p=p: v.tensor_tensor(
                                out=gel[p][:], in0=tnh[p][:], in1=gcp[p][:], op=ALU.mult))
                            dve_pad(1)
                            O("vector", lambda v, p=p, jj=jj, par=par: v.tensor_tensor(
                                out=inter2[par][:, jj * TT:(jj + 1) * TT], in0=gel[p][:],
                                in1=upc[p][:], op=ALU.mult))
                            dve_pad(1)
                            vv = O("vector", lambda v: v.tensor_copy(out=pad1[:], in_=pad1[:]), "sV")
                            vv_g[glob] = vv
                            glob += 1

                            if ns >= 1:
                                emit_dn(ti, ns - 1, jj, 1 - par,
                                        bc_guard=(a_bc if (ns == 1 and jj < 2) else None))
                    inter_done[(ti, ns)] = vv_g[glob - 1]
                for fo in range(NCH):
                    emit_dn(ti, NSUP - 1, fo, (NSUP - 1) % 2)
                acc_done = em.val("sV")
                W("sync", "sV", acc_done)
                for oc in range(NCH):
                    O("sync", lambda e, oc=oc, t0=t0:
                      e.dma_start(out=out[oc * 128:(oc + 1) * 128, t0:t0 + TT],
                                  in_=x2_sb[:, oc * TT:(oc + 1) * TT]), "dS", 16)
                out_st = em.val("dS")

        if dbg_outs:
            W("gpsimd", "dA", em.val("dA"))
            W("gpsimd", "dS", em.val("dS"))
            W("gpsimd", "sV", em.val("sV"))
            if "o_gdec" in dbg_outs:
                O("gpsimd", lambda e: e.dma_start(
                    out=dbg_outs["o_gdec"].rearrange("a b c d -> (a b) (c d)"),
                    in_=gdec.rearrange("a b c d -> (a b) (c d)")), "dG", 16)
                O("gpsimd", lambda e: e.dma_start(
                    out=dbg_outs["o_ddec"].rearrange("a b c d e -> (a b) (c d e)"),
                    in_=ddec.rearrange("a b c d e -> (a b) (c d e)")), "dG", 16)
            if "o_x2" in dbg_outs:
                O("gpsimd", lambda e: e.dma_start(out=dbg_outs["o_x2"], in_=x2d), "dG", 16)

        W("sync", "dS", em.val("dS"))
        W("sync", "dA", em.val("dA"))
        W("sync", "dG", em.val("dG"))
        W("gpsimd", "dG", em.val("dG"))
        W("scalar", "dA", em.val("dA"))

        with nc.Block() as block:
            @block.sync
            def _(e):
                for fn in em.ops["sync"]:
                    fn(e)

            @block.gpsimd
            def _(e):
                for fn in em.ops["gpsimd"]:
                    fn(e)

            @block.vector
            def _(e):
                for fn in em.ops["vector"]:
                    fn(e)

            @block.scalar
            def _(e):
                for fn in em.ops["scalar"]:
                    fn(e)

            @block.tensor
            def _(e):
                for fn in em.ops["tensor"]:
                    fn(e)

    return nc


def _stage_inputs(L0_attn_x, L0_attn_W_q, L0_attn_W_kv, L0_attn_W_o,
                  L0_ffn_gate_W_packed, L0_ffn_gate_scales,
                  L0_ffn_up_W_packed, L0_ffn_up_scales,
                  L0_ffn_down_W_packed, L0_ffn_down_scales):
    pi = perm2048()
    PI = perm16384()

    def tile_wT(wT):
        return np.ascontiguousarray(wT.reshape(16, 128, 16, 128).transpose(2, 1, 0, 3))

    x = np.asarray(L0_attn_x, dtype=np.float32)
    wq = np.asarray(L0_attn_W_q, dtype=np.float32)
    wkv = np.asarray(L0_attn_W_kv, dtype=np.float32)
    wo = np.asarray(L0_attn_W_o, dtype=np.float32)

    wq_t = tile_wT(np.ascontiguousarray(wq[:, pi].T))
    wkv_t = tile_wT(np.ascontiguousarray(wkv[:, pi].T))
    wo_t = tile_wT(np.ascontiguousarray(wo[pi, :].T))

    def pack16(Wp):
        return np.ascontiguousarray(np.asarray(Wp).view(np.int16)[:, 0::2])

    g16 = np.ascontiguousarray(pack16(L0_ffn_gate_W_packed)[PI, :].T)
    u16 = np.ascontiguousarray(pack16(L0_ffn_up_W_packed)[PI, :].T)
    d16 = np.ascontiguousarray(pack16(L0_ffn_down_W_packed)[pi, :].T)

    gs = np.asarray(L0_ffn_gate_scales, dtype=np.float32)
    us = np.asarray(L0_ffn_up_scales, dtype=np.float32)
    ds = np.asarray(L0_ffn_down_scales, dtype=np.float32)
    gse = np.ascontiguousarray(np.repeat(gs[PI, :].T, 16, axis=0))
    use2 = np.ascontiguousarray(np.repeat(us[PI, :].T, 16, axis=0))
    dse = np.ascontiguousarray(np.repeat(ds[pi, :].T, 16, axis=0))

    ar = np.arange(128, dtype=np.float64)
    ph = 2.0 * np.pi * np.outer(ar, ar) / 128.0
    dftc = np.cos(ph).astype(ml_dtypes.bfloat16)
    dfts = (-np.sin(ph)).astype(ml_dtypes.bfloat16)
    dftci = (np.cos(ph) / 128.0).astype(ml_dtypes.bfloat16)
    dftsi = (-np.sin(ph) / 128.0).astype(ml_dtypes.bfloat16)

    in_maps = []
    for c in range(NCORES):
        xs = [np.ascontiguousarray(x[c * SPC + b].T[pi]) for b in range(SPC)]
        x_core = np.ascontiguousarray(np.concatenate(xs, axis=1))
        in_maps.append(dict(
            x_in=x_core, wq_t=wq_t, wkv_t=wkv_t, wo_t=wo_t,
            g16=g16, u16=u16, d16=d16, gse=gse, use=use2, dse=dse,
            dftc=dftc, dfts=dfts, dftci=dftci, dftsi=dftsi,
        ))
    return in_maps, pi


def kernel(**inputs):
    in_maps, pi = _stage_inputs(**inputs)
    nc = build_nc()
    res = run_bass_kernel_spmd(nc, in_maps, core_ids=list(range(NCORES)))
    outp = np.empty((NB, S, D), dtype=np.float32)
    for c in range(NCORES):
        r = np.asarray(res.results[c]["out"])
        for b in range(SPC):
            seq = r[:, b * S:(b + 1) * S]
            o = np.empty((S, D), dtype=np.float32)
            o[:, pi] = seq.T
            outp[c * SPC + b] = o
    return outp


# version 29
# speedup vs baseline: 1.0563x; 1.0010x over previous
import numpy as np
import ml_dtypes
from contextlib import ExitStack

import concourse.bass as bass
import concourse.mybir as mybir
from concourse.bass_utils import run_bass_kernel_spmd

F32 = mybir.dt.float32
BF16 = mybir.dt.bfloat16
I16 = mybir.dt.int16
I32 = mybir.dt.int32
ALU = mybir.AluOpType
AF = mybir.ActivationFunctionType

NB, S, D = 16, 2048, 2048
NCORES = 8
SPC = NB // NCORES
TCORE = SPC * S
TT = 512
NTILES = TCORE // TT
NCH = D // 128
H = 16
MLP = 16384
NSUPER = 2048
NSUP = MLP // NSUPER
NCPS = NSUPER // 128
GRP = 256
NGRP = NSUPER // GRP
JPG = GRP // 128
A_LEVEL = 4
M_LEVEL = 5
M_SQ = True
M_CP = True


def perm2048():
    p = np.empty(2048, dtype=np.int64)
    j = 0
    for c in range(8):
        for par in range(2):
            for q in range(128):
                p[j] = 2 * (128 * c + q) + par
                j += 1
    return p


def perm16384():
    p = np.empty(16384, dtype=np.int64)
    j = 0
    for c in range(64):
        for par in range(2):
            for q in range(128):
                p[j] = 2 * (128 * c + q) + par
                j += 1
    return p


class Em:
    def __init__(self):
        self.ops = {k: [] for k in ("sync", "gpsimd", "vector", "scalar", "tensor")}
        self.cnt = {}

    def val(self, name):
        return self.cnt.get(name, 0)

    def bump(self, name, k):
        self.cnt[name] = self.cnt.get(name, 0) + k
        return self.cnt[name]


class Pool:
    def __init__(self, tensor, n_i16):
        self.t = tensor
        self.n = n_i16
        self.off = 0

    def take(self, shape, dt):
        assert shape[0] == 128
        elems = int(np.prod(shape[1:]))
        nbytes = elems * mybir.dt.size(dt)
        assert nbytes % 2 == 0
        n16 = nbytes // 2
        off = (self.off + 31) & ~31
        assert off + n16 <= self.n, f"pool overflow: {off}+{n16}>{self.n}"
        self.off = off + n16
        ap = self.t[:, off:off + n16].bitcast(dt)
        if len(shape) > 2:
            ap = ap.rearrange("p (a b) -> p a b", b=shape[-1])
        return ap


def build_nc(phases=("dec", "A", "B"), dbg=(), detect_races=True):
    nc = bass.Bass("TRN2", target_bir_lowering=False,
                   detect_race_conditions=detect_races)

    x_in = nc.dram_tensor("x_in", [D, TCORE], F32, kind="ExternalInput").ap()
    wq_t = nc.dram_tensor("wq_t", [16, 128, 16, 128], F32, kind="ExternalInput").ap()
    wkv_t = nc.dram_tensor("wkv_t", [16, 128, 16, 128], F32, kind="ExternalInput").ap()
    wo_t = nc.dram_tensor("wo_t", [16, 128, 16, 128], F32, kind="ExternalInput").ap()
    g16 = nc.dram_tensor("g16", [1024, MLP], I16, kind="ExternalInput").ap()
    u16 = nc.dram_tensor("u16", [1024, MLP], I16, kind="ExternalInput").ap()
    d16 = nc.dram_tensor("d16", [8192, 2048], I16, kind="ExternalInput").ap()
    gse = nc.dram_tensor("gse", [1024, MLP], F32, kind="ExternalInput").ap()
    use_ = nc.dram_tensor("use", [1024, MLP], F32, kind="ExternalInput").ap()
    dse = nc.dram_tensor("dse", [8192, 2048], F32, kind="ExternalInput").ap()
    dftc = nc.dram_tensor("dftc", [128, 128], BF16, kind="ExternalInput").ap()
    dfts = nc.dram_tensor("dfts", [128, 128], BF16, kind="ExternalInput").ap()
    dftci = nc.dram_tensor("dftci", [128, 128], BF16, kind="ExternalInput").ap()
    dftsi = nc.dram_tensor("dftsi", [128, 128], BF16, kind="ExternalInput").ap()

    out = nc.dram_tensor("out", [D, TCORE], F32, kind="ExternalOutput").ap()

    NGRPS = MLP // GRP
    gdec = nc.dram_tensor("gdec", [NGRPS, 128, NCH, GRP], BF16, kind="Internal").ap()
    udec = nc.dram_tensor("udec", [NGRPS, 128, NCH, GRP], BF16, kind="Internal").ap()
    ddec = nc.dram_tensor("ddec", [NSUP, NCH, 128, NCPS, 128], BF16, kind="Internal").ap()
    x2d = nc.dram_tensor("x2d", [D, TCORE], F32, kind="Internal").ap()

    dbg_outs = {}
    if "dec" in dbg:
        dbg_outs["o_gdec"] = nc.dram_tensor("o_gdec", [MLP // GRP, 128, NCH, GRP], BF16,
                                            kind="ExternalOutput").ap()
        dbg_outs["o_ddec"] = nc.dram_tensor("o_ddec", [NSUP, NCH, 128, NCPS, 128], BF16,
                                            kind="ExternalOutput").ap()
    if "x2" in dbg:
        dbg_outs["o_x2"] = nc.dram_tensor("o_x2", [D, TCORE], F32, kind="ExternalOutput").ap()

    em = Em()

    with ExitStack() as ctx:
        E = ctx.enter_context
        dS = E(nc.semaphore("dS"))
        dG = E(nc.semaphore("dG"))
        dA = E(nc.semaphore("dA"))
        sP = E(nc.semaphore("sP"))
        sV = E(nc.semaphore("sV"))
        sA = E(nc.semaphore("sA"))
        sG = E(nc.semaphore("sG"))
        SEM = {"dS": dS, "dG": dG, "dA": dA, "sP": sP, "sV": sV, "sA": sA, "sG": sG}

        def W(eng, name, v):
            em.ops[eng].append(lambda e, s=SEM[name], vv=v: e.wait_ge(s, vv))

        def O(eng, fn, incname=None, k=1):
            if incname is None:
                em.ops[eng].append(fn)
                return None
            v = em.bump(incname, k)
            em.ops[eng].append(lambda e, f=fn, s=SEM[incname], kk=k: f(e).then_inc(s, kk))
            return v

        ones_col = E(nc.sbuf_tensor("ones_col", [128, 1], BF16))
        ones_row = E(nc.sbuf_tensor("ones_row", [1, 128], BF16))
        ones_all = E(nc.sbuf_tensor("ones_all", [128, 128], BF16))
        zeros_t = E(nc.sbuf_tensor("zeros_t", [128, TT], F32))
        c_neg = E(nc.sbuf_tensor("c_neg", [1, 1], F32))
        c_inv = E(nc.sbuf_tensor("c_inv", [1, 1], F32))
        c_eps = E(nc.sbuf_tensor("c_eps", [1, 1], F32))
        c_nh = E(nc.sbuf_tensor("c_nh", [1, 1], F32))
        c_nh128 = E(nc.sbuf_tensor("c_nh128", [128, 1], F32))
        c_lnb = E(nc.sbuf_tensor("c_lnb", [128, 1], F32))
        t_dftc = E(nc.sbuf_tensor("t_dftc", [128, 128], BF16))
        t_dfts = E(nc.sbuf_tensor("t_dfts", [128, 128], BF16))
        t_dftci = E(nc.sbuf_tensor("t_dftci", [128, 128], BF16))
        t_dftsi = E(nc.sbuf_tensor("t_dftsi", [128, 128], BF16))
        pad1 = E(nc.sbuf_tensor("pad1", [1, 8], F32))
        pad2 = E(nc.sbuf_tensor("pad2", [1, 8], F32))
        rowS = E(nc.sbuf_tensor("rowS", [1, 8 * TT], F32))
        negm_bf = E(nc.sbuf_tensor("negm_bf", [1, TT], BF16))
        wbar_q = E(nc.sbuf_tensor("wbar_q", [1, 2048], BF16))
        wbar_kv = E(nc.sbuf_tensor("wbar_kv", [1, 2048], BF16))

        POOL_N = 89 * 1024
        pool_t = E(nc.sbuf_tensor("pool", [128, POOL_N], I16))

        ps6 = [E(nc.psum_tensor(f"ps{i}", [128, TT], F32)) for i in range(8)]
        psR = [ps6[6][0:1, :], ps6[7][0:1, :]]

        O("vector", lambda v: v.memset(ones_col[:], 1.0))
        O("vector", lambda v: v.memset(ones_row[:], 1.0))
        O("vector", lambda v: v.memset(ones_all[:], 1.0))
        O("vector", lambda v: v.memset(zeros_t[:], 0.0))
        O("vector", lambda v: v.memset(c_neg[:], -1.0 / 2048.0))
        O("vector", lambda v: v.memset(c_inv[:], 1.0 / 2048.0))
        O("vector", lambda v: v.memset(c_eps[:], 1e-5))
        O("vector", lambda v: v.memset(c_nh[:], -0.5))
        O("vector", lambda v: v.memset(c_nh128[:], -0.5))
        O("vector", lambda v: v.memset(c_lnb[:], -2.4260151319598084))
        O("vector", lambda v: v.memset(pad1[:], 0.0))
        O("gpsimd", lambda g: g.memset(pad2[:], 0.0))
        O("vector", lambda v: v.tensor_copy(out=pad1[:], in_=pad1[:]))
        v_init = O("vector", lambda v: v.tensor_copy(out=pad1[:], in_=pad1[:]), "sV")
        O("sync", lambda e: e.dma_start(out=t_dftc[:], in_=dftc), "dS", 16)
        O("sync", lambda e: e.dma_start(out=t_dfts[:], in_=dfts), "dS", 16)
        O("sync", lambda e: e.dma_start(out=t_dftci[:], in_=dftci), "dS", 16)
        d_init = O("sync", lambda e: e.dma_start(out=t_dftsi[:], in_=dftsi), "dS", 16)

        def dve_pad(n=1):
            for _ in range(n):
                O("vector", lambda v: v.tensor_copy(out=pad1[:], in_=pad1[:]))

        def act_pad(n=1):
            for _ in range(n):
                O("scalar", lambda s: s.activation(out=pad1[:], in_=pad1[:], func=AF.Copy))

        def gp_pad(n=1):
            for _ in range(n):
                O("gpsimd", lambda g: g.tensor_copy(out=pad2[:], in_=pad2[:]))

        if "dec" in phases:
            dp = Pool(pool_t, POOL_N)
            FD = 4096
            pk_b = [dp.take([128, FD], I16) for _ in range(2)]
            sc_b = [dp.take([128, FD], F32) for _ in range(2)]
            uL = dp.take([128, FD], I16); uH = dp.take([128, FD], I16)
            eL = dp.take([128, FD], I16); eH = dp.take([128, FD], I16)
            mL = dp.take([128, FD], I16); mH = dp.take([128, FD], I16)
            bL = dp.take([128, FD], I16); bH = dp.take([128, FD], I16)
            plL = [dp.take([128, FD], BF16) for _ in range(2)]
            plH = [dp.take([128, FD], BF16) for _ in range(2)]

            tiles = []
            for mat, scl, dst in ((g16, gse, gdec), (u16, use_, udec)):
                for c in range(8):
                    for t4 in range(4):
                        n0 = t4 * 4096
                        tiles.append((
                            mat[c * 128:(c + 1) * 128, n0:n0 + 4096],
                            scl[c * 128:(c + 1) * 128, n0:n0 + 4096],
                            (dst, c, n0),
                            False))
            for c2 in range(32):
                r0 = c2 * 256
                dsts = []
                for half in range(2):
                    C = 2 * c2 + half
                    for par in range(2):
                        nch_ = 2 * C + par
                        nsx, r = nch_ // NCPS, nch_ % NCPS
                        dsts.append(ddec[nsx, :, :, r, :].transpose([1, 0, 2]))
                tiles.append((
                    d16[r0:r0 + 256, :].rearrange("(h p) f -> p h f", p=128),
                    dse[r0:r0 + 256, :].rearrange("(h p) f -> p h f", p=128),
                    dsts, True))

            dec_done = []
            st_cnt = []
            for i, (src, ssrc, dsts, is_down) in enumerate(tiles):
                pg = i % 2
                if i >= 2:
                    W("sync", "sV", dec_done[i - 2])
                pk_v, sc_v = pk_b[pg], sc_b[pg]
                if is_down:
                    pk_v = pk_v.rearrange("p (h f) -> p h f", h=2)
                    sc_v = sc_v.rearrange("p (h f) -> p h f", h=2)
                O("sync", lambda e, o=pk_v, s=src: e.dma_start(out=o, in_=s), "dS", 16)
                nl1 = O("sync", lambda e, o=sc_v, s=ssrc: e.dma_start(out=o, in_=s), "dS", 16)
                W("vector", "dS", nl1)
                if i >= 2:
                    W("vector", "dA", st_cnt[i - 2])
                pk, sc = pk_b[pg], sc_b[pg]
                oL, oH = plL[pg], plH[pg]

                def dec_ops(v, pk=pk, sc=sc, oL=oL, oH=oH):
                    v.tensor_scalar(out=uL, in0=pk, scalar1=7, scalar2=None, op0=ALU.bitwise_and)
                    v.tensor_scalar(out=uH, in0=pk, scalar1=4, scalar2=7,
                                    op0=ALU.logical_shift_right, op1=ALU.bitwise_and)
                    v.tensor_scalar(out=mL, in0=uL, scalar1=1.0, scalar2=-1.0,
                                    op0=ALU.is_ge, op1=ALU.mult)
                    v.tensor_scalar(out=mH, in0=uH, scalar1=1.0, scalar2=-1.0,
                                    op0=ALU.is_ge, op1=ALU.mult)
                    v.tensor_scalar(out=eL, in0=uL, scalar1=1.0, scalar2=None, op0=ALU.is_equal)
                    v.tensor_scalar(out=eH, in0=uH, scalar1=1.0, scalar2=None, op0=ALU.is_equal)
                    v.tensor_tensor(out=uL.bitcast(I32), in0=uL.bitcast(I32),
                                    in1=eL.bitcast(I32), op=ALU.bitwise_xor)
                    v.tensor_tensor(out=uH.bitcast(I32), in0=uH.bitcast(I32),
                                    in1=eH.bitcast(I32), op=ALU.bitwise_xor)
                    v.tensor_scalar(out=bL, in0=uL, scalar1=64.0, scalar2=16128.0,
                                    op0=ALU.mult, op1=ALU.add)
                    v.tensor_scalar(out=bH, in0=uH, scalar1=64.0, scalar2=16128.0,
                                    op0=ALU.mult, op1=ALU.add)
                    v.tensor_tensor(out=bL.bitcast(I32), in0=bL.bitcast(I32),
                                    in1=mL.bitcast(I32), op=ALU.bitwise_and)
                    v.tensor_tensor(out=bH.bitcast(I32), in0=bH.bitcast(I32),
                                    in1=mH.bitcast(I32), op=ALU.bitwise_and)
                    v.tensor_scalar(out=eL, in0=pk, scalar1=8, scalar2=12,
                                    op0=ALU.bitwise_and, op1=ALU.logical_shift_left)
                    v.tensor_scalar(out=eH, in0=pk, scalar1=128, scalar2=8,
                                    op0=ALU.bitwise_and, op1=ALU.logical_shift_left)
                    v.tensor_tensor(out=bL.bitcast(I32), in0=bL.bitcast(I32),
                                    in1=eL.bitcast(I32), op=ALU.bitwise_or)
                    v.tensor_tensor(out=bH.bitcast(I32), in0=bH.bitcast(I32),
                                    in1=eH.bitcast(I32), op=ALU.bitwise_or)
                    v.tensor_tensor(out=oL, in0=bL.bitcast(BF16), in1=sc, op=ALU.mult)
                    return v.tensor_tensor(out=oH, in0=bH.bitcast(BF16), in1=sc, op=ALU.mult)
                O("vector", dec_ops)
                dve_pad(1)
                dv = O("vector", lambda v: v.tensor_copy(out=pad1[:], in_=pad1[:]), "sV")
                dec_done.append(dv)
                W("scalar", "sV", dv)
                if not is_down:
                    dst3, cc, n0_ = dsts
                    for par, opl in ((0, oL), (1, oH)):
                        kc = 2 * cc + par
                        for s in range(4096 // GRP):
                            sc2 = O("scalar",
                                    lambda e, o=dst3[n0_ // GRP + s][:, kc, :],
                                    s_=opl[:, s * GRP:(s + 1) * GRP]:
                                    e.dma_start(out=o, in_=s_), "dA", 16)
                else:
                    for hh in range(2):
                        for par, opl in ((0, oL), (1, oH)):
                            srcv = opl.rearrange("p (h f) -> p h f", h=2)[:, hh, :]
                            srcv = srcv.rearrange("p (fo f) -> p fo f", fo=16)
                            sc2 = O("scalar",
                                    lambda e, o=dsts[hh * 2 + par], s=srcv: e.dma_start(out=o, in_=s),
                                    "dA", 16)
                st_cnt.append(sc2)
        dec_dS = em.val("dS")
        dec_dA = em.val("dA")
        dec_sV = em.val("sV")

        if "A" in phases:
            ap_ = Pool(pool_t, POOL_N)
            x_f32 = ap_.take([128, NCH * TT], F32)
            x_bf = ap_.take([128, NCH * TT], BF16)
            q_bf = ap_.take([128, NCH * TT], BF16)
            kv_bf = ap_.take([128, NCH * TT], BF16)
            vh_bf = ap_.take([128, NCH * TT], BF16)
            wbuf = [ap_.take([128, 2048], BF16) for _ in range(2)]
            qfre = [ap_.take([128, TT], BF16) for _ in range(3)]
            qfim = [ap_.take([128, TT], BF16) for _ in range(3)]
            kvre = [ap_.take([128, TT], BF16) for _ in range(3)]
            kvim = [ap_.take([128, TT], BF16) for _ in range(3)]
            sqst = [ap_.take([128, TT], BF16) for _ in range(2)]
            kvnre = ap_.take([128, TT], BF16); kvnim = ap_.take([128, TT], BF16)
            kvcre = ap_.take([128, TT], F32); kvcim = ap_.take([128, TT], F32)
            t1 = ap_.take([128, TT], F32); t2 = ap_.take([128, TT], F32)
            t3 = ap_.take([128, TT], F32); t4 = ap_.take([128, TT], F32)
            qvre = ap_.take([128, TT], BF16); qvim = ap_.take([128, TT], BF16)
            lq_t = ap_.take([128, TT], F32); lkv_t = ap_.take([128, TT], F32)
            rq_t = [ap_.take([128, TT], F32) for _ in range(3)]
            rkv_t = [ap_.take([128, TT], F32) for _ in range(3)]
            carry_re = ap_.take([128, H], F32); carry_im = ap_.take([128, H], F32)
            x2st = [ap_.take([128, TT], F32) for _ in range(2)]
            n2st = ap_.take([128, 16 * TT], BF16)

            def n2row(r):
                return n2st[64 * (r % 2):64 * (r % 2) + 1,
                            (r // 2) * TT:(r // 2 + 1) * TT]

            W("vector", "dS", d_init)
            W("tensor", "sV", v_init)
            W("tensor", "dS", d_init)
            W("sync", "sV", dec_sV)
            W("sync", "dA", dec_dA)
            W("gpsimd", "sV", dec_sV)
            W("gpsimd", "dA", dec_dA)

            xin3 = x_in.rearrange("(c p) t -> p c t", p=128)
            x_f32_3 = x_f32.rearrange("p (c t) -> p c t", t=TT)
            pm_slot = {}
            ev_slot = {}
            o_ev = {}
            o_st = {}
            for ti in range(NTILES):
                t0 = ti * TT
                for cg in range(4):
                    O("sync", lambda e, cg=cg, t0=t0:
                      e.dma_start(out=x_f32_3[:, cg * 4:(cg + 1) * 4, :],
                                  in_=xin3[:, cg * 4:(cg + 1) * 4, t0:t0 + TT]), "dS", 16)
                ld_x = em.val("dS")
                W("vector", "dS", ld_x)
                O("vector", lambda v: v.tensor_copy(out=x_bf[:], in_=x_f32[:]))
                dve_pad(1)
                c_cast = O("vector", lambda v: v.tensor_copy(out=pad1[:], in_=pad1[:]), "sV")
                W("tensor", "sV", c_cast)

                def mean_mms(t):
                    for kc in range(NCH):
                        r = t.matmul(psR[0][:], lhsT=ones_col[:],
                                     rhs=x_bf[:, kc * TT:(kc + 1) * TT],
                                     start=(kc == 0), stop=(kc == NCH - 1))
                    return r
                p_mean = O("tensor", mean_mms, "sP")
                W("scalar", "sP", p_mean)
                O("scalar", lambda s: s.activation(out=negm_bf[:], in_=psR[0][:],
                                                   func=AF.Copy, scale=c_neg[:]))
                act_pad(1)
                a_negm = O("scalar", lambda s: s.activation(out=pad1[:], in_=pad1[:],
                                                            func=AF.Copy), "sA")

                a_sqp = {}; p_n2 = {}; v_n2 = {}

                def emit_n2(sl):
                    W("tensor", "sA", a_sqp[sl])
                    pb = psR[0] if ti == 0 else psR[sl % 2]
                    gidx = sl - 1 if ti == 0 else sl - 2
                    if gidx >= 0 and gidx in v_n2:
                        W("tensor", "sV", v_n2[gidx])
                    pmn = O("tensor", lambda t, pb=pb, sl=sl: t.matmul(
                        pb[:], lhsT=ones_col[:], rhs=sqst[sl % 2][:],
                        start=True, stop=True), "sP")
                    p_n2[sl] = pmn
                    W("vector", "sP", pmn)
                    O("vector", lambda v, pb=pb, sl=sl: v.tensor_copy(
                        out=n2row(sl), in_=pb[:]))
                    dve_pad(1)
                    v_n2[sl] = O("vector", lambda v: v.tensor_copy(out=pad1[:], in_=pad1[:]),
                                 "sV")

                for pri, (wt, dst, wbar) in (enumerate(((wq_t, q_bf, wbar_q),
                                                       (wkv_t, kv_bf, wbar_kv))) if A_LEVEL >= 2 else []):
                    for oc in range(NCH):
                        slot = pri * NCH + oc
                        pg = slot % 2
                        if slot >= 2:
                            emit_n2(slot - 2)
                        prev = (ti, slot - 2) if slot >= 2 else (ti - 1, 46 + slot)
                        if prev in pm_slot:
                            W("gpsimd", "sP", pm_slot[prev])
                        ldw = O("gpsimd", lambda e, wt=wt, oc=oc, b=wbuf[pg]:
                                e.dma_start(out=b, in_=wt[oc]), "dG", 16)
                        W("tensor", "dG", ldw)
                        if prev in ev_slot:
                            W("tensor", "sA", ev_slot[prev])
                        ps = ps6[pg]

                        def proj_mms(t, b=wbuf[pg], ps=ps):
                            for kc in range(NCH):
                                t.matmul(ps[:], lhsT=b[:, kc * 128:(kc + 1) * 128],
                                         rhs=x_bf[:, kc * TT:(kc + 1) * TT],
                                         start=(kc == 0), stop=False)
                            return t
                        O("tensor", proj_mms)
                        if ti == 0:
                            def wbar_mms(t, b=wbuf[pg]):
                                for kc in range(NCH):
                                    r = t.matmul(psR[1][:, 0:128],
                                                 lhsT=ones_col[:],
                                                 rhs=b[:, kc * 128:(kc + 1) * 128],
                                                 start=(kc == 0), stop=(kc == NCH - 1))
                                return r
                            pw = O("tensor", wbar_mms, "sP")
                            W("scalar", "sP", pw)
                            O("scalar", lambda s, w=wbar, oc=oc:
                              s.activation(out=w[:, oc * 128:(oc + 1) * 128],
                                           in_=psR[1][:, 0:128], func=AF.Copy))
                            aw = O("scalar", lambda s: s.activation(out=pad1[:], in_=pad1[:],
                                                                    func=AF.Copy), "sA")
                            W("tensor", "sA", aw)
                        if slot == 0:
                            W("tensor", "sA", a_negm)
                        pm = O("tensor", lambda t, w=wbar, oc=oc, ps=ps:
                               t.matmul(ps[:], lhsT=w[:, oc * 128:(oc + 1) * 128],
                                        rhs=negm_bf[:], start=False, stop=True), "sP")
                        pm_slot[(ti, slot)] = pm
                        W("scalar", "sP", pm)
                        O("scalar", lambda s, ps=ps, dst=dst, oc=oc:
                          s.activation(out=dst[:, oc * TT:(oc + 1) * TT], in_=ps[:],
                                       func=AF.Copy))
                        ae = O("scalar", lambda s: s.activation(out=pad1[:], in_=pad1[:],
                                                                func=AF.Copy), "sA")
                        ev_slot[(ti, slot)] = ae
                        if slot >= 2 and (slot - 2) in p_n2:
                            W("scalar", "sP", p_n2[slot - 2])
                        O("scalar", lambda s, dst=dst, oc=oc, sl=slot: s.activation(
                            out=sqst[sl % 2][:], in_=dst[:, oc * TT:(oc + 1) * TT],
                            func=AF.Square))
                        a_sqp[slot] = O("scalar", lambda s: s.activation(
                            out=pad1[:], in_=pad1[:], func=AF.Copy), "sA")
                if A_LEVEL >= 2:
                    emit_n2(30)
                    emit_n2(31)
                proj_all = em.val("sA")

                first_of_seq = (ti % (S // TT) == 0)
                f_pfft = {}; f_vcp = {}; f_acp = {}; f_pbc = {}; f_arsq = {}
                f_vscan = {}; f_vmix = {}; f_gmix = {}; f_gkvn = {}
                f_pifft = {}; f_vvh = {}
                for i in (range(H + 5) if A_LEVEL >= 3 else []):
                    if i < H:
                        hs = slice(i * TT, (i + 1) * TT)
                        if i == 0:
                            W("tensor", "sA", proj_all)
                        if i >= 1:
                            W("tensor", "sV", f_vcp[i - 1])
                            W("tensor", "sA", f_acp[i - 1])

                        def fft_mms(t, hs=hs):
                            t.matmul(ps6[0][:], lhsT=t_dftc[:], rhs=q_bf[:, hs], start=True, stop=True)
                            t.matmul(ps6[1][:], lhsT=t_dfts[:], rhs=q_bf[:, hs], start=True, stop=True)
                            t.matmul(ps6[2][:], lhsT=t_dftc[:], rhs=kv_bf[:, hs], start=True, stop=True)
                            return t.matmul(ps6[3][:], lhsT=t_dfts[:], rhs=kv_bf[:, hs],
                                            start=True, stop=True)
                        f_pfft[i] = O("tensor", fft_mms, "sP")
                    k = i - 2
                    if 0 <= k < H:
                        W("tensor", "sV", v_n2[16 + k])
                        if k >= 1:
                            W("tensor", "sA", f_arsq[k - 1])

                        def bcn_mms(t, k=k):
                            bkv = 64 * ((16 + k) % 2)
                            bq = 64 * (k % 2)
                            t.matmul(ps6[4][:], lhsT=ones_all[bkv:bkv + 1, :],
                                     rhs=n2row(16 + k), start=True, stop=True)
                            return t.matmul(ps6[5][:], lhsT=ones_all[bq:bq + 1, :],
                                            rhs=n2row(k), start=True, stop=True)
                        f_pbc[k] = O("tensor", bcn_mms, "sP")
                    k = i - 4
                    if 0 <= k < H:
                        W("tensor", "sV", f_vmix[k])
                        W("tensor", "sG", f_gmix[k])
                        if k >= 2:
                            W("tensor", "sV", f_vvh[k - 2])
                        elif k < 2:
                            W("tensor", "sV", v_n2[30 + k])

                        def ifft_mms(t, k=k):
                            pb = ps6[6 + k % 2]
                            t.matmul(pb[:], lhsT=t_dftci[:], rhs=qvre[:], start=True, stop=False)
                            return t.matmul(pb[:], lhsT=t_dftsi[:], rhs=qvim[:],
                                            start=False, stop=True)
                        f_pifft[k] = O("tensor", ifft_mms, "sP")

                    k = i - 5
                    if 0 <= k < H:
                        W("vector", "sP", f_pifft[k])
                        O("vector", lambda v, k=k: v.tensor_tensor(
                            out=vh_bf[:, k * TT:(k + 1) * TT], in0=ps6[6 + k % 2][:],
                            in1=rq_t[k % 3][:], op=ALU.mult))
                        dve_pad(1)
                        f_vvh[k] = O("vector", lambda v: v.tensor_copy(out=pad1[:], in_=pad1[:]),
                                     "sV")
                    k = i - 2
                    if 0 <= k < H:
                        W("scalar", "sP", f_pbc[k])
                        if k >= 3:
                            W("scalar", "sV", f_vvh[k - 3])

                        def rsq_ops(s, k=k):
                            s.activation(out=lkv_t[:], in_=ps6[4][:], func=AF.Ln)
                            s.activation(out=lq_t[:], in_=ps6[5][:], func=AF.Ln)
                            s.activation(out=rkv_t[k % 3][:], in_=lkv_t[:], func=AF.Exp,
                                         scale=c_nh128[:], bias=c_lnb[:])
                            return s.activation(out=rq_t[k % 3][:], in_=lq_t[:], func=AF.Exp,
                                                scale=c_nh128[:], bias=c_lnb[:])
                        O("scalar", rsq_ops)
                        act_pad(1)
                        f_arsq[k] = O("scalar", lambda s: s.activation(out=pad1[:], in_=pad1[:],
                                                                       func=AF.Copy), "sA")

                    k = i - 3
                    if 0 <= k < H:
                        W("vector", "sG", f_gkvn[k])
                        if k >= 1:
                            W("vector", "sG", f_gmix[k - 1])

                        def scan_ops(v, k=k, first=first_of_seq):
                            ire = 0.0 if first else carry_re[:, k:k + 1]
                            iim = 0.0 if first else carry_im[:, k:k + 1]
                            v.tensor_tensor_scan(out=kvcre[:], data0=kvnre[:], data1=zeros_t[:],
                                                 initial=ire, op0=ALU.add, op1=ALU.add)
                            return v.tensor_tensor_scan(out=kvcim[:], data0=kvnim[:], data1=zeros_t[:],
                                                        initial=iim, op0=ALU.add, op1=ALU.add)
                        O("vector", scan_ops)
                        dve_pad(1)
                        f_vscan[k] = O("vector", lambda v: v.tensor_copy(out=pad1[:], in_=pad1[:]),
                                       "sV")
                        if k >= 1:
                            W("vector", "sP", f_pifft[k - 1])

                        def cm_ops(v, k=k):
                            v.tensor_copy(out=carry_re[:, k:k + 1], in_=kvcre[:, TT - 1:TT])
                            v.tensor_copy(out=carry_im[:, k:k + 1], in_=kvcim[:, TT - 1:TT])
                            v.tensor_tensor(out=t1[:], in0=qfre[k % 3][:], in1=kvcre[:], op=ALU.mult)
                            return v.tensor_tensor(out=t2[:], in0=qfim[k % 3][:], in1=kvcim[:],
                                                   op=ALU.mult)
                        O("vector", cm_ops)
                        dve_pad(1)
                        O("vector", lambda v: v.tensor_tensor(out=qvre[:], in0=t1[:], in1=t2[:],
                                                              op=ALU.subtract))
                        dve_pad(1)
                        f_vmix[k] = O("vector", lambda v: v.tensor_copy(out=pad1[:], in_=pad1[:]),
                                      "sV")
                    k = i - 2
                    if 0 <= k < H:
                        W("gpsimd", "sA", f_arsq[k])
                        if k >= 1:
                            W("gpsimd", "sV", f_vscan[k - 1])

                        def gkvn_ops(g, k=k):
                            g.tensor_tensor(out=kvnre[:], in0=kvre[k % 3][:], in1=rkv_t[k % 3][:],
                                            op=ALU.mult)
                            g.tensor_copy(out=pad2[:], in_=pad2[:])
                            return g.tensor_tensor(out=kvnim[:], in0=kvim[k % 3][:],
                                                   in1=rkv_t[k % 3][:], op=ALU.mult)
                        O("gpsimd", gkvn_ops)
                        gp_pad(1)
                        f_gkvn[k] = O("gpsimd", lambda g: g.tensor_copy(out=pad2[:], in_=pad2[:]),
                                      "sG")
                    k = i - 3
                    if 0 <= k < H:
                        W("gpsimd", "sV", f_vscan[k])
                        if k >= 1:
                            W("gpsimd", "sP", f_pifft[k - 1])

                        def gmix_ops(g, k=k):
                            g.tensor_tensor(out=t3[:], in0=qfre[k % 3][:], in1=kvcim[:], op=ALU.mult)
                            g.tensor_tensor(out=t4[:], in0=qfim[k % 3][:], in1=kvcre[:], op=ALU.mult)
                            g.tensor_copy(out=pad2[:], in_=pad2[:])
                            return g.tensor_tensor(out=qvim[:], in0=t3[:], in1=t4[:], op=ALU.add)
                        O("gpsimd", gmix_ops)
                        gp_pad(1)
                        f_gmix[k] = O("gpsimd", lambda g: g.tensor_copy(out=pad2[:], in_=pad2[:]),
                                      "sG")
                    if i < H:
                        W("vector", "sP", f_pfft[i])
                        if i >= 3:
                            W("vector", "sG", f_gmix[i - 3])

                        def cp_ops(v, i=i):
                            v.tensor_copy(out=qfre[i % 3][:], in_=ps6[0][:])
                            return v.tensor_copy(out=qfim[i % 3][:], in_=ps6[1][:])
                        O("vector", cp_ops)
                        dve_pad(1)
                        f_vcp[i] = O("vector", lambda v: v.tensor_copy(out=pad1[:], in_=pad1[:]),
                                     "sV")
                    if i < H:
                        W("scalar", "sP", f_pfft[i])
                        if i >= 3:
                            W("scalar", "sV", f_vscan[i - 3])

                        def cpkv_ops(s, i=i):
                            s.activation(out=kvre[i % 3][:], in_=ps6[2][:], func=AF.Copy)
                            return s.activation(out=kvim[i % 3][:], in_=ps6[3][:], func=AF.Copy)
                        O("scalar", cpkv_ops)
                        act_pad(1)
                        f_acp[i] = O("scalar", lambda s: s.activation(out=pad1[:], in_=pad1[:],
                                                                      func=AF.Copy), "sA")
                mix_all = em.val("sV")

                for oc in (range(NCH) if A_LEVEL >= 4 else []):
                    slot = 32 + oc
                    pg = slot % 2
                    prev = (ti, slot - 2)
                    if prev in pm_slot:
                        W("gpsimd", "sP", pm_slot[prev])
                    ldw = O("gpsimd", lambda e, oc=oc, b=wbuf[pg]:
                            e.dma_start(out=b, in_=wo_t[oc]), "dG", 16)
                    W("tensor", "dG", ldw)
                    if oc == 0:
                        W("tensor", "sV", mix_all)
                    if (ti, oc - 2) in o_ev:
                        W("tensor", "sV", o_ev[(ti, oc - 2)])
                    ps = ps6[pg]

                    def omms(t, b=wbuf[pg], ps=ps):
                        for kc in range(NCH):
                            r = t.matmul(ps[:], lhsT=b[:, kc * 128:(kc + 1) * 128],
                                         rhs=vh_bf[:, kc * TT:(kc + 1) * TT],
                                         start=(kc == 0), stop=(kc == NCH - 1))
                        return r
                    pm = O("tensor", omms, "sP")
                    pm_slot[(ti, slot)] = pm
                    W("vector", "sP", pm)
                    if (ti, oc - 2) in o_st:
                        W("vector", "dS", o_st[(ti, oc - 2)])
                    O("vector", lambda v, ps=ps, oc=oc, pg=pg:
                      v.tensor_tensor(out=x2st[pg][:], in0=ps[:],
                                      in1=x_f32[:, oc * TT:(oc + 1) * TT], op=ALU.add))
                    dve_pad(1)
                    ev = O("vector", lambda v: v.tensor_copy(out=pad1[:], in_=pad1[:]), "sV")
                    o_ev[(ti, oc)] = ev
                    W("sync", "sV", ev)
                    st = O("sync", lambda e, oc=oc, pg=pg, t0=t0:
                           e.dma_start(out=x2d[oc * 128:(oc + 1) * 128, t0:t0 + TT],
                                       in_=x2st[pg][:]), "dS", 16)
                    o_st[(ti, oc)] = st
                W("sync", "sV", em.val("sV"))
        A_dS = em.val("dS")
        A_sV = em.val("sV")

        if "B" in phases:
            bp = Pool(pool_t, POOL_N)
            x2_sb = bp.take([128, NCH * TT], F32)
            xln = bp.take([128, NCH * TT], BF16)
            inter2 = [bp.take([128, NCPS * TT], BF16) for _ in range(2)]
            gbuf = [bp.take([128, 16 * GRP], BF16) for _ in range(2)]
            ubuf = [bp.take([128, 16 * GRP], BF16) for _ in range(2)]
            dbuf = [bp.take([128, NCPS * 128], BF16) for _ in range(4)]
            sq_c = [bp.take([128, TT], BF16) for _ in range(3)]
            R_t = bp.take([128, TT], F32)
            M_t = bp.take([128, TT], BF16)
            gcp = [bp.take([128, TT], F32) for _ in range(2)]
            g2t = [bp.take([128, TT], F32) for _ in range(2)]
            tnh = [bp.take([128, TT], F32) for _ in range(2)]
            gel = [bp.take([128, TT], BF16) for _ in range(2)]
            upc = [bp.take([128, TT], BF16) for _ in range(2)]
            mean_r = rowS[:, 0:TT]; msq_r = rowS[:, TT:2 * TT]
            tv_r = rowS[:, 2 * TT:3 * TT]; var_r = rowS[:, 3 * TT:4 * TT]
            lnv_r = rowS[:, 4 * TT:5 * TT]; rstd_r = rowS[:, 5 * TT:6 * TT]
            rstd_bfv = rowS[:, 6 * TT:6 * TT + TT // 2].bitcast(BF16)[:, 0:TT]
            mean_bfv = rowS[:, 6 * TT + TT // 2:7 * TT].bitcast(BF16)[:, 0:TT]

            W("tensor", "dA", dec_dA)
            W("tensor", "dS", A_dS)
            W("tensor", "sV", A_sV)
            W("sync", "dA", dec_dA)
            W("sync", "sV", A_sV)
            W("scalar", "sV", A_sV)
            W("scalar", "dS", A_dS)
            W("gpsimd", "dA", dec_dA)
            W("gpsimd", "sV", A_sV)
            W("gpsimd", "dS", A_dS)

            x23 = x2d.rearrange("(c p) t -> p c t", p=128)
            x2_sb3 = x2_sb.rearrange("p (c t) -> p c t", t=TT)

            GA, GB = 0.035677408136300795, 0.7978845608028654

            gu_pm = {}
            rel_v = {}
            rel_a = {}
            vv_g = {}
            inter_done = {}
            dn_pm = {}
            dn_ev = {}
            last_rd = {}
            glob = 0
            gfo = 0
            out_st = None

            def emit_dn(ti_, ns_src, fo, par_src, bc_guard=None):
                nonlocal gfo
                fp = gfo % 4
                if gfo >= 4:
                    W("gpsimd", "sP", dn_pm[gfo - 4])
                ldd = O("gpsimd", lambda e, ns=ns_src, fo=fo, b=dbuf[fp]:
                        e.dma_start(out=b.rearrange("p (r f) -> p r f", f=128),
                                    in_=ddec[ns, fo]), "dG", 16)
                W("tensor", "dG", ldd)
                if fo == 0:
                    W("tensor", "sV", inter_done[(ti_, ns_src)])
                if gfo >= 2:
                    W("tensor", "sV", dn_ev[gfo - 2])
                if bc_guard is not None:
                    W("tensor", "sA", bc_guard)
                Dp = ps6[4 + gfo % 2]
                itr = inter2[par_src]

                def dn_mms(t, b=dbuf[fp], Dp=Dp, itr=itr):
                    for r in range(NCPS):
                        rr = t.matmul(Dp[:], lhsT=b[:, r * 128:(r + 1) * 128],
                                      rhs=itr[:, r * TT:(r + 1) * TT],
                                      start=(r == 0), stop=(r == NCPS - 1))
                    return rr
                pm = O("tensor", dn_mms, "sP")
                dn_pm[gfo] = pm
                last_rd[par_src] = pm
                W("vector", "sP", pm)
                O("vector", lambda v, Dp=Dp, fo=fo: v.tensor_tensor(
                    out=x2_sb[:, fo * TT:(fo + 1) * TT], in0=Dp[:],
                    in1=x2_sb[:, fo * TT:(fo + 1) * TT], op=ALU.add))
                dve_pad(1)
                dn_ev[gfo] = O("vector", lambda v: v.tensor_copy(out=pad1[:], in_=pad1[:]), "sV")
                gfo += 1

            for ti in range(NTILES):
                t0 = ti * TT
                if out_st is not None:
                    W("sync", "dS", out_st)
                for cg in range(4):
                    O("sync", lambda e, cg=cg, t0=t0:
                      e.dma_start(out=x2_sb3[:, cg * 4:(cg + 1) * 4, :],
                                  in_=x23[:, cg * 4:(cg + 1) * 4, t0:t0 + TT]), "dS", 16)
                ld_x2 = em.val("dS")
                W("vector", "dS", ld_x2)
                O("vector", lambda v: v.tensor_copy(out=xln[:], in_=x2_sb[:]))
                dve_pad(1)
                v_c = O("vector", lambda v: v.tensor_copy(out=pad1[:], in_=pad1[:]), "sV")
                W("scalar", "sV", v_c)
                W("tensor", "sV", v_c)

                def mean_mms2(t):
                    for kc in range(NCH):
                        r = t.matmul(psR[0][:], lhsT=ones_col[:], rhs=xln[:, kc * TT:(kc + 1) * TT],
                                     start=(kc == 0), stop=(kc == NCH - 1))
                    return r
                O("tensor", mean_mms2)
                sq_used = {}
                for kc in range(NCH):
                    if kc >= 3:
                        W("scalar", "sP", sq_used[kc - 3])
                    ak = O("scalar", lambda s, kc=kc: s.activation(
                        out=sq_c[kc % 3][:], in_=xln[:, kc * TT:(kc + 1) * TT],
                        func=AF.Square), "sA")
                    W("tensor", "sA", ak)
                    sq_used[kc] = O("tensor", lambda t, kc=kc: t.matmul(
                        psR[1][:], lhsT=ones_col[:], rhs=sq_c[kc % 3][:],
                        start=(kc == 0), stop=(kc == NCH - 1)), "sP")
                p_ln = em.val("sP")
                W("scalar", "sP", p_ln)
                O("scalar", lambda s: s.activation(out=mean_r, in_=psR[0][:], func=AF.Copy,
                                                   scale=c_inv[:]))
                O("scalar", lambda s: s.activation(out=msq_r, in_=psR[1][:], func=AF.Copy,
                                                   scale=c_inv[:]))
                a_mr = O("scalar", lambda s: s.activation(out=pad1[:], in_=pad1[:],
                                                          func=AF.Copy), "sA")
                W("vector", "sA", a_mr)
                O("vector", lambda v: v.tensor_tensor(out=tv_r, in0=mean_r, in1=mean_r, op=ALU.mult))
                dve_pad(1)
                O("vector", lambda v: v.tensor_tensor(out=var_r, in0=msq_r, in1=tv_r, op=ALU.subtract))
                dve_pad(1)
                v_var = O("vector", lambda v: v.tensor_copy(out=pad1[:], in_=pad1[:]), "sV")
                W("scalar", "sV", v_var)
                O("scalar", lambda s: s.activation(out=lnv_r, in_=var_r, func=AF.Ln, bias=c_eps[:]))
                act_pad(1)
                O("scalar", lambda s: s.activation(out=rstd_r, in_=lnv_r, func=AF.Exp,
                                                   scale=c_nh[:]))
                a_std = O("scalar", lambda s: s.activation(out=pad1[:], in_=pad1[:],
                                                           func=AF.Copy), "sA")
                W("vector", "sA", a_std)
                O("vector", lambda v: v.tensor_copy(out=rstd_bfv, in_=rstd_r))
                O("vector", lambda v: v.tensor_copy(out=mean_bfv, in_=mean_r))
                dve_pad(1)
                v_rw = O("vector", lambda v: v.tensor_copy(out=pad1[:], in_=pad1[:]), "sV")
                W("tensor", "sV", v_rw)
                if gfo >= 1:
                    W("tensor", "sV", dn_ev[gfo - 1])

                def bc2(t):
                    t.matmul(ps6[4][:], lhsT=ones_row[:], rhs=rstd_bfv, start=True, stop=True)
                    return t.matmul(ps6[5][:], lhsT=ones_row[:], rhs=mean_bfv, start=True, stop=True)
                p_bc2 = O("tensor", bc2, "sP")
                W("scalar", "sP", p_bc2)
                O("scalar", lambda s: s.activation(out=R_t[:], in_=ps6[4][:], func=AF.Copy))
                O("scalar", lambda s: s.activation(out=M_t[:], in_=ps6[5][:], func=AF.Copy))
                a_bc = O("scalar", lambda s: s.activation(out=pad1[:], in_=pad1[:],
                                                          func=AF.Copy), "sA")
                W("vector", "sA", a_bc)

                def center_ops(v):
                    for oc in range(NCH):
                        v.tensor_tensor(out=xln[:, oc * TT:(oc + 1) * TT],
                                        in0=xln[:, oc * TT:(oc + 1) * TT],
                                        in1=M_t[:], op=ALU.subtract)
                    return v.tensor_copy(out=pad1[:], in_=pad1[:])
                O("vector", center_ops)
                v_xln = O("vector", lambda v: v.tensor_copy(out=pad1[:], in_=pad1[:]), "sV")
                W("tensor", "sV", v_xln)

                for ns in range(NSUP):
                    par = ns % 2
                    first_write_guard = last_rd.get(par)
                    for g in range(NGRP):
                        gp = (glob // JPG) % 2
                        grp_abs = ns * NGRP + g
                        if glob >= 2 * JPG:
                            W("sync", "sP", gu_pm[glob - JPG - 1])
                        O("sync", lambda e, gi=grp_abs, b=gbuf[gp]:
                          e.dma_start(out=b, in_=gdec[gi].rearrange("p kc n -> p (kc n)")),
                          "dS", 16)
                        O("sync", lambda e, gi=grp_abs, b=ubuf[gp]:
                          e.dma_start(out=b, in_=udec[gi].rearrange("p kc n -> p (kc n)")),
                          "dS", 16)
                        ldg = em.val("dS")
                        for j in range(JPG):
                            jj = g * JPG + j
                            p = glob % 2
                            W("tensor", "dS", ldg)
                            G = ps6[p]
                            U = ps6[2 + p]
                            if glob >= 2:
                                W("tensor", "sV", rel_v[glob - 2])
                                W("tensor", "sA", rel_a[glob - 2])

                            def gu_mms(t, b=gbuf[gp], bu=ubuf[gp], j=j, G=G, U=U):
                                for kc in range(NCH):
                                    t.matmul(G[:], lhsT=b[:, kc * GRP + j * 128:kc * GRP + (j + 1) * 128],
                                             rhs=xln[:, kc * TT:(kc + 1) * TT],
                                             start=(kc == 0), stop=(kc == NCH - 1))
                                for kc in range(NCH):
                                    r = t.matmul(U[:], lhsT=bu[:, kc * GRP + j * 128:kc * GRP + (j + 1) * 128],
                                                 rhs=xln[:, kc * TT:(kc + 1) * TT],
                                                 start=(kc == 0), stop=(kc == NCH - 1))
                                return r
                            pm = O("tensor", gu_mms, "sP")
                            gu_pm[glob] = pm

                            W("vector", "sP", pm)
                            O("vector", lambda v, G=G, p=p: v.tensor_tensor(
                                out=gcp[p][:], in0=G[:], in1=R_t[:], op=ALU.mult))
                            O("vector", lambda v, U=U, p=p: v.tensor_tensor(
                                out=upc[p][:], in0=U[:], in1=R_t[:], op=ALU.mult))
                            dve_pad(1)
                            va = O("vector", lambda v: v.tensor_copy(out=pad1[:], in_=pad1[:]), "sV")
                            rel_v[glob] = va

                            W("scalar", "sV", va)
                            O("scalar", lambda s, p=p: s.activation(
                                out=g2t[p][:], in_=gcp[p][:], func=AF.Square))
                            aa = O("scalar", lambda s: s.activation(out=pad1[:], in_=pad1[:],
                                                                    func=AF.Copy), "sA")
                            rel_a[glob] = aa

                            W("vector", "sA", aa)
                            O("vector", lambda v, p=p: v.tensor_scalar(
                                out=g2t[p][:], in0=g2t[p][:], scalar1=GA, scalar2=GB,
                                op0=ALU.mult, op1=ALU.add))
                            dve_pad(1)
                            O("vector", lambda v, p=p: v.tensor_tensor(
                                out=g2t[p][:], in0=g2t[p][:], in1=gcp[p][:], op=ALU.mult))
                            dve_pad(1)
                            vb = O("vector", lambda v: v.tensor_copy(out=pad1[:], in_=pad1[:]), "sV")

                            W("scalar", "sV", vb)
                            O("scalar", lambda s, p=p: s.activation(
                                out=tnh[p][:], in_=g2t[p][:], func=AF.Tanh))
                            at = O("scalar", lambda s: s.activation(out=pad1[:], in_=pad1[:],
                                                                    func=AF.Copy), "sA")

                            W("vector", "sA", at)
                            if jj == 0 and first_write_guard is not None:
                                W("vector", "sP", first_write_guard)
                            O("vector", lambda v, p=p: v.tensor_scalar(
                                out=tnh[p][:], in0=tnh[p][:], scalar1=0.5, scalar2=0.5,
                                op0=ALU.mult, op1=ALU.add))
                            dve_pad(1)
                            O("vector", lambda v, p=p: v.tensor_tensor(
                                out=gel[p][:], in0=tnh[p][:], in1=gcp[p][:], op=ALU.mult))
                            dve_pad(1)
                            O("vector", lambda v, p=p, jj=jj, par=par: v.tensor_tensor(
                                out=inter2[par][:, jj * TT:(jj + 1) * TT], in0=gel[p][:],
                                in1=upc[p][:], op=ALU.mult))
                            dve_pad(1)
                            vv = O("vector", lambda v: v.tensor_copy(out=pad1[:], in_=pad1[:]), "sV")
                            vv_g[glob] = vv
                            glob += 1

                            if ns >= 1:
                                emit_dn(ti, ns - 1, jj, 1 - par,
                                        bc_guard=(a_bc if (ns == 1 and jj < 2) else None))
                    inter_done[(ti, ns)] = vv_g[glob - 1]
                for fo in range(NCH):
                    emit_dn(ti, NSUP - 1, fo, (NSUP - 1) % 2)
                acc_done = em.val("sV")
                W("sync", "sV", acc_done)
                for oc in range(NCH):
                    O("sync", lambda e, oc=oc, t0=t0:
                      e.dma_start(out=out[oc * 128:(oc + 1) * 128, t0:t0 + TT],
                                  in_=x2_sb[:, oc * TT:(oc + 1) * TT]), "dS", 16)
                out_st = em.val("dS")

        if dbg_outs:
            W("gpsimd", "dA", em.val("dA"))
            W("gpsimd", "dS", em.val("dS"))
            W("gpsimd", "sV", em.val("sV"))
            if "o_gdec" in dbg_outs:
                O("gpsimd", lambda e: e.dma_start(
                    out=dbg_outs["o_gdec"].rearrange("a b c d -> (a b) (c d)"),
                    in_=gdec.rearrange("a b c d -> (a b) (c d)")), "dG", 16)
                O("gpsimd", lambda e: e.dma_start(
                    out=dbg_outs["o_ddec"].rearrange("a b c d e -> (a b) (c d e)"),
                    in_=ddec.rearrange("a b c d e -> (a b) (c d e)")), "dG", 16)
            if "o_x2" in dbg_outs:
                O("gpsimd", lambda e: e.dma_start(out=dbg_outs["o_x2"], in_=x2d), "dG", 16)

        W("sync", "dS", em.val("dS"))
        W("sync", "dA", em.val("dA"))
        W("sync", "dG", em.val("dG"))
        W("gpsimd", "dG", em.val("dG"))
        W("scalar", "dA", em.val("dA"))

        with nc.Block() as block:
            @block.sync
            def _(e):
                for fn in em.ops["sync"]:
                    fn(e)

            @block.gpsimd
            def _(e):
                for fn in em.ops["gpsimd"]:
                    fn(e)

            @block.vector
            def _(e):
                for fn in em.ops["vector"]:
                    fn(e)

            @block.scalar
            def _(e):
                for fn in em.ops["scalar"]:
                    fn(e)

            @block.tensor
            def _(e):
                for fn in em.ops["tensor"]:
                    fn(e)

    return nc


def _stage_inputs(L0_attn_x, L0_attn_W_q, L0_attn_W_kv, L0_attn_W_o,
                  L0_ffn_gate_W_packed, L0_ffn_gate_scales,
                  L0_ffn_up_W_packed, L0_ffn_up_scales,
                  L0_ffn_down_W_packed, L0_ffn_down_scales):
    pi = perm2048()
    PI = perm16384()

    def tile_wT(wT):
        return np.ascontiguousarray(wT.reshape(16, 128, 16, 128).transpose(2, 1, 0, 3))

    x = np.asarray(L0_attn_x, dtype=np.float32)
    wq = np.asarray(L0_attn_W_q, dtype=np.float32)
    wkv = np.asarray(L0_attn_W_kv, dtype=np.float32)
    wo = np.asarray(L0_attn_W_o, dtype=np.float32)

    wq_t = tile_wT(np.ascontiguousarray(wq[:, pi].T))
    wkv_t = tile_wT(np.ascontiguousarray(wkv[:, pi].T))
    wo_t = tile_wT(np.ascontiguousarray(wo[pi, :].T))

    def pack16(Wp):
        return np.ascontiguousarray(np.asarray(Wp).view(np.int16)[:, 0::2])

    g16 = np.ascontiguousarray(pack16(L0_ffn_gate_W_packed)[PI, :].T)
    u16 = np.ascontiguousarray(pack16(L0_ffn_up_W_packed)[PI, :].T)
    d16 = np.ascontiguousarray(pack16(L0_ffn_down_W_packed)[pi, :].T)

    gs = np.asarray(L0_ffn_gate_scales, dtype=np.float32)
    us = np.asarray(L0_ffn_up_scales, dtype=np.float32)
    ds = np.asarray(L0_ffn_down_scales, dtype=np.float32)
    gse = np.ascontiguousarray(np.repeat(gs[PI, :].T, 16, axis=0))
    use2 = np.ascontiguousarray(np.repeat(us[PI, :].T, 16, axis=0))
    dse = np.ascontiguousarray(np.repeat(ds[pi, :].T, 16, axis=0))

    ar = np.arange(128, dtype=np.float64)
    ph = 2.0 * np.pi * np.outer(ar, ar) / 128.0
    dftc = np.cos(ph).astype(ml_dtypes.bfloat16)
    dfts = (-np.sin(ph)).astype(ml_dtypes.bfloat16)
    dftci = (np.cos(ph) / 128.0).astype(ml_dtypes.bfloat16)
    dftsi = (-np.sin(ph) / 128.0).astype(ml_dtypes.bfloat16)

    in_maps = []
    for c in range(NCORES):
        xs = [np.ascontiguousarray(x[c * SPC + b].T[pi]) for b in range(SPC)]
        x_core = np.ascontiguousarray(np.concatenate(xs, axis=1))
        in_maps.append(dict(
            x_in=x_core, wq_t=wq_t, wkv_t=wkv_t, wo_t=wo_t,
            g16=g16, u16=u16, d16=d16, gse=gse, use=use2, dse=dse,
            dftc=dftc, dfts=dfts, dftci=dftci, dftsi=dftsi,
        ))
    return in_maps, pi


def kernel(**inputs):
    in_maps, pi = _stage_inputs(**inputs)
    nc = build_nc()
    res = run_bass_kernel_spmd(nc, in_maps, core_ids=list(range(NCORES)))
    outp = np.empty((NB, S, D), dtype=np.float32)
    for c in range(NCORES):
        r = np.asarray(res.results[c]["out"])
        for b in range(SPC):
            seq = r[:, b * S:(b + 1) * S]
            o = np.empty((S, D), dtype=np.float32)
            o[:, pi] = seq.T
            outp[c * SPC + b] = o
    return outp


# version 30
# speedup vs baseline: 1.1940x; 1.1304x over previous
import numpy as np
import ml_dtypes
from contextlib import ExitStack

import concourse.bass as bass
import concourse.mybir as mybir
from concourse.bass_utils import run_bass_kernel_spmd

F32 = mybir.dt.float32
BF16 = mybir.dt.bfloat16
I16 = mybir.dt.int16
ALU = mybir.AluOpType
AF = mybir.ActivationFunctionType

NB, S, D = 16, 2048, 2048
NCORES = 8
SPC = NB // NCORES
TCORE = SPC * S
TT = 512
NTILES = TCORE // TT
NCH = D // 128
H = 16
MLP = 16384
NSUPER = 2048
NSUP = MLP // NSUPER
NCPS = NSUPER // 128
GRP = 256
NGRP = NSUPER // GRP
JPG = GRP // 128
A_LEVEL = 4
M_LEVEL = 5
M_SQ = True
M_CP = True


def perm2048():
    p = np.empty(2048, dtype=np.int64)
    j = 0
    for c in range(8):
        for par in range(2):
            for q in range(128):
                p[j] = 2 * (128 * c + q) + par
                j += 1
    return p


def perm16384():
    p = np.empty(16384, dtype=np.int64)
    j = 0
    for c in range(64):
        for par in range(2):
            for q in range(128):
                p[j] = 2 * (128 * c + q) + par
                j += 1
    return p


class Em:
    def __init__(self):
        self.ops = {k: [] for k in ("sync", "gpsimd", "vector", "scalar", "tensor")}
        self.cnt = {}

    def val(self, name):
        return self.cnt.get(name, 0)

    def bump(self, name, k):
        self.cnt[name] = self.cnt.get(name, 0) + k
        return self.cnt[name]


class Pool:
    def __init__(self, tensor, n_i16):
        self.t = tensor
        self.n = n_i16
        self.off = 0

    def take(self, shape, dt):
        assert shape[0] == 128
        elems = int(np.prod(shape[1:]))
        nbytes = elems * mybir.dt.size(dt)
        assert nbytes % 2 == 0
        n16 = nbytes // 2
        off = (self.off + 31) & ~31
        assert off + n16 <= self.n, f"pool overflow: {off}+{n16}>{self.n}"
        self.off = off + n16
        ap = self.t[:, off:off + n16].bitcast(dt)
        if len(shape) > 2:
            ap = ap.rearrange("p (a b) -> p a b", b=shape[-1])
        return ap


def build_nc(phases=("dec", "A", "B"), dbg=(), detect_races=True):
    nc = bass.Bass("TRN2", target_bir_lowering=False,
                   detect_race_conditions=detect_races)

    x_in = nc.dram_tensor("x_in", [D, TCORE], F32, kind="ExternalInput").ap()
    wq_t = nc.dram_tensor("wq_t", [16, 128, 16, 128], F32, kind="ExternalInput").ap()
    wkv_t = nc.dram_tensor("wkv_t", [16, 128, 16, 128], F32, kind="ExternalInput").ap()
    wo_t = nc.dram_tensor("wo_t", [16, 128, 16, 128], F32, kind="ExternalInput").ap()
    g16 = nc.dram_tensor("g16", [1024, MLP], I16, kind="ExternalInput").ap()
    u16 = nc.dram_tensor("u16", [1024, MLP], I16, kind="ExternalInput").ap()
    d16 = nc.dram_tensor("d16", [8192, 2048], I16, kind="ExternalInput").ap()
    gse = nc.dram_tensor("gse", [1024, MLP], F32, kind="ExternalInput").ap()
    use_ = nc.dram_tensor("use", [1024, MLP], F32, kind="ExternalInput").ap()
    dse = nc.dram_tensor("dse", [8192, 2048], F32, kind="ExternalInput").ap()
    dftc = nc.dram_tensor("dftc", [128, 128], BF16, kind="ExternalInput").ap()
    dfts = nc.dram_tensor("dfts", [128, 128], BF16, kind="ExternalInput").ap()
    dftci = nc.dram_tensor("dftci", [128, 128], BF16, kind="ExternalInput").ap()
    dftsi = nc.dram_tensor("dftsi", [128, 128], BF16, kind="ExternalInput").ap()

    out = nc.dram_tensor("out", [D, TCORE], F32, kind="ExternalOutput").ap()

    NGRPS = MLP // GRP
    gdec = nc.dram_tensor("gdec", [NGRPS, 128, NCH, GRP], BF16, kind="Internal").ap()
    udec = nc.dram_tensor("udec", [NGRPS, 128, NCH, GRP], BF16, kind="Internal").ap()
    ddec = nc.dram_tensor("ddec", [NSUP, NCH, 128, NCPS, 128], BF16, kind="Internal").ap()
    x2d = nc.dram_tensor("x2d", [D, TCORE], F32, kind="Internal").ap()

    dbg_outs = {}
    if "dec" in dbg:
        dbg_outs["o_gdec"] = nc.dram_tensor("o_gdec", [MLP // GRP, 128, NCH, GRP], BF16,
                                            kind="ExternalOutput").ap()
        dbg_outs["o_ddec"] = nc.dram_tensor("o_ddec", [NSUP, NCH, 128, NCPS, 128], BF16,
                                            kind="ExternalOutput").ap()
    if "x2" in dbg:
        dbg_outs["o_x2"] = nc.dram_tensor("o_x2", [D, TCORE], F32, kind="ExternalOutput").ap()

    em = Em()

    with ExitStack() as ctx:
        E = ctx.enter_context
        dS = E(nc.semaphore("dS"))
        dG = E(nc.semaphore("dG"))
        dA = E(nc.semaphore("dA"))
        sP = E(nc.semaphore("sP"))
        sV = E(nc.semaphore("sV"))
        sA = E(nc.semaphore("sA"))
        sG = E(nc.semaphore("sG"))
        SEM = {"dS": dS, "dG": dG, "dA": dA, "sP": sP, "sV": sV, "sA": sA, "sG": sG}

        def W(eng, name, v):
            em.ops[eng].append(lambda e, s=SEM[name], vv=v: e.wait_ge(s, vv))

        def O(eng, fn, incname=None, k=1):
            if incname is None:
                em.ops[eng].append(fn)
                return None
            v = em.bump(incname, k)
            em.ops[eng].append(lambda e, f=fn, s=SEM[incname], kk=k: f(e).then_inc(s, kk))
            return v

        ones_col = E(nc.sbuf_tensor("ones_col", [128, 1], BF16))
        ones_row = E(nc.sbuf_tensor("ones_row", [1, 128], BF16))
        ones_all = E(nc.sbuf_tensor("ones_all", [128, 128], BF16))
        zeros_t = E(nc.sbuf_tensor("zeros_t", [128, TT], F32))
        c_neg = E(nc.sbuf_tensor("c_neg", [1, 1], F32))
        c_inv = E(nc.sbuf_tensor("c_inv", [1, 1], F32))
        c_eps = E(nc.sbuf_tensor("c_eps", [1, 1], F32))
        c_nh = E(nc.sbuf_tensor("c_nh", [1, 1], F32))
        c_nh128 = E(nc.sbuf_tensor("c_nh128", [128, 1], F32))
        c_lnb = E(nc.sbuf_tensor("c_lnb", [128, 1], F32))
        t_dftc = E(nc.sbuf_tensor("t_dftc", [128, 128], BF16))
        t_dfts = E(nc.sbuf_tensor("t_dfts", [128, 128], BF16))
        t_dftci = E(nc.sbuf_tensor("t_dftci", [128, 128], BF16))
        t_dftsi = E(nc.sbuf_tensor("t_dftsi", [128, 128], BF16))
        pad1 = E(nc.sbuf_tensor("pad1", [1, 8], F32))
        pad2 = E(nc.sbuf_tensor("pad2", [1, 8], F32))
        rowS = E(nc.sbuf_tensor("rowS", [1, 8 * TT], F32))
        negm_bf = E(nc.sbuf_tensor("negm_bf", [1, TT], BF16))
        wbar_q = E(nc.sbuf_tensor("wbar_q", [1, 2048], BF16))
        wbar_kv = E(nc.sbuf_tensor("wbar_kv", [1, 2048], BF16))

        POOL_N = 89 * 1024
        pool_t = E(nc.sbuf_tensor("pool", [128, POOL_N], I16))

        ps6 = [E(nc.psum_tensor(f"ps{i}", [128, TT], F32)) for i in range(8)]
        psR = [ps6[6][0:1, :], ps6[7][0:1, :]]

        O("vector", lambda v: v.memset(ones_col[:], 1.0))
        O("vector", lambda v: v.memset(ones_row[:], 1.0))
        O("vector", lambda v: v.memset(ones_all[:], 1.0))
        O("vector", lambda v: v.memset(zeros_t[:], 0.0))
        O("vector", lambda v: v.memset(c_neg[:], -1.0 / 2048.0))
        O("vector", lambda v: v.memset(c_inv[:], 1.0 / 2048.0))
        O("vector", lambda v: v.memset(c_eps[:], 1e-5))
        O("vector", lambda v: v.memset(c_nh[:], -0.5))
        O("vector", lambda v: v.memset(c_nh128[:], -0.5))
        O("vector", lambda v: v.memset(c_lnb[:], -2.4260151319598084))
        O("vector", lambda v: v.memset(pad1[:], 0.0))
        O("gpsimd", lambda g: g.memset(pad2[:], 0.0))
        O("vector", lambda v: v.tensor_copy(out=pad1[:], in_=pad1[:]))
        v_init = O("vector", lambda v: v.tensor_copy(out=pad1[:], in_=pad1[:]), "sV")
        O("sync", lambda e: e.dma_start(out=t_dftc[:], in_=dftc), "dS", 16)
        O("sync", lambda e: e.dma_start(out=t_dfts[:], in_=dfts), "dS", 16)
        O("sync", lambda e: e.dma_start(out=t_dftci[:], in_=dftci), "dS", 16)
        d_init = O("sync", lambda e: e.dma_start(out=t_dftsi[:], in_=dftsi), "dS", 16)

        def dve_pad(n=1):
            for _ in range(n):
                O("vector", lambda v: v.tensor_copy(out=pad1[:], in_=pad1[:]))

        def act_pad(n=1):
            for _ in range(n):
                O("scalar", lambda s: s.activation(out=pad1[:], in_=pad1[:], func=AF.Copy))

        def gp_pad(n=1):
            for _ in range(n):
                O("gpsimd", lambda g: g.tensor_copy(out=pad2[:], in_=pad2[:]))

        if "dec" in phases:
            dp = Pool(pool_t, POOL_N)
            FD = 4096
            pk_b = [dp.take([128, FD], I16) for _ in range(2)]
            sc_b = [dp.take([128, FD], F32) for _ in range(2)]
            uL = dp.take([128, FD], I16); uH = dp.take([128, FD], I16)
            eL = dp.take([128, FD], I16); eH = dp.take([128, FD], I16)
            mL = dp.take([128, FD], I16); mH = dp.take([128, FD], I16)
            bL = dp.take([128, FD], I16); bH = dp.take([128, FD], I16)
            plL = [dp.take([128, FD], BF16) for _ in range(2)]
            plH = [dp.take([128, FD], BF16) for _ in range(2)]
            sc_bf = dp.take([128, FD], BF16)

            tiles = []
            for mat, scl, dst in ((g16, gse, gdec), (u16, use_, udec)):
                for c in range(8):
                    for t4 in range(4):
                        n0 = t4 * 4096
                        tiles.append((
                            mat[c * 128:(c + 1) * 128, n0:n0 + 4096],
                            scl[c * 128:(c + 1) * 128, n0:n0 + 4096],
                            (dst, c, n0),
                            False))
            for c2 in range(32):
                r0 = c2 * 256
                dsts = []
                for half in range(2):
                    C = 2 * c2 + half
                    for par in range(2):
                        nch_ = 2 * C + par
                        nsx, r = nch_ // NCPS, nch_ % NCPS
                        dsts.append(ddec[nsx, :, :, r, :].transpose([1, 0, 2]))
                tiles.append((
                    d16[r0:r0 + 256, :].rearrange("(h p) f -> p h f", p=128),
                    dse[r0:r0 + 256, :].rearrange("(h p) f -> p h f", p=128),
                    dsts, True))

            dec_done = []
            st_cnt = []
            for i, (src, ssrc, dsts, is_down) in enumerate(tiles):
                pg = i % 2
                if i >= 2:
                    W("sync", "sV", dec_done[i - 2])
                pk_v, sc_v = pk_b[pg], sc_b[pg]
                if is_down:
                    pk_v = pk_v.rearrange("p (h f) -> p h f", h=2)
                    sc_v = sc_v.rearrange("p (h f) -> p h f", h=2)
                O("sync", lambda e, o=pk_v, s=src: e.dma_start(out=o, in_=s), "dS", 16)
                nl1 = O("sync", lambda e, o=sc_v, s=ssrc: e.dma_start(out=o, in_=s), "dS", 16)
                W("vector", "dS", nl1)
                if i >= 2:
                    W("vector", "dA", st_cnt[i - 2])
                pk, sc = pk_b[pg], sc_b[pg]
                oL, oH = plL[pg], plH[pg]

                def dec_ops(v, pk=pk, sc=sc, oL=oL, oH=oH):
                    v.tensor_scalar(out=uL, in0=pk, scalar1=7, scalar2=None, op0=ALU.bitwise_and)
                    v.tensor_scalar(out=uH, in0=pk, scalar1=4, scalar2=7,
                                    op0=ALU.logical_shift_right, op1=ALU.bitwise_and)
                    v.tensor_scalar(out=mL, in0=uL, scalar1=1.0, scalar2=-1.0,
                                    op0=ALU.is_ge, op1=ALU.mult)
                    v.tensor_scalar(out=mH, in0=uH, scalar1=1.0, scalar2=-1.0,
                                    op0=ALU.is_ge, op1=ALU.mult)
                    v.tensor_scalar(out=eL, in0=uL, scalar1=1.0, scalar2=None, op0=ALU.is_equal)
                    v.tensor_scalar(out=eH, in0=uH, scalar1=1.0, scalar2=None, op0=ALU.is_equal)
                    v.tensor_tensor(out=uL, in0=uL, in1=eL, op=ALU.bitwise_xor)
                    v.tensor_tensor(out=uH, in0=uH, in1=eH, op=ALU.bitwise_xor)
                    v.tensor_scalar(out=bL, in0=uL, scalar1=64.0, scalar2=16128.0,
                                    op0=ALU.mult, op1=ALU.add)
                    v.tensor_scalar(out=bH, in0=uH, scalar1=64.0, scalar2=16128.0,
                                    op0=ALU.mult, op1=ALU.add)
                    v.tensor_tensor(out=bL, in0=bL, in1=mL, op=ALU.bitwise_and)
                    v.tensor_tensor(out=bH, in0=bH, in1=mH, op=ALU.bitwise_and)
                    v.tensor_scalar(out=eL, in0=pk, scalar1=8, scalar2=12,
                                    op0=ALU.bitwise_and, op1=ALU.logical_shift_left)
                    v.tensor_scalar(out=eH, in0=pk, scalar1=128, scalar2=8,
                                    op0=ALU.bitwise_and, op1=ALU.logical_shift_left)
                    v.tensor_tensor(out=bL, in0=bL, in1=eL, op=ALU.bitwise_or)
                    v.tensor_copy(out=sc_bf, in_=sc)
                    v.tensor_tensor(out=bH, in0=bH, in1=eH, op=ALU.bitwise_or)
                    v.tensor_tensor(out=oL, in0=bL.bitcast(BF16), in1=sc_bf, op=ALU.mult)
                    return v.tensor_tensor(out=oH, in0=bH.bitcast(BF16), in1=sc_bf, op=ALU.mult)
                O("vector", dec_ops)
                dve_pad(1)
                dv = O("vector", lambda v: v.tensor_copy(out=pad1[:], in_=pad1[:]), "sV")
                dec_done.append(dv)
                W("scalar", "sV", dv)
                if not is_down:
                    dst3, cc, n0_ = dsts
                    for par, opl in ((0, oL), (1, oH)):
                        kc = 2 * cc + par
                        for s in range(4096 // GRP):
                            sc2 = O("scalar",
                                    lambda e, o=dst3[n0_ // GRP + s][:, kc, :],
                                    s_=opl[:, s * GRP:(s + 1) * GRP]:
                                    e.dma_start(out=o, in_=s_), "dA", 16)
                else:
                    for hh in range(2):
                        for par, opl in ((0, oL), (1, oH)):
                            srcv = opl.rearrange("p (h f) -> p h f", h=2)[:, hh, :]
                            srcv = srcv.rearrange("p (fo f) -> p fo f", fo=16)
                            sc2 = O("scalar",
                                    lambda e, o=dsts[hh * 2 + par], s=srcv: e.dma_start(out=o, in_=s),
                                    "dA", 16)
                st_cnt.append(sc2)
        dec_dS = em.val("dS")
        dec_dA = em.val("dA")
        dec_sV = em.val("sV")

        if "A" in phases:
            ap_ = Pool(pool_t, POOL_N)
            x_f32 = ap_.take([128, NCH * TT], F32)
            x_bf = ap_.take([128, NCH * TT], BF16)
            q_bf = ap_.take([128, NCH * TT], BF16)
            kv_bf = ap_.take([128, NCH * TT], BF16)
            vh_bf = ap_.take([128, NCH * TT], BF16)
            wbuf = [ap_.take([128, 2048], BF16) for _ in range(2)]
            qfre = [ap_.take([128, TT], BF16) for _ in range(3)]
            qfim = [ap_.take([128, TT], BF16) for _ in range(3)]
            kvre = [ap_.take([128, TT], BF16) for _ in range(3)]
            kvim = [ap_.take([128, TT], BF16) for _ in range(3)]
            sqst = [ap_.take([128, TT], BF16) for _ in range(2)]
            kvnre = [ap_.take([128, TT], BF16) for _ in range(2)]
            kvnim = [ap_.take([128, TT], BF16) for _ in range(2)]
            kvcre = ap_.take([128, TT], F32); kvcim = ap_.take([128, TT], F32)
            t1 = ap_.take([128, TT], F32); t2 = ap_.take([128, TT], F32)
            t3 = ap_.take([128, TT], F32); t4 = ap_.take([128, TT], F32)
            qvre = ap_.take([128, TT], BF16); qvim = ap_.take([128, TT], BF16)
            lq_t = ap_.take([128, TT], F32); lkv_t = ap_.take([128, TT], F32)
            rq_t = [ap_.take([128, TT], F32) for _ in range(3)]
            rkv_t = [ap_.take([128, TT], F32) for _ in range(3)]
            carry_re = ap_.take([128, H], F32); carry_im = ap_.take([128, H], F32)
            x2st = [ap_.take([128, TT], F32) for _ in range(2)]
            n2st = ap_.take([128, 16 * TT], BF16)

            def n2row(r):
                return n2st[64 * (r % 2):64 * (r % 2) + 1,
                            (r // 2) * TT:(r // 2 + 1) * TT]

            W("vector", "dS", d_init)
            W("tensor", "sV", v_init)
            W("tensor", "dS", d_init)
            W("sync", "sV", dec_sV)
            W("sync", "dA", dec_dA)
            W("gpsimd", "sV", dec_sV)
            W("gpsimd", "dA", dec_dA)

            xin3 = x_in.rearrange("(c p) t -> p c t", p=128)
            x_f32_3 = x_f32.rearrange("p (c t) -> p c t", t=TT)
            pm_slot = {}
            ev_slot = {}
            o_ev = {}
            o_st = {}
            for ti in range(NTILES):
                t0 = ti * TT
                for cg in range(4):
                    O("sync", lambda e, cg=cg, t0=t0:
                      e.dma_start(out=x_f32_3[:, cg * 4:(cg + 1) * 4, :],
                                  in_=xin3[:, cg * 4:(cg + 1) * 4, t0:t0 + TT]), "dS", 16)
                ld_x = em.val("dS")
                W("vector", "dS", ld_x)
                O("vector", lambda v: v.tensor_copy(out=x_bf[:], in_=x_f32[:]))
                dve_pad(1)
                c_cast = O("vector", lambda v: v.tensor_copy(out=pad1[:], in_=pad1[:]), "sV")
                W("tensor", "sV", c_cast)

                def mean_mms(t):
                    for kc in range(NCH):
                        r = t.matmul(psR[0][:], lhsT=ones_col[:],
                                     rhs=x_bf[:, kc * TT:(kc + 1) * TT],
                                     start=(kc == 0), stop=(kc == NCH - 1))
                    return r
                p_mean = O("tensor", mean_mms, "sP")
                W("scalar", "sP", p_mean)
                O("scalar", lambda s: s.activation(out=negm_bf[:], in_=psR[0][:],
                                                   func=AF.Copy, scale=c_neg[:]))
                act_pad(1)
                a_negm = O("scalar", lambda s: s.activation(out=pad1[:], in_=pad1[:],
                                                            func=AF.Copy), "sA")

                a_sqp = {}; p_n2 = {}; v_n2 = {}

                def emit_n2(sl):
                    W("tensor", "sA", a_sqp[sl])
                    pb = psR[0] if ti == 0 else psR[sl % 2]
                    gidx = sl - 1 if ti == 0 else sl - 2
                    if gidx >= 0 and gidx in v_n2:
                        W("tensor", "sV", v_n2[gidx])
                    pmn = O("tensor", lambda t, pb=pb, sl=sl: t.matmul(
                        pb[:], lhsT=ones_col[:], rhs=sqst[sl % 2][:],
                        start=True, stop=True), "sP")
                    p_n2[sl] = pmn
                    W("vector", "sP", pmn)
                    O("vector", lambda v, pb=pb, sl=sl: v.tensor_copy(
                        out=n2row(sl), in_=pb[:]))
                    dve_pad(1)
                    v_n2[sl] = O("vector", lambda v: v.tensor_copy(out=pad1[:], in_=pad1[:]),
                                 "sV")

                for pri, (wt, dst, wbar) in (enumerate(((wq_t, q_bf, wbar_q),
                                                       (wkv_t, kv_bf, wbar_kv))) if A_LEVEL >= 2 else []):
                    for oc in range(NCH):
                        slot = pri * NCH + oc
                        pg = slot % 2
                        if slot >= 2:
                            emit_n2(slot - 2)
                        prev = (ti, slot - 2) if slot >= 2 else (ti - 1, 46 + slot)
                        if prev in pm_slot:
                            W("gpsimd", "sP", pm_slot[prev])
                        ldw = O("gpsimd", lambda e, wt=wt, oc=oc, b=wbuf[pg]:
                                e.dma_start(out=b, in_=wt[oc]), "dG", 16)
                        W("tensor", "dG", ldw)
                        if prev in ev_slot:
                            W("tensor", "sA", ev_slot[prev])
                        ps = ps6[pg]

                        def proj_mms(t, b=wbuf[pg], ps=ps):
                            for kc in range(NCH):
                                t.matmul(ps[:], lhsT=b[:, kc * 128:(kc + 1) * 128],
                                         rhs=x_bf[:, kc * TT:(kc + 1) * TT],
                                         start=(kc == 0), stop=False)
                            return t
                        O("tensor", proj_mms)
                        if ti == 0:
                            def wbar_mms(t, b=wbuf[pg]):
                                for kc in range(NCH):
                                    r = t.matmul(psR[1][:, 0:128],
                                                 lhsT=ones_col[:],
                                                 rhs=b[:, kc * 128:(kc + 1) * 128],
                                                 start=(kc == 0), stop=(kc == NCH - 1))
                                return r
                            pw = O("tensor", wbar_mms, "sP")
                            W("scalar", "sP", pw)
                            O("scalar", lambda s, w=wbar, oc=oc:
                              s.activation(out=w[:, oc * 128:(oc + 1) * 128],
                                           in_=psR[1][:, 0:128], func=AF.Copy))
                            aw = O("scalar", lambda s: s.activation(out=pad1[:], in_=pad1[:],
                                                                    func=AF.Copy), "sA")
                            W("tensor", "sA", aw)
                        if slot == 0:
                            W("tensor", "sA", a_negm)
                        pm = O("tensor", lambda t, w=wbar, oc=oc, ps=ps:
                               t.matmul(ps[:], lhsT=w[:, oc * 128:(oc + 1) * 128],
                                        rhs=negm_bf[:], start=False, stop=True), "sP")
                        pm_slot[(ti, slot)] = pm
                        W("scalar", "sP", pm)
                        O("scalar", lambda s, ps=ps, dst=dst, oc=oc:
                          s.activation(out=dst[:, oc * TT:(oc + 1) * TT], in_=ps[:],
                                       func=AF.Copy))
                        ae = O("scalar", lambda s: s.activation(out=pad1[:], in_=pad1[:],
                                                                func=AF.Copy), "sA")
                        ev_slot[(ti, slot)] = ae
                        if slot >= 2 and (slot - 2) in p_n2:
                            W("scalar", "sP", p_n2[slot - 2])
                        O("scalar", lambda s, dst=dst, oc=oc, sl=slot: s.activation(
                            out=sqst[sl % 2][:], in_=dst[:, oc * TT:(oc + 1) * TT],
                            func=AF.Square))
                        a_sqp[slot] = O("scalar", lambda s: s.activation(
                            out=pad1[:], in_=pad1[:], func=AF.Copy), "sA")
                if A_LEVEL >= 2:
                    emit_n2(30)
                    emit_n2(31)
                proj_all = em.val("sA")

                first_of_seq = (ti % (S // TT) == 0)
                f_pfft = {}; f_vcp = {}; f_acp = {}; f_pbc = {}; f_arsq = {}
                f_vscan = {}; f_vmix = {}; f_gmix = {}; f_gkvn = {}
                f_pifft = {}; f_vvh = {}
                for i in (range(H + 5) if A_LEVEL >= 3 else []):
                    if i < H:
                        hs = slice(i * TT, (i + 1) * TT)
                        if i == 0:
                            W("tensor", "sA", proj_all)
                        if i >= 1:
                            W("tensor", "sV", f_vcp[i - 1])
                            W("tensor", "sA", f_acp[i - 1])

                        def fft_mms(t, hs=hs):
                            t.matmul(ps6[0][:], lhsT=t_dftc[:], rhs=q_bf[:, hs], start=True, stop=True)
                            t.matmul(ps6[1][:], lhsT=t_dfts[:], rhs=q_bf[:, hs], start=True, stop=True)
                            t.matmul(ps6[2][:], lhsT=t_dftc[:], rhs=kv_bf[:, hs], start=True, stop=True)
                            return t.matmul(ps6[3][:], lhsT=t_dfts[:], rhs=kv_bf[:, hs],
                                            start=True, stop=True)
                        f_pfft[i] = O("tensor", fft_mms, "sP")
                    k = i - 2
                    if 0 <= k < H:
                        W("tensor", "sV", v_n2[16 + k])
                        if k >= 1:
                            W("tensor", "sA", f_arsq[k - 1])

                        def bcn_mms(t, k=k):
                            bkv = 64 * ((16 + k) % 2)
                            bq = 64 * (k % 2)
                            t.matmul(ps6[4][:], lhsT=ones_all[bkv:bkv + 1, :],
                                     rhs=n2row(16 + k), start=True, stop=True)
                            return t.matmul(ps6[5][:], lhsT=ones_all[bq:bq + 1, :],
                                            rhs=n2row(k), start=True, stop=True)
                        f_pbc[k] = O("tensor", bcn_mms, "sP")
                    k = i - 4
                    if 0 <= k < H:
                        W("tensor", "sV", f_vmix[k])
                        W("tensor", "sG", f_gmix[k])
                        if k >= 2:
                            W("tensor", "sV", f_vvh[k - 2])
                        elif k < 2:
                            W("tensor", "sV", v_n2[30 + k])

                        def ifft_mms(t, k=k):
                            pb = ps6[6 + k % 2]
                            t.matmul(pb[:], lhsT=t_dftci[:], rhs=qvre[:], start=True, stop=False)
                            return t.matmul(pb[:], lhsT=t_dftsi[:], rhs=qvim[:],
                                            start=False, stop=True)
                        f_pifft[k] = O("tensor", ifft_mms, "sP")

                    k = i - 5
                    if 0 <= k < H:
                        W("vector", "sP", f_pifft[k])
                        O("vector", lambda v, k=k: v.tensor_tensor(
                            out=vh_bf[:, k * TT:(k + 1) * TT], in0=ps6[6 + k % 2][:],
                            in1=rq_t[k % 3][:], op=ALU.mult))
                        dve_pad(1)
                        f_vvh[k] = O("vector", lambda v: v.tensor_copy(out=pad1[:], in_=pad1[:]),
                                     "sV")
                    k = i - 2
                    if 0 <= k < H:
                        W("scalar", "sP", f_pbc[k])
                        if k >= 3:
                            W("scalar", "sV", f_vvh[k - 3])

                        def rsq_ops(s, k=k):
                            s.activation(out=lkv_t[:], in_=ps6[4][:], func=AF.Ln)
                            s.activation(out=lq_t[:], in_=ps6[5][:], func=AF.Ln)
                            s.activation(out=rkv_t[k % 3][:], in_=lkv_t[:], func=AF.Exp,
                                         scale=c_nh128[:], bias=c_lnb[:])
                            return s.activation(out=rq_t[k % 3][:], in_=lq_t[:], func=AF.Exp,
                                                scale=c_nh128[:], bias=c_lnb[:])
                        O("scalar", rsq_ops)
                        act_pad(1)
                        f_arsq[k] = O("scalar", lambda s: s.activation(out=pad1[:], in_=pad1[:],
                                                                       func=AF.Copy), "sA")

                    k = i - 3
                    if 0 <= k < H:
                        W("vector", "sG", f_gkvn[k])
                        if k >= 1:
                            W("vector", "sG", f_gmix[k - 1])

                        def scan_ops(v, k=k, first=first_of_seq):
                            ire = 0.0 if first else carry_re[:, k:k + 1]
                            iim = 0.0 if first else carry_im[:, k:k + 1]
                            v.tensor_tensor_scan(out=kvcre[:], data0=kvnre[k % 2][:], data1=zeros_t[:],
                                                 initial=ire, op0=ALU.add, op1=ALU.add)
                            return v.tensor_tensor_scan(out=kvcim[:], data0=kvnim[k % 2][:], data1=zeros_t[:],
                                                        initial=iim, op0=ALU.add, op1=ALU.add)
                        O("vector", scan_ops)
                        dve_pad(1)
                        f_vscan[k] = O("vector", lambda v: v.tensor_copy(out=pad1[:], in_=pad1[:]),
                                       "sV")
                        if k >= 1:
                            W("vector", "sP", f_pifft[k - 1])

                        def cm_ops(v, k=k):
                            v.tensor_copy(out=carry_re[:, k:k + 1], in_=kvcre[:, TT - 1:TT])
                            v.tensor_copy(out=carry_im[:, k:k + 1], in_=kvcim[:, TT - 1:TT])
                            v.tensor_tensor(out=t1[:], in0=qfre[k % 3][:], in1=kvcre[:], op=ALU.mult)
                            return v.tensor_tensor(out=t2[:], in0=qfim[k % 3][:], in1=kvcim[:],
                                                   op=ALU.mult)
                        O("vector", cm_ops)
                        dve_pad(1)
                        O("vector", lambda v: v.tensor_tensor(out=qvre[:], in0=t1[:], in1=t2[:],
                                                              op=ALU.subtract))
                        dve_pad(1)
                        f_vmix[k] = O("vector", lambda v: v.tensor_copy(out=pad1[:], in_=pad1[:]),
                                      "sV")
                    k = i - 2
                    if 0 <= k < H:
                        W("gpsimd", "sA", f_arsq[k])
                        if k >= 2:
                            W("gpsimd", "sV", f_vscan[k - 2])

                        def gkvn_ops(g, k=k):
                            g.tensor_tensor(out=kvnre[k % 2][:], in0=kvre[k % 3][:], in1=rkv_t[k % 3][:],
                                            op=ALU.mult)
                            g.tensor_copy(out=pad2[:], in_=pad2[:])
                            return g.tensor_tensor(out=kvnim[k % 2][:], in0=kvim[k % 3][:],
                                                   in1=rkv_t[k % 3][:], op=ALU.mult)
                        O("gpsimd", gkvn_ops)
                        gp_pad(1)
                        f_gkvn[k] = O("gpsimd", lambda g: g.tensor_copy(out=pad2[:], in_=pad2[:]),
                                      "sG")
                    k = i - 3
                    if 0 <= k < H:
                        W("gpsimd", "sV", f_vscan[k])
                        if k >= 1:
                            W("gpsimd", "sP", f_pifft[k - 1])

                        def gmix_ops(g, k=k):
                            g.tensor_tensor(out=t3[:], in0=qfre[k % 3][:], in1=kvcim[:], op=ALU.mult)
                            g.tensor_tensor(out=t4[:], in0=qfim[k % 3][:], in1=kvcre[:], op=ALU.mult)
                            g.tensor_copy(out=pad2[:], in_=pad2[:])
                            return g.tensor_tensor(out=qvim[:], in0=t3[:], in1=t4[:], op=ALU.add)
                        O("gpsimd", gmix_ops)
                        gp_pad(1)
                        f_gmix[k] = O("gpsimd", lambda g: g.tensor_copy(out=pad2[:], in_=pad2[:]),
                                      "sG")
                    if i < H:
                        W("vector", "sP", f_pfft[i])
                        if i >= 3:
                            W("vector", "sG", f_gmix[i - 3])

                        def cp_ops(v, i=i):
                            v.tensor_copy(out=qfre[i % 3][:], in_=ps6[0][:])
                            return v.tensor_copy(out=qfim[i % 3][:], in_=ps6[1][:])
                        O("vector", cp_ops)
                        dve_pad(1)
                        f_vcp[i] = O("vector", lambda v: v.tensor_copy(out=pad1[:], in_=pad1[:]),
                                     "sV")
                    if i < H:
                        W("scalar", "sP", f_pfft[i])
                        if i >= 3:
                            W("scalar", "sV", f_vscan[i - 3])

                        def cpkv_ops(s, i=i):
                            s.activation(out=kvre[i % 3][:], in_=ps6[2][:], func=AF.Copy)
                            return s.activation(out=kvim[i % 3][:], in_=ps6[3][:], func=AF.Copy)
                        O("scalar", cpkv_ops)
                        act_pad(1)
                        f_acp[i] = O("scalar", lambda s: s.activation(out=pad1[:], in_=pad1[:],
                                                                      func=AF.Copy), "sA")
                mix_all = em.val("sV")

                for oc in (range(NCH) if A_LEVEL >= 4 else []):
                    slot = 32 + oc
                    pg = slot % 2
                    prev = (ti, slot - 2)
                    if prev in pm_slot:
                        W("gpsimd", "sP", pm_slot[prev])
                    ldw = O("gpsimd", lambda e, oc=oc, b=wbuf[pg]:
                            e.dma_start(out=b, in_=wo_t[oc]), "dG", 16)
                    W("tensor", "dG", ldw)
                    if oc == 0:
                        W("tensor", "sV", mix_all)
                    if (ti, oc - 2) in o_ev:
                        W("tensor", "sV", o_ev[(ti, oc - 2)])
                    ps = ps6[pg]

                    def omms(t, b=wbuf[pg], ps=ps):
                        for kc in range(NCH):
                            r = t.matmul(ps[:], lhsT=b[:, kc * 128:(kc + 1) * 128],
                                         rhs=vh_bf[:, kc * TT:(kc + 1) * TT],
                                         start=(kc == 0), stop=(kc == NCH - 1))
                        return r
                    pm = O("tensor", omms, "sP")
                    pm_slot[(ti, slot)] = pm
                    W("vector", "sP", pm)
                    if (ti, oc - 2) in o_st:
                        W("vector", "dS", o_st[(ti, oc - 2)])
                    O("vector", lambda v, ps=ps, oc=oc, pg=pg:
                      v.tensor_tensor(out=x2st[pg][:], in0=ps[:],
                                      in1=x_f32[:, oc * TT:(oc + 1) * TT], op=ALU.add))
                    dve_pad(1)
                    ev = O("vector", lambda v: v.tensor_copy(out=pad1[:], in_=pad1[:]), "sV")
                    o_ev[(ti, oc)] = ev
                    W("sync", "sV", ev)
                    st = O("sync", lambda e, oc=oc, pg=pg, t0=t0:
                           e.dma_start(out=x2d[oc * 128:(oc + 1) * 128, t0:t0 + TT],
                                       in_=x2st[pg][:]), "dS", 16)
                    o_st[(ti, oc)] = st
                W("sync", "sV", em.val("sV"))
        A_dS = em.val("dS")
        A_sV = em.val("sV")

        if "B" in phases:
            bp = Pool(pool_t, POOL_N)
            x2_sb = bp.take([128, NCH * TT], F32)
            xln = bp.take([128, NCH * TT], BF16)
            inter2 = [bp.take([128, NCPS * TT], BF16) for _ in range(2)]
            gbuf = [bp.take([128, 16 * GRP], BF16) for _ in range(2)]
            ubuf = [bp.take([128, 16 * GRP], BF16) for _ in range(2)]
            dbuf = [bp.take([128, NCPS * 128], BF16) for _ in range(4)]
            sq_c = [bp.take([128, TT], BF16) for _ in range(3)]
            R_t = bp.take([128, TT], F32)
            M_t = bp.take([128, TT], BF16)
            gcp = [bp.take([128, TT], F32) for _ in range(2)]
            g2t = [bp.take([128, TT], F32) for _ in range(2)]
            tnh = [bp.take([128, TT], F32) for _ in range(2)]
            gel = [bp.take([128, TT], BF16) for _ in range(2)]
            upc = [bp.take([128, TT], BF16) for _ in range(2)]
            mean_r = rowS[:, 0:TT]; msq_r = rowS[:, TT:2 * TT]
            tv_r = rowS[:, 2 * TT:3 * TT]; var_r = rowS[:, 3 * TT:4 * TT]
            lnv_r = rowS[:, 4 * TT:5 * TT]; rstd_r = rowS[:, 5 * TT:6 * TT]
            rstd_bfv = rowS[:, 6 * TT:6 * TT + TT // 2].bitcast(BF16)[:, 0:TT]
            mean_bfv = rowS[:, 6 * TT + TT // 2:7 * TT].bitcast(BF16)[:, 0:TT]

            W("tensor", "dA", dec_dA)
            W("tensor", "dS", A_dS)
            W("tensor", "sV", A_sV)
            W("sync", "dA", dec_dA)
            W("sync", "sV", A_sV)
            W("scalar", "sV", A_sV)
            W("scalar", "dS", A_dS)
            W("gpsimd", "dA", dec_dA)
            W("gpsimd", "sV", A_sV)
            W("gpsimd", "dS", A_dS)

            x23 = x2d.rearrange("(c p) t -> p c t", p=128)
            x2_sb3 = x2_sb.rearrange("p (c t) -> p c t", t=TT)

            GA, GB = 0.035677408136300795, 0.7978845608028654

            gu_pm = {}
            rel_v = {}
            rel_a = {}
            vv_g = {}
            inter_done = {}
            dn_pm = {}
            dn_ev = {}
            last_rd = {}
            glob = 0
            gfo = 0
            out_st = None

            def emit_dn(ti_, ns_src, fo, par_src, bc_guard=None):
                nonlocal gfo
                fp = gfo % 4
                if gfo >= 4:
                    W("gpsimd", "sP", dn_pm[gfo - 4])
                ldd = O("gpsimd", lambda e, ns=ns_src, fo=fo, b=dbuf[fp]:
                        e.dma_start(out=b.rearrange("p (r f) -> p r f", f=128),
                                    in_=ddec[ns, fo]), "dG", 16)
                W("tensor", "dG", ldd)
                if fo == 0:
                    W("tensor", "sV", inter_done[(ti_, ns_src)])
                if gfo >= 2:
                    W("tensor", "sV", dn_ev[gfo - 2])
                if bc_guard is not None:
                    W("tensor", "sA", bc_guard)
                Dp = ps6[4 + gfo % 2]
                itr = inter2[par_src]

                def dn_mms(t, b=dbuf[fp], Dp=Dp, itr=itr):
                    for r in range(NCPS):
                        rr = t.matmul(Dp[:], lhsT=b[:, r * 128:(r + 1) * 128],
                                      rhs=itr[:, r * TT:(r + 1) * TT],
                                      start=(r == 0), stop=(r == NCPS - 1))
                    return rr
                pm = O("tensor", dn_mms, "sP")
                dn_pm[gfo] = pm
                last_rd[par_src] = pm
                W("vector", "sP", pm)
                O("vector", lambda v, Dp=Dp, fo=fo: v.tensor_tensor(
                    out=x2_sb[:, fo * TT:(fo + 1) * TT], in0=Dp[:],
                    in1=x2_sb[:, fo * TT:(fo + 1) * TT], op=ALU.add))
                dve_pad(1)
                dn_ev[gfo] = O("vector", lambda v: v.tensor_copy(out=pad1[:], in_=pad1[:]), "sV")
                gfo += 1

            for ti in range(NTILES):
                t0 = ti * TT
                if out_st is not None:
                    W("sync", "dS", out_st)
                for cg in range(4):
                    O("sync", lambda e, cg=cg, t0=t0:
                      e.dma_start(out=x2_sb3[:, cg * 4:(cg + 1) * 4, :],
                                  in_=x23[:, cg * 4:(cg + 1) * 4, t0:t0 + TT]), "dS", 16)
                ld_x2 = em.val("dS")
                W("vector", "dS", ld_x2)
                O("vector", lambda v: v.tensor_copy(out=xln[:], in_=x2_sb[:]))
                dve_pad(1)
                v_c = O("vector", lambda v: v.tensor_copy(out=pad1[:], in_=pad1[:]), "sV")
                W("scalar", "sV", v_c)
                W("tensor", "sV", v_c)

                def mean_mms2(t):
                    for kc in range(NCH):
                        r = t.matmul(psR[0][:], lhsT=ones_col[:], rhs=xln[:, kc * TT:(kc + 1) * TT],
                                     start=(kc == 0), stop=(kc == NCH - 1))
                    return r
                O("tensor", mean_mms2)
                sq_used = {}
                for kc in range(NCH):
                    if kc >= 3:
                        W("scalar", "sP", sq_used[kc - 3])
                    ak = O("scalar", lambda s, kc=kc: s.activation(
                        out=sq_c[kc % 3][:], in_=xln[:, kc * TT:(kc + 1) * TT],
                        func=AF.Square), "sA")
                    W("tensor", "sA", ak)
                    sq_used[kc] = O("tensor", lambda t, kc=kc: t.matmul(
                        psR[1][:], lhsT=ones_col[:], rhs=sq_c[kc % 3][:],
                        start=(kc == 0), stop=(kc == NCH - 1)), "sP")
                p_ln = em.val("sP")
                W("scalar", "sP", p_ln)
                O("scalar", lambda s: s.activation(out=mean_r, in_=psR[0][:], func=AF.Copy,
                                                   scale=c_inv[:]))
                O("scalar", lambda s: s.activation(out=msq_r, in_=psR[1][:], func=AF.Copy,
                                                   scale=c_inv[:]))
                a_mr = O("scalar", lambda s: s.activation(out=pad1[:], in_=pad1[:],
                                                          func=AF.Copy), "sA")
                W("vector", "sA", a_mr)
                O("vector", lambda v: v.tensor_tensor(out=tv_r, in0=mean_r, in1=mean_r, op=ALU.mult))
                dve_pad(1)
                O("vector", lambda v: v.tensor_tensor(out=var_r, in0=msq_r, in1=tv_r, op=ALU.subtract))
                dve_pad(1)
                v_var = O("vector", lambda v: v.tensor_copy(out=pad1[:], in_=pad1[:]), "sV")
                W("scalar", "sV", v_var)
                O("scalar", lambda s: s.activation(out=lnv_r, in_=var_r, func=AF.Ln, bias=c_eps[:]))
                act_pad(1)
                O("scalar", lambda s: s.activation(out=rstd_r, in_=lnv_r, func=AF.Exp,
                                                   scale=c_nh[:]))
                a_std = O("scalar", lambda s: s.activation(out=pad1[:], in_=pad1[:],
                                                           func=AF.Copy), "sA")
                W("vector", "sA", a_std)
                O("vector", lambda v: v.tensor_copy(out=rstd_bfv, in_=rstd_r))
                O("vector", lambda v: v.tensor_copy(out=mean_bfv, in_=mean_r))
                dve_pad(1)
                v_rw = O("vector", lambda v: v.tensor_copy(out=pad1[:], in_=pad1[:]), "sV")
                W("tensor", "sV", v_rw)
                if gfo >= 1:
                    W("tensor", "sV", dn_ev[gfo - 1])

                def bc2(t):
                    t.matmul(ps6[4][:], lhsT=ones_row[:], rhs=rstd_bfv, start=True, stop=True)
                    return t.matmul(ps6[5][:], lhsT=ones_row[:], rhs=mean_bfv, start=True, stop=True)
                p_bc2 = O("tensor", bc2, "sP")
                W("scalar", "sP", p_bc2)
                O("scalar", lambda s: s.activation(out=R_t[:], in_=ps6[4][:], func=AF.Copy))
                O("scalar", lambda s: s.activation(out=M_t[:], in_=ps6[5][:], func=AF.Copy))
                a_bc = O("scalar", lambda s: s.activation(out=pad1[:], in_=pad1[:],
                                                          func=AF.Copy), "sA")
                W("vector", "sA", a_bc)

                def center_ops(v):
                    for oc in range(NCH):
                        v.tensor_tensor(out=xln[:, oc * TT:(oc + 1) * TT],
                                        in0=xln[:, oc * TT:(oc + 1) * TT],
                                        in1=M_t[:], op=ALU.subtract)
                    return v.tensor_copy(out=pad1[:], in_=pad1[:])
                O("vector", center_ops)
                v_xln = O("vector", lambda v: v.tensor_copy(out=pad1[:], in_=pad1[:]), "sV")
                W("tensor", "sV", v_xln)

                for ns in range(NSUP):
                    par = ns % 2
                    first_write_guard = last_rd.get(par)
                    for g in range(NGRP):
                        gp = (glob // JPG) % 2
                        grp_abs = ns * NGRP + g
                        if glob >= 2 * JPG:
                            W("sync", "sP", gu_pm[glob - JPG - 1])
                        O("sync", lambda e, gi=grp_abs, b=gbuf[gp]:
                          e.dma_start(out=b, in_=gdec[gi].rearrange("p kc n -> p (kc n)")),
                          "dS", 16)
                        O("sync", lambda e, gi=grp_abs, b=ubuf[gp]:
                          e.dma_start(out=b, in_=udec[gi].rearrange("p kc n -> p (kc n)")),
                          "dS", 16)
                        ldg = em.val("dS")
                        for j in range(JPG):
                            jj = g * JPG + j
                            p = glob % 2
                            W("tensor", "dS", ldg)
                            G = ps6[p]
                            U = ps6[2 + p]
                            if glob >= 2:
                                W("tensor", "sV", rel_v[glob - 2])
                                W("tensor", "sA", rel_a[glob - 2])

                            def gu_mms(t, b=gbuf[gp], bu=ubuf[gp], j=j, G=G, U=U):
                                for kc in range(NCH):
                                    t.matmul(G[:], lhsT=b[:, kc * GRP + j * 128:kc * GRP + (j + 1) * 128],
                                             rhs=xln[:, kc * TT:(kc + 1) * TT],
                                             start=(kc == 0), stop=(kc == NCH - 1))
                                for kc in range(NCH):
                                    r = t.matmul(U[:], lhsT=bu[:, kc * GRP + j * 128:kc * GRP + (j + 1) * 128],
                                                 rhs=xln[:, kc * TT:(kc + 1) * TT],
                                                 start=(kc == 0), stop=(kc == NCH - 1))
                                return r
                            pm = O("tensor", gu_mms, "sP")
                            gu_pm[glob] = pm

                            W("vector", "sP", pm)
                            O("vector", lambda v, G=G, p=p: v.tensor_tensor(
                                out=gcp[p][:], in0=G[:], in1=R_t[:], op=ALU.mult))
                            O("vector", lambda v, U=U, p=p: v.tensor_tensor(
                                out=upc[p][:], in0=U[:], in1=R_t[:], op=ALU.mult))
                            dve_pad(1)
                            va = O("vector", lambda v: v.tensor_copy(out=pad1[:], in_=pad1[:]), "sV")
                            rel_v[glob] = va

                            W("scalar", "sV", va)
                            O("scalar", lambda s, p=p: s.activation(
                                out=g2t[p][:], in_=gcp[p][:], func=AF.Square))
                            aa = O("scalar", lambda s: s.activation(out=pad1[:], in_=pad1[:],
                                                                    func=AF.Copy), "sA")
                            rel_a[glob] = aa

                            W("vector", "sA", aa)
                            O("vector", lambda v, p=p: v.tensor_scalar(
                                out=g2t[p][:], in0=g2t[p][:], scalar1=GA, scalar2=GB,
                                op0=ALU.mult, op1=ALU.add))
                            dve_pad(1)
                            O("vector", lambda v, p=p: v.tensor_tensor(
                                out=g2t[p][:], in0=g2t[p][:], in1=gcp[p][:], op=ALU.mult))
                            dve_pad(1)
                            vb = O("vector", lambda v: v.tensor_copy(out=pad1[:], in_=pad1[:]), "sV")

                            W("scalar", "sV", vb)
                            O("scalar", lambda s, p=p: s.activation(
                                out=tnh[p][:], in_=g2t[p][:], func=AF.Tanh))
                            at = O("scalar", lambda s: s.activation(out=pad1[:], in_=pad1[:],
                                                                    func=AF.Copy), "sA")

                            W("vector", "sA", at)
                            if jj == 0 and first_write_guard is not None:
                                W("vector", "sP", first_write_guard)
                            O("vector", lambda v, p=p: v.tensor_scalar(
                                out=tnh[p][:], in0=tnh[p][:], scalar1=0.5, scalar2=0.5,
                                op0=ALU.mult, op1=ALU.add))
                            dve_pad(1)
                            O("vector", lambda v, p=p: v.tensor_tensor(
                                out=gel[p][:], in0=tnh[p][:], in1=gcp[p][:], op=ALU.mult))
                            dve_pad(1)
                            O("vector", lambda v, p=p, jj=jj, par=par: v.tensor_tensor(
                                out=inter2[par][:, jj * TT:(jj + 1) * TT], in0=gel[p][:],
                                in1=upc[p][:], op=ALU.mult))
                            dve_pad(1)
                            vv = O("vector", lambda v: v.tensor_copy(out=pad1[:], in_=pad1[:]), "sV")
                            vv_g[glob] = vv
                            glob += 1

                            if ns >= 1:
                                emit_dn(ti, ns - 1, jj, 1 - par,
                                        bc_guard=(a_bc if (ns == 1 and jj < 2) else None))
                    inter_done[(ti, ns)] = vv_g[glob - 1]
                for fo in range(NCH):
                    emit_dn(ti, NSUP - 1, fo, (NSUP - 1) % 2)
                acc_done = em.val("sV")
                W("sync", "sV", acc_done)
                for oc in range(NCH):
                    O("sync", lambda e, oc=oc, t0=t0:
                      e.dma_start(out=out[oc * 128:(oc + 1) * 128, t0:t0 + TT],
                                  in_=x2_sb[:, oc * TT:(oc + 1) * TT]), "dS", 16)
                out_st = em.val("dS")

        if dbg_outs:
            W("gpsimd", "dA", em.val("dA"))
            W("gpsimd", "dS", em.val("dS"))
            W("gpsimd", "sV", em.val("sV"))
            if "o_gdec" in dbg_outs:
                O("gpsimd", lambda e: e.dma_start(
                    out=dbg_outs["o_gdec"].rearrange("a b c d -> (a b) (c d)"),
                    in_=gdec.rearrange("a b c d -> (a b) (c d)")), "dG", 16)
                O("gpsimd", lambda e: e.dma_start(
                    out=dbg_outs["o_ddec"].rearrange("a b c d e -> (a b) (c d e)"),
                    in_=ddec.rearrange("a b c d e -> (a b) (c d e)")), "dG", 16)
            if "o_x2" in dbg_outs:
                O("gpsimd", lambda e: e.dma_start(out=dbg_outs["o_x2"], in_=x2d), "dG", 16)

        W("sync", "dS", em.val("dS"))
        W("sync", "dA", em.val("dA"))
        W("sync", "dG", em.val("dG"))
        W("gpsimd", "dG", em.val("dG"))
        W("scalar", "dA", em.val("dA"))

        with nc.Block() as block:
            @block.sync
            def _(e):
                for fn in em.ops["sync"]:
                    fn(e)

            @block.gpsimd
            def _(e):
                for fn in em.ops["gpsimd"]:
                    fn(e)

            @block.vector
            def _(e):
                for fn in em.ops["vector"]:
                    fn(e)

            @block.scalar
            def _(e):
                for fn in em.ops["scalar"]:
                    fn(e)

            @block.tensor
            def _(e):
                for fn in em.ops["tensor"]:
                    fn(e)

    return nc


def _stage_inputs(L0_attn_x, L0_attn_W_q, L0_attn_W_kv, L0_attn_W_o,
                  L0_ffn_gate_W_packed, L0_ffn_gate_scales,
                  L0_ffn_up_W_packed, L0_ffn_up_scales,
                  L0_ffn_down_W_packed, L0_ffn_down_scales):
    pi = perm2048()
    PI = perm16384()

    def tile_wT(wT):
        return np.ascontiguousarray(wT.reshape(16, 128, 16, 128).transpose(2, 1, 0, 3))

    x = np.asarray(L0_attn_x, dtype=np.float32)
    wq = np.asarray(L0_attn_W_q, dtype=np.float32)
    wkv = np.asarray(L0_attn_W_kv, dtype=np.float32)
    wo = np.asarray(L0_attn_W_o, dtype=np.float32)

    wq_t = tile_wT(np.ascontiguousarray(wq[:, pi].T))
    wkv_t = tile_wT(np.ascontiguousarray(wkv[:, pi].T))
    wo_t = tile_wT(np.ascontiguousarray(wo[pi, :].T))

    def pack16(Wp):
        return np.ascontiguousarray(np.asarray(Wp).view(np.int16)[:, 0::2])

    g16 = np.ascontiguousarray(pack16(L0_ffn_gate_W_packed)[PI, :].T)
    u16 = np.ascontiguousarray(pack16(L0_ffn_up_W_packed)[PI, :].T)
    d16 = np.ascontiguousarray(pack16(L0_ffn_down_W_packed)[pi, :].T)

    gs = np.asarray(L0_ffn_gate_scales, dtype=np.float32)
    us = np.asarray(L0_ffn_up_scales, dtype=np.float32)
    ds = np.asarray(L0_ffn_down_scales, dtype=np.float32)
    gse = np.ascontiguousarray(np.repeat(gs[PI, :].T, 16, axis=0))
    use2 = np.ascontiguousarray(np.repeat(us[PI, :].T, 16, axis=0))
    dse = np.ascontiguousarray(np.repeat(ds[pi, :].T, 16, axis=0))

    ar = np.arange(128, dtype=np.float64)
    ph = 2.0 * np.pi * np.outer(ar, ar) / 128.0
    dftc = np.cos(ph).astype(ml_dtypes.bfloat16)
    dfts = (-np.sin(ph)).astype(ml_dtypes.bfloat16)
    dftci = (np.cos(ph) / 128.0).astype(ml_dtypes.bfloat16)
    dftsi = (-np.sin(ph) / 128.0).astype(ml_dtypes.bfloat16)

    in_maps = []
    for c in range(NCORES):
        xs = [np.ascontiguousarray(x[c * SPC + b].T[pi]) for b in range(SPC)]
        x_core = np.ascontiguousarray(np.concatenate(xs, axis=1))
        in_maps.append(dict(
            x_in=x_core, wq_t=wq_t, wkv_t=wkv_t, wo_t=wo_t,
            g16=g16, u16=u16, d16=d16, gse=gse, use=use2, dse=dse,
            dftc=dftc, dfts=dfts, dftci=dftci, dftsi=dftsi,
        ))
    return in_maps, pi


def kernel(**inputs):
    in_maps, pi = _stage_inputs(**inputs)
    nc = build_nc()
    res = run_bass_kernel_spmd(nc, in_maps, core_ids=list(range(NCORES)))
    outp = np.empty((NB, S, D), dtype=np.float32)
    for c in range(NCORES):
        r = np.asarray(res.results[c]["out"])
        for b in range(SPC):
            seq = r[:, b * S:(b + 1) * S]
            o = np.empty((S, D), dtype=np.float32)
            o[:, pi] = seq.T
            outp[c * SPC + b] = o
    return outp
